# version 07
# speedup vs baseline: 1.1825x; 1.0028x over previous
import contextlib
import numpy as np
import ml_dtypes
import concourse.bass as bass
import concourse.mybir as mybir
from concourse.bass_utils import run_bass_kernel_spmd

F32 = mybir.dt.float32
BF16 = mybir.dt.bfloat16
AF = mybir.ActivationFunctionType
ALU = mybir.AluOpType
AX = mybir.AxisListType
BF16NP = ml_dtypes.bfloat16

N_CORES = 8
B = 2
S = 16384
D = 1024
TLOC = S // N_CORES
NT = TLOC // 128


ENGINES = ("tensor", "vector", "scalar", "gpsimd", "sync")


class Prog:
    def __init__(self, name="prog"):
        self.nc = bass.Bass("TRN2", target_bir_lowering=False)
        self.name = name
        self.stack = contextlib.ExitStack()
        self.ops = []
        self.counts = {e: 0 for e in ENGINES}
        self.dma_counts = {}
        self.deps = {}
        self.waited = {e: {} for e in ENGINES}
        self.sem_handles = {}
        self._coll_sems = set()
        self._finalized = False

    def dram(self, name, shape, dtype, kind="Internal"):
        t = self.nc.dram_tensor(name, list(shape), dtype, kind=kind)
        return t

    def sbuf(self, name, shape, dtype):
        pre = getattr(self, "_stage_name", "")
        return self.stack.enter_context(self.nc.sbuf_tensor(pre + name, list(shape), dtype))

    def psum(self, name, shape, dtype=F32):
        pre = getattr(self, "_stage_name", "")
        return self.stack.enter_context(self.nc.psum_tensor(pre + name, list(shape), dtype))

    @staticmethod
    def _key(item):
        if isinstance(item, tuple):
            h, sk = item
        else:
            h, sk = item, None
        name = getattr(h, "name", None)
        if name is None:
            name = h.tensor.name if hasattr(h, "tensor") else str(h)
        return (name, sk)

    def _entry(self, name):
        return self.deps.setdefault(name, {"w": {}, "r": {}})

    def _conflicts(self, table, sk):
        out = {}
        for (sk2, sem), val in table.items():
            if sk is None or sk2 is None or sk == sk2:
                if out.get(sem, 0) < val:
                    out[sem] = val
        return out

    def _record(self, table, name, sk, sem, val):
        k = (sk, sem)
        if table.get(k, 0) < val:
            table[k] = val

    def op(self, engine, fn, reads=(), writes=(), dma=False, dma_sem=None):
        needed = {}
        rkeys = [self._key(r) for r in reads]
        wkeys = [self._key(w) for w in writes]
        for (name, sk) in rkeys:
            e = self._entry(name)
            for sem, val in self._conflicts(e["w"], sk).items():
                if needed.get(sem, 0) < val:
                    needed[sem] = val
        for (name, sk) in wkeys:
            e = self._entry(name)
            for tab in (e["w"], e["r"]):
                for sem, val in self._conflicts(tab, sk).items():
                    if needed.get(sem, 0) < val:
                        needed[sem] = val
        waits = []
        wt = self.waited[engine]
        for sem, val in sorted(needed.items()):
            if sem.startswith("eng_" + engine) and not dma:
                pass
            if wt.get(sem, 0) < val:
                waits.append((sem, val))
                wt[sem] = val
        if dma:
            if dma_sem == "__coll":
                sem = "dcoll"
                amt = 1
            else:
                slots = getattr(self, "_dma_slots", None)
                if slots is None:
                    self._dma_slots = slots = {}
                if dma_sem not in slots:
                    slots[dma_sem] = f"ds{len(slots)}"
                sem = slots[dma_sem]
                amt = 16
            self.dma_counts[sem] = self.dma_counts.get(sem, 0) + amt
            val_after = self.dma_counts[sem]
        else:
            sem = "eng_" + engine
            amt = 1
            self.counts[engine] += 1
            val_after = self.counts[engine]
        incs = [(sem, amt)]
        for (name, sk) in rkeys:
            self._record(self._entry(name)["r"], name, sk, sem, val_after)
        for (name, sk) in wkeys:
            self._record(self._entry(name)["w"], name, sk, sem, val_after)
        self.ops.append((engine, fn, waits, incs))
        return (sem, val_after)

    def dma(self, queue, out, in_, reads=(), writes=(), sem=None):
        if sem is None:
            pool = list(writes) or list(reads)
            sem = self._key(pool[0])[0]
        return self.op(queue, lambda e: e.dma_start(out=out, in_=in_),
                       reads=reads, writes=writes, dma=True, dma_sem=sem)

    def barrier_wait(self, engine, token):
        sem, val = token
        wt = self.waited[engine]
        if wt.get(sem, 0) >= val:
            return
        wt[sem] = val
        self.ops.append((engine, None, [(sem, val)], []))

    def barrier(self):
        toks = [("eng_" + e, self.counts[e]) for e in ENGINES if self.counts[e]]
        toks += [(s, v) for s, v in self.dma_counts.items() if s not in self._coll_sems]
        for e in ENGINES:
            for t in toks:
                self.barrier_wait(e, t)

    @contextlib.contextmanager
    def stage(self, name):
        old_stack = self.stack
        self.stack = contextlib.ExitStack()
        old_slot = getattr(self, "_dma_slots", None)
        old_name = getattr(self, "_stage_name", "")
        self._dma_slots = {}
        self._stage_name = name + "_"
        try:
            yield
        finally:
            self.barrier()
            self.stack.close()
            self.stack = old_stack
            self._dma_slots = old_slot
            self._stage_name = old_name

    def dyn_dma(self, queue, out, in_, reg, reads=(), writes=(), sem=None):
        return self.dma(queue, out, in_, reads=reads, writes=writes, sem=sem)

    def coll(self, kind, op_, ins_ap, outs_ap, reads=(), writes=()):
        def f(e):
            return e.collective_compute(kind, op_, replica_groups=[list(range(N_CORES))],
                                        ins=[a.opt() for a in ins_ap], outs=[a.opt() for a in outs_ap])
        tok = self.op("gpsimd", f, reads=reads, writes=writes, dma=True, dma_sem="__coll")
        self._coll_sems.add(tok[0])
        return tok

    def finalize(self, final_tokens=()):
        assert not self._finalized
        self._finalized = True
        nc = self.nc
        sems = {}
        for e in ENGINES:
            if self.counts[e]:
                sems["eng_" + e] = self.stack.enter_context(nc.semaphore("eng_" + e))
        for sname in self.dma_counts:
            sems[sname] = self.stack.enter_context(nc.semaphore(sname))
        self.sem_handles = sems
        block = self.stack.enter_context(nc.Block())
        per_engine = {e: [] for e in ENGINES}
        for rec in self.ops:
            per_engine[rec[0]].append(rec)
        fin_engine = "gpsimd"
        final_tokens = list(final_tokens) + [(s, v) for s, v in self.dma_counts.items()]
        for e in ENGINES:
            eng_ops = per_engine[e]
            fins = [t for t in final_tokens] if e == fin_engine else []
            if not eng_ops and not fins:
                continue

            def make(eng_name, eng_ops, fins):
                def body(eng):
                    for (_, fn, waits, incs) in eng_ops:
                        for (sem, val) in waits:
                            if sem in sems:
                                eng.wait_ge(sems[sem], val)
                        if fn is not None:
                            inst = fn(eng)
                            for (sem, amt) in incs:
                                inst.then_inc(sems[sem], amt)
                    for (sem, val) in fins:
                        if sem in sems:
                            eng.wait_ge(sems[sem], val)
                return body

            getattr(block, e)(make(e, eng_ops, fins))
        return nc

    def close(self):
        self.stack.close()



def ln_stats(p, x_tile, sq_scratch, stat, n_feat=D, eps=1e-5):
    v, s = "vector", "scalar"
    p.op(v, lambda e: e.reduce_sum(out=stat["sum"][:], in_=x_tile[:], axis=AX.X),
         reads=[x_tile], writes=[stat["sum"]])
    p.op(s, lambda e: e.activation(out=sq_scratch[:], in_=x_tile[:], func=AF.Square,
                                   accum_out=stat["sumsq"][:]),
         reads=[x_tile], writes=[sq_scratch, stat["sumsq"]])
    p.op(v, lambda e: e.tensor_scalar(out=stat["mean"][:], in0=stat["sum"][:],
                                      scalar1=1.0 / n_feat, scalar2=None, op0=ALU.mult),
         reads=[stat["sum"]], writes=[stat["mean"]])
    p.op(v, lambda e: e.tensor_mul(out=stat["m2"][:], in0=stat["mean"][:], in1=stat["mean"][:]),
         reads=[stat["mean"]], writes=[stat["m2"]])
    p.op(v, lambda e: e.tensor_scalar(out=stat["nm2"][:], in0=stat["m2"][:], scalar1=-1.0,
                                      scalar2=None, op0=ALU.mult),
         reads=[stat["m2"]], writes=[stat["nm2"]])
    p.op(v, lambda e: e.scalar_tensor_tensor(out=stat["var"][:], in0=stat["sumsq"][:],
                                             scalar=1.0 / n_feat, in1=stat["nm2"][:],
                                             op0=ALU.mult, op1=ALU.add),
         reads=[stat["sumsq"], stat["nm2"]], writes=[stat["var"]])
    p.op(v, lambda e: e.tensor_scalar(out=stat["vare"][:], in0=stat["var"][:], scalar1=eps,
                                      scalar2=None, op0=ALU.add),
         reads=[stat["var"]], writes=[stat["vare"]])
    p.op(s, lambda e: e.activation(out=stat["std"][:], in_=stat["vare"][:], func=AF.Ln),
         reads=[stat["vare"]], writes=[stat["std"]])
    p.op(s, lambda e: e.activation(out=stat["rstd"][:], in_=stat["std"][:], func=AF.Exp, scale=-0.5),
         reads=[stat["std"]], writes=[stat["rstd"]])
    p.op(v, lambda e: e.tensor_mul(out=stat["mr"][:], in0=stat["mean"][:], in1=stat["rstd"][:]),
         reads=[stat["mean"], stat["rstd"]], writes=[stat["mr"]])
    p.op(v, lambda e: e.tensor_scalar(out=stat["negmur"][:], in0=stat["mr"][:], scalar1=-1.0,
                                      scalar2=None, op0=ALU.mult),
         reads=[stat["mr"]], writes=[stat["negmur"]])


def make_stat(p, tag):
    names = ("sum", "sumsq", "mean", "m2", "nm2", "var", "vare", "std", "rstd", "mr", "negmur")
    return {n: p.sbuf(f"{tag}_{n}", [128, 1], F32) for n in names}



def emit_moe(p, io):
    x_in = io["x"]; We = io["We"]; be = io["be"]; Wr = io["Wr"]; br = io["br"]
    ident_in = io["ident_in"]; x_out = io["xo"]

    ident = p.sbuf("ident", [128, 128], F32)
    identb = p.sbuf("identb", [128, 128], BF16)
    ones_row = p.sbuf("ones_row", [1, 128], BF16)
    p.dma("sync", ident[:], ident_in, writes=[ident])
    p.op("vector", lambda e: e.tensor_copy(out=identb[:], in_=ident[:]), reads=[ident], writes=[identb])
    p.op("vector", lambda e: e.memset(ones_row[:], 1.0), writes=[ones_row])

    WeT = p.sbuf("WeT", [128, 8, 8, 1024], BF16)
    wld = [p.sbuf(f"wld{i}", [128, 1024], F32) for i in range(2)]
    ex_ps = [p.psum(f"ex_ps{i}", [128, 512], F32) for i in range(4)]
    tr_ps = [ex_ps[0], ex_ps[1]]
    for e_i in range(8):
        for oc in range(8):
            w = wld[(e_i * 8 + oc) % 2]
            p.dma("sync", w[:], We[e_i, oc * 128:(oc + 1) * 128, :], writes=[w])
            for kb in range(8):
                tp = tr_ps[kb % 2]
                sl = w[:, kb * 128:(kb + 1) * 128]
                p.op("tensor", lambda e, sl=sl, tp=tp: e.transpose(out=tp[:, 0:128], in_=sl, identity=ident[:]),
                     reads=[w, ident], writes=[tp])
                dsl = WeT[:, e_i, kb, oc * 128:(oc + 1) * 128]
                p.op("scalar", lambda e, dsl=dsl, tp=tp: e.activation(out=dsl, in_=tp[:, 0:128], func=AF.Copy),
                     reads=[tp], writes=[(WeT, (e_i, kb, oc))])
    WrT = p.sbuf("WrT", [128, 8, 8], BF16)
    wr_s = p.sbuf("wr_s", [8, 1024], F32)
    p.dma("sync", wr_s[:], Wr, writes=[wr_s])
    for kb in range(8):
        tp = tr_ps[kb % 2]
        sl = wr_s[:, kb * 128:(kb + 1) * 128]
        p.op("tensor", lambda e, sl=sl, tp=tp: e.transpose(out=tp[:, 0:8], in_=sl, identity=ident[0:8, 0:8]),
             reads=[wr_s, ident], writes=[tp])
        dsl = WrT[:, kb, :]
        p.op("scalar", lambda e, dsl=dsl, tp=tp: e.activation(out=dsl, in_=tp[:, 0:8], func=AF.Copy),
             reads=[tp], writes=[(WrT, kb)])
    beS = p.sbuf("beS", [8, 1024], BF16)
    be_f = p.sbuf("be_f", [8, 1024], F32)
    p.dma("sync", be_f[:], be, writes=[be_f])
    p.op("vector", lambda e: e.tensor_copy(out=beS[:], in_=be_f[:]), reads=[be_f], writes=[beS])
    brS = p.sbuf("brS", [1, 8], BF16)
    br_f = p.sbuf("br_f", [1, 8], F32)
    p.dma("sync", br_f[:], br.unsqueeze(0), writes=[br_f])
    p.op("vector", lambda e: e.tensor_copy(out=brS[:], in_=br_f[:]), reads=[br_f], writes=[brS])

    NPAR = 2
    xt = [p.sbuf(f"xt{i}", [128, D], F32) for i in range(NPAR)]
    sq = [p.sbuf(f"sq{i}", [128, D], F32) for i in range(NPAR)]
    lnb = [p.sbuf(f"lnb{i}", [128, D], BF16) for i in range(NPAR)]
    lnT = [p.sbuf(f"lnT{i}", [128, 8, 128], BF16) for i in range(NPAR)]
    stats = [make_stat(p, f"st{i}") for i in range(NPAR)]
    tr_bps = [p.psum(f"trb{i}", [128, 128], BF16) for i in range(2)]
    rt_ps = p.psum("rt_ps", [128, 8], F32)
    rw_p = [p.sbuf(f"rw{i}", [128, 8], F32) for i in range(NPAR)]
    rwm_p = [p.sbuf(f"rwm{i}", [128, 8], F32) for i in range(NPAR)]
    m1_p = [p.sbuf(f"m1{i}", [128, 1], F32) for i in range(NPAR)]
    m2_p = [p.sbuf(f"m2{i}", [128, 1], F32) for i in range(NPAR)]
    t8a_p = [p.sbuf(f"t8a{i}", [128, 8], F32) for i in range(NPAR)]
    t8b_p = [p.sbuf(f"t8b{i}", [128, 8], F32) for i in range(NPAR)]
    t8c_p = [p.sbuf(f"t8c{i}", [128, 8], F32) for i in range(NPAR)]
    ssum_p = [p.sbuf(f"ssum{i}", [128, 1], F32) for i in range(NPAR)]
    srec_p = [p.sbuf(f"srec{i}", [128, 1], F32) for i in range(NPAR)]
    rwT_ps = p.psum("rwT_ps", [8, 128], F32)
    rwT_p = [p.sbuf(f"rwT{i}", [8, 128], BF16) for i in range(NPAR)]
    acc = [p.sbuf(f"acc{i}", [128, D], F32) for i in range(3)]
    out_t = [p.sbuf(f"out_t{i}", [128, D], F32) for i in range(NPAR)]
    out_tokens = []

    for b in range(B):
        for ti in range(NT):
            par = (b * NT + ti) % NPAR
            rw, rwm, m1, m2 = rw_p[par], rwm_p[par], m1_p[par], m2_p[par]
            t8a, t8b, t8c, ssum, srec, rwT = (t8a_p[par], t8b_p[par], t8c_p[par],
                                              ssum_p[par], srec_p[par], rwT_p[par])
            X = xt[par]
            p.dma("sync", X[:], x_in[b, ti * 128:(ti + 1) * 128, :], writes=[X])
            st = stats[par]
            ln_stats(p, X, sq[par], st)
            LNB = lnb[par]
            p.op("scalar", lambda e, X=X, LNB=LNB, st=st, rw=rw, rwm=rwm, m1=m1, m2=m2, t8a=t8a, t8b=t8b, t8c=t8c, ssum=ssum, srec=srec, rwT=rwT:
                 e.activation(out=LNB[:], in_=X[:], func=AF.Identity,
                              bias=st["negmur"][:, 0:1], scale=st["rstd"][:, 0:1]),
                 reads=[X, st["negmur"], st["rstd"]], writes=[LNB])
            LT = lnT[par]
            for kb in range(8):
                TPS = tr_bps[kb % 2]
                sl = LNB[:, kb * 128:(kb + 1) * 128]
                p.op("tensor", lambda e, sl=sl, TPS=TPS, rw=rw, rwm=rwm, m1=m1, m2=m2, t8a=t8a, t8b=t8b, t8c=t8c, ssum=ssum, srec=srec, rwT=rwT: e.transpose(out=TPS[:], in_=sl, identity=identb[:]),
                     reads=[LNB, identb], writes=[TPS])
                dsl = LT[:, kb, :]
                p.op("scalar", lambda e, dsl=dsl, TPS=TPS, rw=rw, rwm=rwm, m1=m1, m2=m2, t8a=t8a, t8b=t8b, t8c=t8c, ssum=ssum, srec=srec, rwT=rwT: e.activation(out=dsl, in_=TPS[:], func=AF.Copy),
                     reads=[TPS], writes=[(LT, kb)])
            def rt_mm(e, LT=LT):
                e.matmul(out=rt_ps[:], lhsT=ones_row[:], rhs=brS[:], start=True, stop=False)
                last = None
                for kb in range(8):
                    last = e.matmul(out=rt_ps[:], lhsT=LT[:, kb, :], rhs=WrT[:, kb, :],
                                    start=False, stop=(kb == 7))
                return last
            p.op("tensor", rt_mm, reads=[ones_row, brS, LT, WrT], writes=[rt_ps])
            p.op("vector", lambda e, rw=rw, rwm=rwm, m1=m1, m2=m2, t8a=t8a, t8b=t8b, t8c=t8c, ssum=ssum, srec=srec, rwT=rwT: e.tensor_reduce(out=m1[:], in_=rt_ps[:], axis=AX.X, op=ALU.max, negate=True),
                 reads=[rt_ps], writes=[m1])
            p.op("scalar", lambda e, rw=rw, rwm=rwm, m1=m1, m2=m2, t8a=t8a, t8b=t8b, t8c=t8c, ssum=ssum, srec=srec, rwT=rwT: e.activation(out=rw[:], in_=rt_ps[:], func=AF.Exp, bias=m1[:, 0:1]),
                 reads=[rt_ps, m1], writes=[rw])
            p.op("vector", lambda e, rw=rw, rwm=rwm, m1=m1, m2=m2, t8a=t8a, t8b=t8b, t8c=t8c, ssum=ssum, srec=srec, rwT=rwT: e.reduce_sum(out=ssum[:], in_=rw[:], axis=AX.X),
                 reads=[rw], writes=[ssum])
            p.op("vector", lambda e, rw=rw, rwm=rwm, m1=m1, m2=m2, t8a=t8a, t8b=t8b, t8c=t8c, ssum=ssum, srec=srec, rwT=rwT: e.reciprocal(out=srec[:], in_=ssum[:]),
                 reads=[ssum], writes=[srec])
            p.op("vector", lambda e, rw=rw, rwm=rwm, m1=m1, m2=m2, t8a=t8a, t8b=t8b, t8c=t8c, ssum=ssum, srec=srec, rwT=rwT: e.tensor_reduce(out=m1[:], in_=rw[:], axis=AX.X, op=ALU.max, negate=True),
                 reads=[rw], writes=[m1])
            p.op("vector", lambda e, rw=rw, rwm=rwm, m1=m1, m2=m2, t8a=t8a, t8b=t8b, t8c=t8c, ssum=ssum, srec=srec, rwT=rwT: e.tensor_scalar(out=t8a[:], in0=rw[:], scalar1=m1[:, 0:1],
                                                     scalar2=None, op0=ALU.add),
                 reads=[rw, m1], writes=[t8a])
            p.op("vector", lambda e, rw=rw, rwm=rwm, m1=m1, m2=m2, t8a=t8a, t8b=t8b, t8c=t8c, ssum=ssum, srec=srec, rwT=rwT: e.tensor_scalar(out=t8b[:], in0=t8a[:], scalar1=0.0,
                                                     scalar2=None, op0=ALU.is_lt),
                 reads=[t8a], writes=[t8b])
            p.op("vector", lambda e, rw=rw, rwm=rwm, m1=m1, m2=m2, t8a=t8a, t8b=t8b, t8c=t8c, ssum=ssum, srec=srec, rwT=rwT: e.tensor_mul(out=t8c[:], in0=t8b[:], in1=rw[:]),
                 reads=[t8b, rw], writes=[t8c])
            p.op("vector", lambda e, rw=rw, rwm=rwm, m1=m1, m2=m2, t8a=t8a, t8b=t8b, t8c=t8c, ssum=ssum, srec=srec, rwT=rwT: e.tensor_reduce(out=m2[:], in_=t8c[:], axis=AX.X, op=ALU.max, negate=True),
                 reads=[t8c], writes=[m2])
            p.op("vector", lambda e, rw=rw, rwm=rwm, m1=m1, m2=m2, t8a=t8a, t8b=t8b, t8c=t8c, ssum=ssum, srec=srec, rwT=rwT: e.tensor_scalar(out=t8a[:], in0=rw[:], scalar1=m2[:, 0:1],
                                                     scalar2=None, op0=ALU.add),
                 reads=[rw, m2], writes=[t8a])
            p.op("vector", lambda e, rw=rw, rwm=rwm, m1=m1, m2=m2, t8a=t8a, t8b=t8b, t8c=t8c, ssum=ssum, srec=srec, rwT=rwT: e.tensor_scalar(out=t8b[:], in0=t8a[:], scalar1=0.0,
                                                     scalar2=None, op0=ALU.is_ge),
                 reads=[t8a], writes=[t8b])
            p.op("vector", lambda e, rw=rw, rwm=rwm, m1=m1, m2=m2, t8a=t8a, t8b=t8b, t8c=t8c, ssum=ssum, srec=srec, rwT=rwT: e.tensor_mul(out=rwm[:], in0=t8b[:], in1=rw[:]),
                 reads=[t8b, rw], writes=[rwm])
            p.op("vector", lambda e, rw=rw, rwm=rwm, m1=m1, m2=m2, t8a=t8a, t8b=t8b, t8c=t8c, ssum=ssum, srec=srec, rwT=rwT: e.tensor_scalar(out=rw[:], in0=rwm[:], scalar1=srec[:, 0:1],
                                                     scalar2=None, op0=ALU.mult),
                 reads=[rwm, srec], writes=[rw])
            p.op("tensor", lambda e, rw=rw, rwm=rwm, m1=m1, m2=m2, t8a=t8a, t8b=t8b, t8c=t8c, ssum=ssum, srec=srec, rwT=rwT: e.transpose(out=rwT_ps[:, :], in_=rw[:], identity=ident[:]),
                 reads=[rw, ident], writes=[rwT_ps])
            p.op("scalar", lambda e, rw=rw, rwm=rwm, m1=m1, m2=m2, t8a=t8a, t8b=t8b, t8c=t8c, ssum=ssum, srec=srec, rwT=rwT: e.activation(out=rwT[:], in_=rwT_ps[:], func=AF.Copy),
                 reads=[rwT_ps], writes=[rwT])
            for oc in range(2):
                pse = ex_ps[oc * 2]
                p.op("tensor", lambda e, oc=oc, pse=pse, rw=rw, rwm=rwm, m1=m1, m2=m2, t8a=t8a, t8b=t8b, t8c=t8c, ssum=ssum, srec=srec, rwT=rwT: e.matmul(out=pse[:], lhsT=rwT[:],
                                                                  rhs=beS[:, oc * 512:(oc + 1) * 512],
                                                                  start=True, stop=True),
                     reads=[rwT, beS], writes=[pse])
                p.op("vector", lambda e, oc=oc, pse=pse, rw=rw, rwm=rwm, m1=m1, m2=m2, t8a=t8a, t8b=t8b, t8c=t8c, ssum=ssum, srec=srec, rwT=rwT: e.tensor_copy(out=acc[0][:, oc * 512:(oc + 1) * 512],
                                                                       in_=pse[:]),
                     reads=[pse], writes=[(acc[0], oc)])
            prev = acc[0]
            for e_i in range(8):
                nxt = acc[1 + (e_i % 2)]
                for oc in range(2):
                    pse = ex_ps[oc * 2 + (e_i % 2)]
                    def emm(e, e_i=e_i, oc=oc, LT=LT, pse=pse):
                        last = None
                        for kb in range(8):
                            last = e.matmul(out=pse[:], lhsT=LT[:, kb, :],
                                            rhs=WeT[:, e_i, kb, oc * 512:(oc + 1) * 512],
                                            start=(kb == 0), stop=(kb == 7))
                        return last
                    p.op("tensor", emm, reads=[LT, WeT], writes=[pse])
                    p.op("vector", lambda e, oc=oc, pse=pse, prev=prev, nxt=nxt, e_i=e_i, rw=rw, rwm=rwm, m1=m1, m2=m2, t8a=t8a, t8b=t8b, t8c=t8c, ssum=ssum, srec=srec, rwT=rwT:
                         e.scalar_tensor_tensor(out=nxt[:, oc * 512:(oc + 1) * 512],
                                                in0=pse[:], scalar=rw[:, e_i:e_i + 1],
                                                in1=prev[:, oc * 512:(oc + 1) * 512],
                                                op0=ALU.mult, op1=ALU.add),
                         reads=[pse, rw, (prev, oc)], writes=[(nxt, oc)])
                prev = nxt
            OT = out_t[par]
            p.op("vector", lambda e, OT=OT, prev=prev, X=X, rw=rw, rwm=rwm, m1=m1, m2=m2, t8a=t8a, t8b=t8b, t8c=t8c, ssum=ssum, srec=srec, rwT=rwT:
                 e.scalar_tensor_tensor(out=OT[:], in0=prev[:], scalar=0.5, in1=X[:],
                                        op0=ALU.mult, op1=ALU.add),
                 reads=[prev, X], writes=[OT])
            tok = p.dma("gpsimd", x_out[b, ti * 128:(ti + 1) * 128, :], OT[:], reads=[OT])
            out_tokens.append(tok)

    return


def build_moe(tag="moe"):
    p = Prog(tag)
    io = dict(
        x=p.dram("x", [B, TLOC, D], F32, kind="ExternalInput").ap(),
        We=p.dram("We", [8, D, D], F32, kind="ExternalInput").ap(),
        be=p.dram("be", [8, D], F32, kind="ExternalInput").ap(),
        Wr=p.dram("Wr", [8, D], F32, kind="ExternalInput").ap(),
        br=p.dram("br", [8], F32, kind="ExternalInput").ap(),
        ident_in=p.dram("ident_in", [128, 128], F32, kind="ExternalInput").ap(),
        xo=p.dram("xo", [B, TLOC, D], F32, kind="ExternalOutput").ap(),
    )
    with p.stage("moe"):
        emit_moe(p, io)
    p.finalize()
    return p



CHUNK = 512

def emit_mfront(p, io):
    H = 3
    TE = TLOC + 2 * H
    NCH = TLOC // CHUNK
    x_in = io["x"]
    Win = {d_: io[f"Win_{d_}"] for d_ in "fb"}
    convw = {d_: io[f"convw_{d_}"] for d_ in "fb"}
    convb = {d_: io[f"convb_{d_}"] for d_ in "fb"}
    xproj = {d_: io[f"xproj_{d_}"] for d_ in "fb"}
    dtw = {d_: io[f"dtw_{d_}"] for d_ in "fb"}
    dtb = {d_: io[f"dtb_{d_}"] for d_ in "fb"}
    ident_in = io["ident_in"]
    uT_o = {d_: io[f"uT_{d_}"] for d_ in "fb"}
    zT_o = {d_: io[f"zT_{d_}"] for d_ in "fb"}
    dT_o = {d_: io[f"dT_{d_}"] for d_ in "fb"}
    duT_o = {d_: io[f"duT_{d_}"] for d_ in "fb"}
    bcT_o = {d_: io[f"bcT_{d_}"] for d_ in "fb"}

    ident = p.sbuf("ident", [128, 128], F32)
    identb = p.sbuf("identb", [128, 128], BF16)
    p.dma("sync", ident[:], ident_in, writes=[ident])
    p.op("vector", lambda e: e.tensor_copy(out=identb[:], in_=ident[:]), reads=[ident], writes=[identb])

    lnT = p.sbuf("lnT", [128, 8, TE + 2], BF16)
    WinT = p.sbuf("WinT", [128, 8, 2048], BF16)
    xprojT = p.sbuf("xprojT", [128, 16, 96], BF16)
    dtwT = p.sbuf("dtwT", [64, 2048], BF16)
    cwcol = p.sbuf("cwcol", [128, 16, 4], F32)
    cbcol = p.sbuf("cbcol", [128, 16], F32)
    dtbcol = p.sbuf("dtbcol", [128, 16], F32)
    xsT = p.sbuf("xsT", [128, 16, CHUNK + H], BF16)
    uT = p.sbuf("uTs", [128, 16, CHUNK], BF16)
    dTc = p.sbuf("dTc", [128, 16, CHUNK], BF16)
    duTc = p.sbuf("duTc", [128, 16, CHUNK], BF16)
    conv_a = p.sbuf("conv_a", [128, CHUNK], BF16)
    conv_b2 = p.sbuf("conv_b2", [128, CHUNK], BF16)
    conv_c = p.sbuf("conv_c", [128, CHUNK], BF16)
    conv_d2 = p.sbuf("conv_d2", [128, CHUNK], BF16)
    zev = [p.sbuf(f"zev{i}", [128, CHUNK], BF16) for i in range(2)]
    xdb = p.sbuf("xdb", [96, CHUNK], BF16)
    bcs = p.sbuf("bcs", [32, CHUNK], BF16)
    sp0 = p.sbuf("sp0", [128, CHUNK], F32)
    sp1 = p.sbuf("sp1", [128, CHUNK], F32)
    sp2 = p.sbuf("sp2", [128, CHUNK], F32)
    xt = [p.sbuf(f"xt{i}", [128, D], F32) for i in range(2)]
    sq = [p.sbuf(f"sq{i}", [128, D], F32) for i in range(2)]
    lnb = [p.sbuf(f"lnb{i}", [128, D], BF16) for i in range(2)]
    stats = [make_stat(p, f"st{i}") for i in range(2)]
    wld = [p.sbuf(f"wld{i}", [128, 1024], F32) for i in range(2)]
    wsm = p.sbuf("wsm", [96, 2048], F32)
    mm_ps = [p.psum(f"mm_ps{i}", [128, 512], F32) for i in range(4)]
    _tr0 = p.psum("tr_ps0", [128, 128], F32)
    tr_ps = [_tr0, _tr0]
    trb_ps = [p.psum(f"trb_ps{i}", [128, 128], BF16) for i in range(2)]
    xd_ps = p.psum("xd_ps", [96, 512], F32)

    def wprep_win(d_, half):
        for rb in range(16):
            w = wld[rb % 2]
            p.dma("sync", w[:], Win[d_][half * 2048 + rb * 128:half * 2048 + (rb + 1) * 128, :], writes=[w])
            for kb in range(8):
                tp = tr_ps[kb % 2]
                sl = w[:, kb * 128:(kb + 1) * 128]
                p.op("tensor", lambda e, sl=sl, tp=tp: e.transpose(out=tp[:], in_=sl, identity=ident[:]),
                     reads=[w, ident], writes=[tp])
                dsl = WinT[:, kb, rb * 128:(rb + 1) * 128]
                p.op("scalar", lambda e, dsl=dsl, tp=tp: e.activation(out=dsl, in_=tp[:], func=AF.Copy),
                     reads=[tp], writes=[(WinT, rb)])
    def wprep_dir(d_):
        p.dma("sync", wsm[:], xproj[d_], writes=[wsm])
        for kb in range(16):
            tp = tr_ps[kb % 2]
            sl = wsm[:, kb * 128:(kb + 1) * 128]
            p.op("tensor", lambda e, sl=sl, tp=tp: e.transpose(out=tp[:, 0:96], in_=sl,
                                                               identity=ident[0:96, 0:96]),
                 reads=[wsm, ident], writes=[tp])
            dsl = xprojT[:, kb, :]
            p.op("scalar", lambda e, dsl=dsl, tp=tp: e.activation(out=dsl, in_=tp[:, 0:96], func=AF.Copy),
                 reads=[tp], writes=[(xprojT, kb)])
        for rb in range(16):
            w = wld[rb % 2]
            p.dma("sync", w[:, 0:64], dtw[d_][rb * 128:(rb + 1) * 128, :], writes=[w])
            tp = tr_ps[rb % 2]
            p.op("tensor", lambda e, w=w, tp=tp: e.transpose(out=tp[0:64, 0:128], in_=w[:, 0:64],
                                                             identity=ident[:]),
                 reads=[w, ident], writes=[tp])
            dsl = dtwT[:, rb * 128:(rb + 1) * 128]
            p.op("scalar", lambda e, dsl=dsl, tp=tp: e.activation(out=dsl, in_=tp[0:64, 0:128], func=AF.Copy),
                 reads=[tp], writes=[(dtwT, rb)])
        p.dma("sync", cwcol[:], convw[d_], writes=[cwcol])
        p.dma("sync", cbcol[:], convb[d_], writes=[cbcol])
        p.dma("sync", dtbcol[:], dtb[d_], writes=[dtbcol])

    def phase1_b(b):
        def do_tile(tok0, ntok, col0):
            par = (tok0 // 128) % 2
            X, SQ, LNB, st = xt[par], sq[par], lnb[par], stats[par]
            p.dma("sync", X[0:ntok, :], x_in[b, tok0:tok0 + ntok, :], writes=[X])
            ln_stats_n(p, X, SQ, st, ntok)
            p.op("scalar", lambda e, X=X, LNB=LNB, st=st, ntok=ntok:
                 e.activation(out=LNB[0:ntok, :], in_=X[0:ntok, :], func=AF.Identity,
                              bias=st["negmur"][0:ntok, 0:1], scale=st["rstd"][0:ntok, 0:1]),
                 reads=[X, st["negmur"], st["rstd"]], writes=[LNB])
            for kb in range(8):
                tp = trb_ps[kb % 2]
                sl = LNB[0:ntok, kb * 128:(kb + 1) * 128]
                idb = identb[0:ntok, 0:ntok]
                p.op("tensor", lambda e, sl=sl, tp=tp, ntok=ntok, idb=idb:
                     e.transpose(out=tp[:, 0:ntok], in_=sl, identity=idb),
                     reads=[LNB, identb], writes=[tp])
                dsl = lnT[:, kb, col0:col0 + ntok]
                p.op("scalar", lambda e, dsl=dsl, tp=tp, ntok=ntok:
                     e.activation(out=dsl, in_=tp[:, 0:ntok], func=AF.Copy),
                     reads=[tp], writes=[(lnT, (kb, col0))])
        do_tile(0, H, 0)
        for ti in range(NT):
            do_tile(H + ti * 128, 128, H + ti * 128)
        do_tile(H + TLOC, H, H + TLOC)

    def inproj_cols(b, col0, n, ech_list, dst, dst_off, psum_sel):
        for i, ech in enumerate(ech_list):
            ps = mm_ps[psum_sel(i)]
            def f(e, ech=ech, ps=ps, col0=col0, n=n):
                last = None
                for kb in range(8):
                    last = e.matmul(out=ps[:, 0:n], lhsT=WinT[:, kb, ech * 128:(ech + 1) * 128],
                                    rhs=lnT[:, kb, col0:col0 + n], start=(kb == 0), stop=(kb == 7))
                return last
            p.op("tensor", f, reads=[WinT, lnT], writes=[ps])
            yield i, ech, ps

    out_sems = []

    colls_after_dir = io.get("_colls_after_dir")
    for d_ in "fb":
        for b in range(B):
            phase1_b(b)
            wprep_dir(d_)
            wprep_win(d_, 0)
            fwd = (d_ == "f")
            chunk_order = range(NCH) if fwd else range(NCH - 1, -1, -1)
            first = True
            for ci in chunk_order:
                c0 = ci * CHUNK
                if fwd:
                    if first:
                        for i, ech, ps in inproj_cols(b, c0, H, range(16), xsT, 0, lambda i: i % 2):
                            dsl = xsT[:, ech, 0:H]
                            p.op("vector", lambda e, dsl=dsl, ps=ps: e.tensor_copy(out=dsl, in_=ps[:, 0:H]),
                                 reads=[ps], writes=[(xsT, ("h", ech))])
                    else:
                        p.op("vector", lambda e: e.tensor_copy(out=xsT[:, :, 0:H],
                                                               in_=xsT[:, :, CHUNK:CHUNK + H]),
                             reads=[xsT], writes=[xsT])
                    body_off = H
                else:
                    if first:
                        for i, ech, ps in inproj_cols(b, c0 + CHUNK + H, H, range(16), xsT, 0, lambda i: i % 2):
                            dsl = xsT[:, ech, CHUNK:CHUNK + H]
                            p.op("vector", lambda e, dsl=dsl, ps=ps: e.tensor_copy(out=dsl, in_=ps[:, 0:H]),
                                 reads=[ps], writes=[(xsT, ("h", ech))])
                    else:
                        p.op("vector", lambda e: e.tensor_copy(out=xsT[:, :, CHUNK:CHUNK + H],
                                                               in_=xsT[:, :, 0:H]),
                             reads=[xsT], writes=[xsT])
                    body_off = 0
                first = False
                for i, ech, ps in inproj_cols(b, H + c0, CHUNK, range(16), xsT, body_off, lambda i: i % 2):
                    dsl = xsT[:, ech, body_off:body_off + CHUNK]
                    p.op("vector", lambda e, dsl=dsl, ps=ps: e.tensor_copy(out=dsl, in_=ps[:]),
                         reads=[ps], writes=[(xsT, ("b", ech))])
                bufs = [conv_a, conv_b2, conv_c, conv_d2]
                for kb in range(16):
                    ks = [0, 1, 2, 3] if fwd else [3, 2, 1, 0]
                    p.op("vector", lambda e, kb=kb, ks=ks, b0=bufs[0]: e.tensor_scalar(
                        out=b0[:], in0=xsT[:, kb, ks[0]:ks[0] + CHUNK],
                        scalar1=cwcol[:, kb, 0:1], scalar2=None, op0=ALU.mult),
                         reads=[xsT, cwcol], writes=[bufs[0]])
                    for j in range(1, 4):
                        p.op("vector", lambda e, kb=kb, j=j, ks=ks, bj=bufs[j], bp=bufs[j - 1]:
                             e.scalar_tensor_tensor(
                                 out=bj[:], in0=xsT[:, kb, ks[j]:ks[j] + CHUNK],
                                 scalar=cwcol[:, kb, j:j + 1], in1=bp[:],
                                 op0=ALU.mult, op1=ALU.add),
                             reads=[xsT, cwcol, bufs[j - 1]], writes=[bufs[j]])
                    p.op("scalar", lambda e, kb=kb: e.activation(
                        out=uT[:, kb, :], in_=conv_d2[:], func=AF.Silu, bias=cbcol[:, kb:kb + 1]),
                         reads=[conv_d2, cbcol], writes=[(uT, kb)])
                    p.dma("gpsimd", uT_o[d_][b, kb * 128:(kb + 1) * 128, c0:c0 + CHUNK], uT[:, kb, :],
                          reads=[(uT, kb)], sem=f"ust{kb}")
                def xp(e):
                    last = None
                    for kb in range(16):
                        last = e.matmul(out=xd_ps[:], lhsT=xprojT[:, kb, :], rhs=uT[:, kb, :],
                                        start=(kb == 0), stop=(kb == 15))
                    return last
                p.op("tensor", xp, reads=[uT, xprojT], writes=[xd_ps])
                p.op("vector", lambda e: e.tensor_copy(out=xdb[:], in_=xd_ps[:]),
                     reads=[xd_ps], writes=[xdb])
                p.op("vector", lambda e: e.tensor_copy(out=bcs[:], in_=xd_ps[64:96, :]),
                     reads=[xd_ps], writes=[bcs])
                p.dma("gpsimd", bcT_o[d_][b, :, c0:c0 + CHUNK], bcs[:], reads=[bcs], sem="bcst")
                for dch in range(16):
                    ps = mm_ps[dch % 2]
                    p.op("tensor", lambda e, dch=dch, ps=ps: e.matmul(
                        out=ps[:], lhsT=dtwT[:, dch * 128:(dch + 1) * 128], rhs=xdb[0:64, :],
                        start=True, stop=True),
                         reads=[dtwT, xdb], writes=[ps])
                    p.op("scalar", lambda e, ps=ps, dch=dch: e.activation(
                        out=sp0[:], in_=ps[:], func=AF.Identity, bias=dtbcol[:, dch:dch + 1]),
                         reads=[ps, dtbcol], writes=[sp0])
                    p.op("scalar", lambda e: e.activation(out=sp1[:], in_=sp0[:], func=AF.Abs),
                         reads=[sp0], writes=[sp1])
                    p.op("scalar", lambda e: e.activation(out=sp2[:], in_=sp1[:], func=AF.Exp, scale=-1.0),
                         reads=[sp1], writes=[sp2])
                    p.op("scalar", lambda e: e.activation(out=sp1[:], in_=sp2[:], func=AF.Ln, bias=1.0),
                         reads=[sp2], writes=[sp1])
                    p.op("vector", lambda e: e.tensor_scalar(out=sp2[:], in0=sp0[:], scalar1=0.0,
                                                             scalar2=None, op0=ALU.max),
                         reads=[sp0], writes=[sp2])
                    p.op("vector", lambda e, dch=dch: e.tensor_add(out=dTc[:, dch, :], in0=sp2[:], in1=sp1[:]),
                         reads=[sp2, sp1], writes=[(dTc, dch)])
                    p.op("vector", lambda e, dch=dch: e.tensor_mul(out=duTc[:, dch, :], in0=dTc[:, dch, :],
                                                                   in1=uT[:, dch, :]),
                         reads=[(dTc, dch), (uT, dch)], writes=[(duTc, dch)])
                    p.dma("gpsimd", dT_o[d_][b, dch * 128:(dch + 1) * 128, c0:c0 + CHUNK], dTc[:, dch, :],
                          reads=[(dTc, dch)], sem=f"dst{dch}")
                    p.dma("gpsimd", duT_o[d_][b, dch * 128:(dch + 1) * 128, c0:c0 + CHUNK], duTc[:, dch, :],
                          reads=[(duTc, dch)], sem=f"dust{dch}")
            wprep_win(d_, 1)
            for ci in range(NCH):
                c0 = ci * CHUNK
                for i, ech, ps in inproj_cols(b, H + c0, CHUNK, range(16), None, 0, lambda i: 2 + i % 2):
                    zv = zev[i % 2]
                    p.op("vector", lambda e, zv=zv, ps=ps: e.tensor_copy(out=zv[:], in_=ps[:]),
                         reads=[ps], writes=[zv])
                    p.dma("gpsimd", zT_o[d_][b, ech * 128:(ech + 1) * 128, c0:c0 + CHUNK], zv[:],
                          reads=[zv], sem=f"zst{i % 2}")
        if colls_after_dir is not None:
            colls_after_dir(d_)
    return


def _mfront_io(p, kind_in="ExternalInput", kind_out="ExternalOutput", pre="", make_x=True):
    io = {}
    if make_x:
        io["x"] = p.dram(pre + "x", [B, TLOC + 6, D], F32, kind=kind_in).ap()
        io["ident_in"] = p.dram(pre + "ident_in", [128, 128], F32, kind="ExternalInput").ap()
    for d_ in "fb":
        io[f"Win_{d_}"] = p.dram(pre + f"Win_{d_}", [4096, D], F32, kind="ExternalInput").ap()
        io[f"convw_{d_}"] = p.dram(pre + f"convw_{d_}", [128, 16, 4], F32, kind="ExternalInput").ap()
        io[f"convb_{d_}"] = p.dram(pre + f"convb_{d_}", [128, 16], F32, kind="ExternalInput").ap()
        io[f"xproj_{d_}"] = p.dram(pre + f"xproj_{d_}", [96, 2048], F32, kind="ExternalInput").ap()
        io[f"dtw_{d_}"] = p.dram(pre + f"dtw_{d_}", [2048, 64], F32, kind="ExternalInput").ap()
        io[f"dtb_{d_}"] = p.dram(pre + f"dtb_{d_}", [128, 16], F32, kind="ExternalInput").ap()
        for nm in ("uT", "zT", "dT", "duT"):
            io[f"{nm}_{d_}"] = p.dram(pre + f"{nm}_{d_}", [B, 2048, TLOC], BF16, kind=kind_out).ap()
        io[f"bcT_{d_}"] = p.dram(pre + f"bcT_{d_}", [B, 32, TLOC], BF16, kind=kind_out).ap()
    return io


def build_mamba_front(tag="mfront"):
    p = Prog(tag)
    io = _mfront_io(p)
    with p.stage("mf"):
        emit_mfront(p, io)
    p.finalize()
    return p


def ln_stats_n(p, x_tile, sq_scratch, stat, ntok, n_feat=D, eps=1e-5):
    v, s = "vector", "scalar"
    p.op(v, lambda e: e.reduce_sum(out=stat["sum"][0:ntok, :], in_=x_tile[0:ntok, :], axis=AX.X),
         reads=[x_tile], writes=[stat["sum"]])
    p.op(s, lambda e: e.activation(out=sq_scratch[0:ntok, :], in_=x_tile[0:ntok, :], func=AF.Square,
                                   accum_out=stat["sumsq"][0:ntok, :]),
         reads=[x_tile], writes=[sq_scratch, stat["sumsq"]])
    p.op(v, lambda e: e.tensor_scalar(out=stat["mean"][0:ntok, :], in0=stat["sum"][0:ntok, :],
                                      scalar1=1.0 / n_feat, scalar2=None, op0=ALU.mult),
         reads=[stat["sum"]], writes=[stat["mean"]])
    p.op(v, lambda e: e.tensor_mul(out=stat["m2"][0:ntok, :], in0=stat["mean"][0:ntok, :],
                                   in1=stat["mean"][0:ntok, :]),
         reads=[stat["mean"]], writes=[stat["m2"]])
    p.op(v, lambda e: e.tensor_scalar(out=stat["nm2"][0:ntok, :], in0=stat["m2"][0:ntok, :], scalar1=-1.0,
                                      scalar2=None, op0=ALU.mult),
         reads=[stat["m2"]], writes=[stat["nm2"]])
    p.op(v, lambda e: e.scalar_tensor_tensor(out=stat["var"][0:ntok, :], in0=stat["sumsq"][0:ntok, :],
                                             scalar=1.0 / n_feat, in1=stat["nm2"][0:ntok, :],
                                             op0=ALU.mult, op1=ALU.add),
         reads=[stat["sumsq"], stat["nm2"]], writes=[stat["var"]])
    p.op(v, lambda e: e.tensor_scalar(out=stat["vare"][0:ntok, :], in0=stat["var"][0:ntok, :], scalar1=eps,
                                      scalar2=None, op0=ALU.add),
         reads=[stat["var"]], writes=[stat["vare"]])
    p.op(s, lambda e: e.activation(out=stat["std"][0:ntok, :], in_=stat["vare"][0:ntok, :], func=AF.Ln),
         reads=[stat["vare"]], writes=[stat["std"]])
    p.op(s, lambda e: e.activation(out=stat["rstd"][0:ntok, :], in_=stat["std"][0:ntok, :], func=AF.Exp,
                                   scale=-0.5),
         reads=[stat["std"]], writes=[stat["rstd"]])
    p.op(v, lambda e: e.tensor_mul(out=stat["mr"][0:ntok, :], in0=stat["mean"][0:ntok, :],
                                   in1=stat["rstd"][0:ntok, :]),
         reads=[stat["mean"], stat["rstd"]], writes=[stat["mr"]])
    p.op(v, lambda e: e.tensor_scalar(out=stat["negmur"][0:ntok, :], in0=stat["mr"][0:ntok, :], scalar1=-1.0,
                                      scalar2=None, op0=ALU.mult),
         reads=[stat["mr"]], writes=[stat["negmur"]])



SCH = 512

def emit_scan(p, io, T=S):
    NCHS = T // SCH
    dT_i = io["dT"]; duT_i = io["duT"]; bcT_i = io["bcT"]
    alog_i = io["alog"]; sel_i = io["sel"]; yT_o = io["yT"]

    SEL = p.sbuf("SEL", [32, 32, 128], BF16)
    p.dma("sync", SEL[:], sel_i, writes=[SEL])
    acols = p.sbuf("acols", [128, 8, 16], F32)
    atmp = p.sbuf("atmp", [128, 16], F32)
    atmp2 = p.sbuf("atmp2", [128, 16], F32)
    for db in range(8):
        p.dma("sync", atmp[:], alog_i[db], writes=[atmp])
        p.op("scalar", lambda e: e.activation(out=atmp2[:], in_=atmp[:], func=AF.Exp),
             reads=[atmp], writes=[atmp2])
        p.op("vector", lambda e, db=db: e.tensor_scalar(out=acols[:, db, :], in0=atmp2[:], scalar1=-1.0,
                                                        scalar2=None, op0=ALU.mult),
             reads=[atmp2], writes=[(acols, db)])

    bcc = p.sbuf("bcc", [32, SCH], BF16)
    Bb = p.sbuf("Bb", [128, 16, SCH], BF16)
    Cb = p.sbuf("Cb", [128, 16, SCH], BF16)
    bc_ps = [p.psum(f"bc_ps{i}", [128, SCH], F32) for i in range(2)]
    dA = [p.sbuf(f"dA{i}", [128, 16, SCH], F32) for i in range(2)]
    dBu = [p.sbuf(f"dBu{i}", [128, 16, SCH], BF16) for i in range(2)]
    hb = p.sbuf("hb", [128, 16, SCH], BF16)
    ta = p.sbuf("ta", [128, 8, SCH], BF16)
    tb = p.sbuf("tb", [128, 4, SCH], BF16)
    tc = p.sbuf("tc", [128, 2, SCH], BF16)
    yv = p.sbuf("yv", [128, SCH], BF16)
    dTt = [p.sbuf(f"dTt{i}", [128, SCH], BF16) for i in range(2)]
    duTt = [p.sbuf(f"duTt{i}", [128, SCH], BF16) for i in range(2)]
    carry = p.sbuf("carry", [128, 8, 16], F32)
    p.op("vector", lambda e: e.memset(carry[:], 0.0), writes=[carry])

    for ci in range(NCHS):
        c0 = ci * SCH
        p.dma("sync", bcc[:], bcT_i[:, c0:c0 + SCH], writes=[bcc])
        for s in range(16):
            ps = bc_ps[s % 2]
            p.op("tensor", lambda e, s=s, ps=ps: e.matmul(out=ps[:], lhsT=SEL[:, s, :], rhs=bcc[:],
                                                          start=True, stop=True),
                 reads=[SEL, bcc], writes=[ps])
            p.op("scalar", lambda e, s=s, ps=ps: e.activation(out=Bb[:, s, :], in_=ps[:], func=AF.Copy),
                 reads=[ps], writes=[(Bb, s)])
            ps2 = bc_ps[(s + 1) % 2]
            p.op("tensor", lambda e, s=s, ps2=ps2: e.matmul(out=ps2[:], lhsT=SEL[:, 16 + s, :], rhs=bcc[:],
                                                            start=True, stop=True),
                 reads=[SEL, bcc], writes=[ps2])
            p.op("scalar", lambda e, s=s, ps2=ps2: e.activation(out=Cb[:, s, :], in_=ps2[:], func=AF.Copy),
                 reads=[ps2], writes=[(Cb, s)])
        for db in range(8):
            par = db % 2
            dt_, du_ = dTt[par], duTt[par]
            p.dma("sync", dt_[:], dT_i[db * 128:(db + 1) * 128, c0:c0 + SCH], writes=[dt_])
            p.dma("sync", du_[:], duT_i[db * 128:(db + 1) * 128, c0:c0 + SCH], writes=[du_])
            DA, DB = dA[par], dBu[par]
            for s in range(16):
                p.op("scalar", lambda e, s=s, db=db, DA=DA, dt_=dt_: e.activation(
                    out=DA[:, s, :], in_=dt_[:], func=AF.Exp, scale=acols[:, db, s:s + 1]),
                     reads=[dt_, (acols, db)], writes=[(DA, s)])
                p.op("vector", lambda e, s=s, DB=DB, du_=du_: e.tensor_mul(
                    out=DB[:, s, :], in0=du_[:], in1=Bb[:, s, :]),
                     reads=[du_, (Bb, s)], writes=[(DB, s)])
            for s in range(16):
                p.op("vector", lambda e, s=s, db=db, DA=DA, DB=DB: e.tensor_tensor_scan(
                    out=hb[:, s, :], data0=DA[:, s, :], data1=DB[:, s, :],
                    initial=carry[:, db, s:s + 1], op0=ALU.mult, op1=ALU.add),
                     reads=[(DA, s), (DB, s), (carry, db)], writes=[(hb, s)])
            p.op("vector", lambda e, db=db: e.tensor_copy(
                out=carry[:, db, :], in_=bass.AP(hb[:].tensor, hb[:].offset + SCH - 1,
                                                 [[hb[:].ap[0][0], 128], [SCH, 16]])),
                 reads=[hb], writes=[(carry, db)])
            p.op("vector", lambda e, DB=DB: e.tensor_mul(out=DB[:], in0=hb[:], in1=Cb[:]),
                 reads=[hb, Cb], writes=[DB])
            p.op("vector", lambda e, DB=DB: e.tensor_add(out=ta[:], in0=DB[:, 0:8, :], in1=DB[:, 8:16, :]),
                 reads=[DB], writes=[ta])
            p.op("vector", lambda e: e.tensor_add(out=tb[:], in0=ta[:, 0:4, :], in1=ta[:, 4:8, :]),
                 reads=[ta], writes=[tb])
            p.op("vector", lambda e: e.tensor_add(out=tc[:], in0=tb[:, 0:2, :], in1=tb[:, 2:4, :]),
                 reads=[tb], writes=[tc])
            p.op("vector", lambda e: e.tensor_add(out=yv[:], in0=tc[:, 0, :], in1=tc[:, 1, :]),
                 reads=[tc], writes=[yv])
            p.dma("gpsimd", yT_o[db * 128:(db + 1) * 128, c0:c0 + SCH], yv[:], reads=[yv], sem=f"yst{par}")
    return


def build_scan(T=S, tag="scan"):
    p = Prog(tag)
    io = dict(
        dT=p.dram("dT", [1024, T], BF16, kind="ExternalInput").ap(),
        duT=p.dram("duT", [1024, T], BF16, kind="ExternalInput").ap(),
        bcT=p.dram("bcT", [32, T], BF16, kind="ExternalInput").ap(),
        alog=p.dram("alog", [8, 128, 16], F32, kind="ExternalInput").ap(),
        sel=p.dram("sel", [32, 32, 128], BF16, kind="ExternalInput").ap(),
        yT=p.dram("yT", [1024, T], BF16, kind="ExternalOutput").ap(),
    )
    with p.stage("sc"):
        emit_scan(p, io, T=T)
    p.finalize()
    return p



def emit_mback(p, io, yT_colf=None, y_prefetch=None):
    NCH = TLOC // CHUNK
    x_in = io["x"]
    yT_i = {d_: io.get(f"yT_{d_}") for d_ in "fb"}
    uT_i = {d_: io[f"uT_{d_}"] for d_ in "fb"}
    zT_i = {d_: io[f"zT_{d_}"] for d_ in "fb"}
    Dv = {d_: io[f"Dv_{d_}"] for d_ in "fb"}
    outw = {d_: io[f"outw_{d_}"] for d_ in "fb"}
    fw = io["fw"]; fb_ = io["fb"]; ident_in = io["ident_in"]; x_out = io["xo"]

    ident = p.sbuf("ident", [128, 128], F32)
    p.dma("sync", ident[:], ident_in, writes=[ident])
    ones_row = p.sbuf("ones_row", [1, 128], BF16)
    p.op("vector", lambda e: e.memset(ones_row[:], 1.0), writes=[ones_row])
    fbS = p.sbuf("fbS", [1, 1024], BF16)
    fbf = p.sbuf("fbf", [1, 1024], F32)
    p.dma("sync", fbf[:], fb_, writes=[fbf])
    p.op("vector", lambda e: e.tensor_copy(out=fbS[:], in_=fbf[:]), reads=[fbf], writes=[fbS])
    Dcol = {}
    for d_ in "fb":
        Dcol[d_] = p.sbuf(f"Dcol{d_}", [128, 16], F32)
        p.dma("sync", Dcol[d_][:], Dv[d_], writes=[Dcol[d_]])

    owT = {d_: p.sbuf(f"owT{d_}", [128, 16, 1024], BF16) for d_ in "fb"}
    fwT = p.sbuf("fwT", [128, 16, 1024], BF16)
    wld = p.sbuf("wld", [128, 2048], F32)
    tr_ps = p.psum("tr_ps", [128, 128], F32)
    def prep_T(dst, W, nrow_blocks, ncol_blocks):
        for rb in range(nrow_blocks):
            p.dma("sync", wld[:, 0:ncol_blocks * 128], W[rb * 128:(rb + 1) * 128, :], writes=[wld])
            for kb in range(ncol_blocks):
                sl = wld[:, kb * 128:(kb + 1) * 128]
                p.op("tensor", lambda e, sl=sl: e.transpose(out=tr_ps[:], in_=sl, identity=ident[:]),
                     reads=[wld, ident], writes=[tr_ps])
                dsl = dst[:, kb, rb * 128:(rb + 1) * 128]
                p.op("scalar", lambda e, dsl=dsl: e.activation(out=dsl, in_=tr_ps[:], func=AF.Copy),
                     reads=[tr_ps], writes=[(dst, (kb, rb))])
    for d_ in "fb":
        prep_T(owT[d_], outw[d_], 8, 16)
    prep_T(fwT, fw, 8, 16)

    yt_p = [p.sbuf(f"yt{i}", [128, CHUNK], BF16) for i in range(2)]
    ut_p = [p.sbuf(f"ut{i}", [128, CHUNK], BF16) for i in range(2)]
    zt_p = [p.sbuf(f"zt{i}", [128, CHUNK], BF16) for i in range(2)]
    sz_p = [p.sbuf(f"sz{i}", [128, CHUNK], F32) for i in range(2)]
    g1_p = [p.sbuf(f"g1{i}", [128, CHUNK], F32) for i in range(2)]
    yg = p.sbuf("yg", [128, 16, CHUNK], BF16)
    oT = p.sbuf("oT", [128, 16, CHUNK], BF16)
    xr = [p.sbuf(f"xr{i}", [128, D], F32) for i in range(2)]
    xo_t = [p.sbuf(f"xo_t{i}", [128, D], F32) for i in range(2)]
    mm_ps = [p.psum(f"mm_ps{i}", [128, 512], F32) for i in range(4)]

    kb0 = 0
    for b in range(B):
        for ci in range(NCH):
            c0 = ci * CHUNK
            if y_prefetch is not None:
                y_prefetch(b, ci)
            for di, d_ in enumerate("fb"):
                for kb in range(16):
                    yt, ut, zt, sz, g1 = (yt_p[kb % 2], ut_p[kb % 2], zt_p[kb % 2],
                                          sz_p[kb % 2], g1_p[kb % 2])
                    if yT_colf is not None:
                        ysl = yT_colf(d_, b, kb, c0)
                        p.op("vector", lambda e, ysl=ysl, yt=yt, ut=ut, zt=zt, sz=sz, g1=g1: e.tensor_copy(out=yt[:], in_=ysl),
                             reads=[("yslab", None)], writes=[yt])
                    else:
                        p.dma("sync", yt[:], yT_i[d_][b, kb * 128:(kb + 1) * 128, c0:c0 + CHUNK],
                              writes=[yt])
                    p.dma("sync", ut[:], uT_i[d_][b, kb * 128:(kb + 1) * 128, c0:c0 + CHUNK], writes=[ut])
                    p.dma("sync", zt[:], zT_i[d_][b, kb * 128:(kb + 1) * 128, c0:c0 + CHUNK], writes=[zt])
                    p.op("scalar", lambda e, yt=yt, ut=ut, zt=zt, sz=sz, g1=g1: e.activation(out=sz[:], in_=zt[:], func=AF.Silu),
                         reads=[zt], writes=[sz])
                    p.op("vector", lambda e, kb=kb, d_=d_, yt=yt, ut=ut, zt=zt, sz=sz, g1=g1: e.scalar_tensor_tensor(
                        out=g1[:], in0=ut[:], scalar=Dcol[d_][:, kb:kb + 1], in1=yt[:],
                        op0=ALU.mult, op1=ALU.add),
                         reads=[ut, Dcol[d_], yt], writes=[g1])
                    p.op("vector", lambda e, kb=kb, yt=yt, ut=ut, zt=zt, sz=sz, g1=g1: e.tensor_mul(out=yg[:, kb, :], in0=g1[:], in1=sz[:]),
                         reads=[g1, sz], writes=[(yg, kb)])
                for och in range(8):
                    ps = mm_ps[och % 2]
                    def f(e, och=och, d_=d_, ps=ps):
                        last = None
                        for kb in range(16):
                            last = e.matmul(out=ps[:], lhsT=owT[d_][:, kb, och * 128:(och + 1) * 128],
                                            rhs=yg[:, kb, :], start=(kb == 0), stop=(kb == 15))
                        return last
                    p.op("tensor", f, reads=[yg, owT[d_]], writes=[ps])
                    p.op("scalar", lambda e, och=och, di=di, ps=ps, yt=yt, ut=ut, zt=zt, sz=sz, g1=g1: e.activation(
                        out=oT[:, di * 8 + och, :], in_=ps[:], func=AF.Copy),
                         reads=[ps], writes=[(oT, di * 8 + och)])
            for tt in range(CHUNK // 128):
                X = xr[tt % 2]
                XO = xo_t[tt % 2]
                p.dma("sync", X[:], x_in[b, c0 + tt * 128:c0 + (tt + 1) * 128, :], writes=[X])
                for oc in range(2):
                    ps = mm_ps[2 + oc]
                    def f2(e, tt=tt, oc=oc, ps=ps):
                        e.matmul(out=ps[:], lhsT=ones_row[:], rhs=fbS[:, oc * 512:(oc + 1) * 512],
                                 start=True, stop=False)
                        last = None
                        for kb in range(16):
                            last = e.matmul(out=ps[:], lhsT=oT[:, kb, tt * 128:(tt + 1) * 128],
                                            rhs=fwT[:, kb, oc * 512:(oc + 1) * 512],
                                            start=False, stop=(kb == 15))
                        return last
                    p.op("tensor", f2, reads=[oT, fwT, ones_row, fbS], writes=[ps])
                    p.op("vector", lambda e, oc=oc, ps=ps, X=X, XO=XO, yt=yt, ut=ut, zt=zt, sz=sz, g1=g1: e.tensor_add(
                        out=XO[:, oc * 512:(oc + 1) * 512], in0=ps[:], in1=X[:, oc * 512:(oc + 1) * 512]),
                         reads=[ps, X], writes=[(XO, oc)])
                p.dma("gpsimd", x_out[b, c0 + tt * 128:c0 + (tt + 1) * 128, :], XO[:],
                      reads=[XO], sem=f"xst{tt % 2}")
    return


def _mback_w_io(p):
    io = dict(
        fw=p.dram("fw", [1024, 2048], F32, kind="ExternalInput").ap(),
        fb=p.dram("fb", [1, 1024], F32, kind="ExternalInput").ap(),
    )
    for d_ in "fb":
        io[f"Dv_{d_}"] = p.dram(f"Dv_{d_}", [128, 16], F32, kind="ExternalInput").ap()
        io[f"outw_{d_}"] = p.dram(f"outw_{d_}", [1024, 2048], F32, kind="ExternalInput").ap()
    return io


def build_mamba_back(tag="mback"):
    p = Prog(tag)
    io = _mback_w_io(p)
    io["x"] = p.dram("x", [B, TLOC, D], F32, kind="ExternalInput").ap()
    io["ident_in"] = p.dram("ident_in", [128, 128], F32, kind="ExternalInput").ap()
    io["xo"] = p.dram("xo", [B, TLOC, D], F32, kind="ExternalOutput").ap()
    for d_ in "fb":
        for nm in ("yT", "uT", "zT"):
            io[f"{nm}_{d_}"] = p.dram(f"{nm}_{d_}", [B, 2048, TLOC], BF16, kind="ExternalInput").ap()
    with p.stage("mb"):
        emit_mback(p, io)
    p.finalize()
    return p



def emit_l1ffn(p, io):
    NCH = TLOC // CHUNK
    x_in = io["x"]
    Wr_i = io["Wr"]; Wk_i = io["Wk"]; Wv_i = io["Wv"]
    mur_i = io["mur"]; muk_i = io["muk"]
    ident_in = io["ident_in"]; x_out = io["xo"]

    ident = p.sbuf("ident", [128, 128], F32)
    identb = p.sbuf("identb", [128, 128], BF16)
    p.dma("sync", ident[:], ident_in, writes=[ident])
    p.op("vector", lambda e: e.tensor_copy(out=identb[:], in_=ident[:]), reads=[ident], writes=[identb])
    mur = p.sbuf("mur_s", [128, 8], F32)
    muk = p.sbuf("muk_s", [128, 8], F32)
    murm = p.sbuf("murm", [128, 8], F32)
    mukm = p.sbuf("mukm", [128, 8], F32)
    p.dma("sync", mur[:], mur_i, writes=[mur])
    p.dma("sync", muk[:], muk_i, writes=[muk])
    for src_, dst_ in ((mur, murm), (muk, mukm)):
        p.op("vector", lambda e, src_=src_, dst_=dst_: e.tensor_scalar(
            out=dst_[:], in0=src_[:], scalar1=-1.0, scalar2=1.0, op0=ALU.mult, op1=ALU.add),
             reads=[src_], writes=[dst_])

    WT = {n: p.sbuf(f"WT{n}", [128, 8, D], BF16) for n in "rkv"}
    wld = p.sbuf("wld", [128, D], F32)
    tr_ps = p.psum("tr_ps", [128, 128], F32)
    for n, W in (("r", Wr_i), ("k", Wk_i), ("v", Wv_i)):
        for rb in range(8):
            p.dma("sync", wld[:], W[rb * 128:(rb + 1) * 128, :], writes=[wld])
            for kb in range(8):
                sl = wld[:, kb * 128:(kb + 1) * 128]
                p.op("tensor", lambda e, sl=sl: e.transpose(out=tr_ps[:], in_=sl, identity=ident[:]),
                     reads=[wld, ident], writes=[tr_ps])
                dsl = WT[n][:, kb, rb * 128:(rb + 1) * 128]
                p.op("scalar", lambda e, dsl=dsl: e.activation(out=dsl, in_=tr_ps[:], func=AF.Copy),
                     reads=[tr_ps], writes=[(WT[n], (kb, rb))])

    xt = [p.sbuf(f"xt{i}", [128, D], F32) for i in range(2)]
    sq = [p.sbuf(f"sq{i}", [128, D], F32) for i in range(2)]
    lnb = [p.sbuf(f"lnb{i}", [128, D], BF16) for i in range(2)]
    stats = [make_stat(p, f"st{i}") for i in range(2)]
    trb_ps = [p.psum(f"trb_ps{i}", [128, 128], BF16) for i in range(2)]
    lnT = p.sbuf("lnT", [128, 8, CHUNK + 1], BF16)
    xrT = p.sbuf("xrT", [128, 8, CHUNK], BF16)
    xkT = p.sbuf("xkT", [128, 8, CHUNK], BF16)
    t1 = p.sbuf("t1", [128, CHUNK], BF16)
    kaT = p.sbuf("kaT", [128, 8, CHUNK], BF16)
    krel = p.sbuf("krel", [128, CHUNK], F32)
    rS = p.sbuf("rS", [128, 1024], F32)
    xc = [p.sbuf(f"xc{i}", [128, D], F32) for i in range(2)]
    vo = p.sbuf("vo", [128, D], F32)
    xo_t = [p.sbuf(f"xo_t{i}", [128, D], F32) for i in range(2)]
    mm_ps = [p.psum(f"mm_ps{i}", [128, 512], F32) for i in range(4)]

    for b in range(B):
        for ci in range(NCH):
            c0 = ci * CHUNK
            if ci == 0:
                X = xt[0]
                p.dma("sync", X[0:1, :], x_in[b, 0:1, :], writes=[X])
                st0 = stats[0]
                ln_stats_n(p, X, sq[0], st0, 1)
                p.op("scalar", lambda e, X=X, st0=st0: e.activation(
                    out=lnb[0][0:1, :], in_=X[0:1, :], func=AF.Identity,
                    bias=st0["negmur"][0:1, 0:1], scale=st0["rstd"][0:1, 0:1]),
                     reads=[X, st0["negmur"], st0["rstd"]], writes=[lnb[0]])
                for kb in range(8):
                    tp = trb_ps[kb % 2]
                    p.op("tensor", lambda e, kb=kb, tp=tp: e.transpose(
                        out=tp[:, 0:1], in_=lnb[0][0:1, kb * 128:(kb + 1) * 128], identity=identb[0:1, 0:1]),
                         reads=[lnb[0], identb], writes=[tp])
                    p.op("scalar", lambda e, kb=kb, tp=tp: e.activation(
                        out=lnT[:, kb, 0:1], in_=tp[:, 0:1], func=AF.Copy),
                         reads=[tp], writes=[(lnT, ("h", kb))])
            else:
                p.op("vector", lambda e: e.tensor_copy(out=lnT[:, :, 0:1], in_=lnT[:, :, CHUNK:CHUNK + 1]),
                     reads=[lnT], writes=[lnT])
            for tt in range(CHUNK // 128):
                par = tt % 2
                X, SQ, LNB, st = xt[par], sq[par], lnb[par], stats[par]
                p.dma("sync", X[:], x_in[b, 1 + c0 + tt * 128:1 + c0 + (tt + 1) * 128, :], writes=[X])
                ln_stats_n(p, X, SQ, st, 128)
                p.op("scalar", lambda e, X=X, LNB=LNB, st=st: e.activation(
                    out=LNB[:], in_=X[:], func=AF.Identity,
                    bias=st["negmur"][:, 0:1], scale=st["rstd"][:, 0:1]),
                     reads=[X, st["negmur"], st["rstd"]], writes=[LNB])
                for kb in range(8):
                    tp = trb_ps[kb % 2]
                    p.op("tensor", lambda e, kb=kb, tp=tp, LNB=LNB: e.transpose(
                        out=tp[:], in_=LNB[:, kb * 128:(kb + 1) * 128], identity=identb[:]),
                         reads=[LNB, identb], writes=[tp])
                    p.op("scalar", lambda e, kb=kb, tp=tp, tt=tt: e.activation(
                        out=lnT[:, kb, 1 + tt * 128:1 + (tt + 1) * 128], in_=tp[:], func=AF.Copy),
                         reads=[tp], writes=[(lnT, (kb, tt))])
            for kb in range(8):
                p.op("vector", lambda e, kb=kb: e.tensor_scalar(
                    out=t1[:], in0=lnT[:, kb, 1:1 + CHUNK], scalar1=mur[:, kb:kb + 1],
                    scalar2=None, op0=ALU.mult),
                     reads=[lnT, mur], writes=[t1])
                p.op("vector", lambda e, kb=kb: e.scalar_tensor_tensor(
                    out=xrT[:, kb, :], in0=lnT[:, kb, 0:CHUNK], scalar=murm[:, kb:kb + 1],
                    in1=t1[:], op0=ALU.mult, op1=ALU.add),
                     reads=[lnT, murm, t1], writes=[(xrT, kb)])
                p.op("vector", lambda e, kb=kb: e.tensor_scalar(
                    out=t1[:], in0=lnT[:, kb, 1:1 + CHUNK], scalar1=muk[:, kb:kb + 1],
                    scalar2=None, op0=ALU.mult),
                     reads=[lnT, muk], writes=[t1])
                p.op("vector", lambda e, kb=kb: e.scalar_tensor_tensor(
                    out=xkT[:, kb, :], in0=lnT[:, kb, 0:CHUNK], scalar=mukm[:, kb:kb + 1],
                    in1=t1[:], op0=ALU.mult, op1=ALU.add),
                     reads=[lnT, mukm, t1], writes=[(xkT, kb)])
            for och in range(8):
                ps = mm_ps[och % 2]
                def fk(e, och=och, ps=ps):
                    last = None
                    for kb in range(8):
                        last = e.matmul(out=ps[:], lhsT=WT["k"][:, kb, och * 128:(och + 1) * 128],
                                        rhs=xkT[:, kb, :], start=(kb == 0), stop=(kb == 7))
                    return last
                p.op("tensor", fk, reads=[xkT, WT["k"]], writes=[ps])
                p.op("scalar", lambda e, ps=ps: e.activation(out=krel[:], in_=ps[:], func=AF.Relu),
                     reads=[ps], writes=[krel])
                p.op("scalar", lambda e, och=och: e.activation(out=kaT[:, och, :], in_=krel[:], func=AF.Square),
                     reads=[krel], writes=[(kaT, och)])
            for tt in range(CHUNK // 128):
                for oc in range(2):
                    ps = mm_ps[oc]
                    def fr(e, tt=tt, oc=oc, ps=ps):
                        last = None
                        for kb in range(8):
                            last = e.matmul(out=ps[:], lhsT=xrT[:, kb, tt * 128:(tt + 1) * 128],
                                            rhs=WT["r"][:, kb, oc * 512:(oc + 1) * 512],
                                            start=(kb == 0), stop=(kb == 7))
                        return last
                    p.op("tensor", fr, reads=[xrT, WT["r"]], writes=[ps])
                    p.op("scalar", lambda e, oc=oc, ps=ps: e.activation(
                        out=rS[:, oc * 512:(oc + 1) * 512], in_=ps[:], func=AF.Sigmoid),
                         reads=[ps], writes=[(rS, oc)])
                for oc in range(2):
                    ps = mm_ps[2 + oc]
                    def fv(e, tt=tt, oc=oc, ps=ps):
                        last = None
                        for kb in range(8):
                            last = e.matmul(out=ps[:], lhsT=kaT[:, kb, tt * 128:(tt + 1) * 128],
                                            rhs=WT["v"][:, kb, oc * 512:(oc + 1) * 512],
                                            start=(kb == 0), stop=(kb == 7))
                        return last
                    p.op("tensor", fv, reads=[kaT, WT["v"]], writes=[ps])
                    p.op("vector", lambda e, oc=oc, ps=ps: e.tensor_mul(
                        out=vo[:, oc * 512:(oc + 1) * 512],
                        in0=rS[:, oc * 512:(oc + 1) * 512], in1=ps[:]),
                         reads=[(rS, oc), ps], writes=[(vo, oc)])
                XO = xo_t[tt % 2]
                X2 = xc[tt % 2]
                p.dma("sync", X2[:], x_in[b, 1 + c0 + tt * 128:1 + c0 + (tt + 1) * 128, :], writes=[X2])
                p.op("vector", lambda e, XO=XO, X2=X2: e.scalar_tensor_tensor(
                    out=XO[:], in0=vo[:], scalar=0.5, in1=X2[:], op0=ALU.mult, op1=ALU.add),
                     reads=[vo, X2], writes=[XO])
                p.dma("gpsimd", x_out[b, c0 + tt * 128:c0 + (tt + 1) * 128, :], XO[:],
                      reads=[XO], sem=f"xst{tt % 2}")
    return


def build_l1ffn(tag="l1ffn"):
    p = Prog(tag)
    io = dict(
        x=p.dram("x", [B, TLOC + 1, D], F32, kind="ExternalInput").ap(),
        Wr=p.dram("Wr", [D, D], F32, kind="ExternalInput").ap(),
        Wk=p.dram("Wk", [D, D], F32, kind="ExternalInput").ap(),
        Wv=p.dram("Wv", [D, D], F32, kind="ExternalInput").ap(),
        mur=p.dram("mur", [128, 8], F32, kind="ExternalInput").ap(),
        muk=p.dram("muk", [128, 8], F32, kind="ExternalInput").ap(),
        ident_in=p.dram("ident_in", [128, 128], F32, kind="ExternalInput").ap(),
        xo=p.dram("xo", [B, TLOC, D], F32, kind="ExternalOutput").ap(),
    )
    with p.stage("lf"):
        emit_l1ffn(p, io)
    p.finalize()
    return p



def emit_l1af(p, io):
    H = 3
    NCH = TLOC // CHUNK
    x_in = io["x"]
    Wq_i = io["Wq"]; Wk_i = io["Wk"]; Wv_i = io["Wv"]
    bqkv_i = io["bqkv"]; cw_i = io["cw"]; cb_i = io["cb"]; hmask_i = io["hmask"]
    lncols_i = io["lncols"]; ident_in = io["ident_in"]
    g1_o = io["g1"]; qT_o = io["qT"]; kT_o = io["kT"]; vT_o = io["vT"]

    ident = p.sbuf("ident", [128, 128], F32)
    identb = p.sbuf("identb", [128, 128], BF16)
    p.dma("sync", ident[:], ident_in, writes=[ident])
    p.op("vector", lambda e: e.tensor_copy(out=identb[:], in_=ident[:]), reads=[ident], writes=[identb])
    bqkv = p.sbuf("bqkv_s", [128, 3, 8], F32)
    p.dma("sync", bqkv[:], bqkv_i, writes=[bqkv])
    lncols = p.sbuf("lncols_s", [128, 6, 8], F32)
    p.dma("sync", lncols[:], lncols_i, writes=[lncols])
    cbS = p.sbuf("cbS", [1, 1024], BF16)
    cbf = p.sbuf("cbf", [1, 1024], F32)
    p.dma("sync", cbf[:], cb_i, writes=[cbf])
    p.op("vector", lambda e: e.tensor_copy(out=cbS[:], in_=cbf[:]), reads=[cbf], writes=[cbS])
    ones_row = p.sbuf("ones_row", [1, 128], BF16)
    p.op("vector", lambda e: e.memset(ones_row[:], 1.0), writes=[ones_row])

    WT = {n: p.sbuf(f"WT{n}", [128, 8, D], BF16) for n in "qkv"}
    wld = p.sbuf("wld", [128, D], F32)
    tr_ps = p.psum("tr_ps", [128, 128], F32)
    for n, W in (("q", Wq_i), ("k", Wk_i), ("v", Wv_i)):
        for rb in range(8):
            p.dma("sync", wld[:], W[rb * 128:(rb + 1) * 128, :], writes=[wld])
            for kb in range(8):
                sl = wld[:, kb * 128:(kb + 1) * 128]
                p.op("tensor", lambda e, sl=sl: e.transpose(out=tr_ps[:], in_=sl, identity=ident[:]),
                     reads=[wld, ident], writes=[tr_ps])
                dsl = WT[n][:, kb, rb * 128:(rb + 1) * 128]
                p.op("scalar", lambda e, dsl=dsl: e.activation(out=dsl, in_=tr_ps[:], func=AF.Copy),
                     reads=[tr_ps], writes=[(WT[n], (kb, rb))])
    WcT = p.sbuf("WcT", [128, 4, 4, 2, 256], BF16)
    for g in range(4):
        for ob in range(2):
            for k_ in range(4):
                p.dma("sync", wld[:, 0:256], cw_i[k_, g * 256 + ob * 128:g * 256 + (ob + 1) * 128, :],
                      writes=[wld], sem="wld_c")
                for cb in range(2):
                    sl = wld[:, cb * 128:(cb + 1) * 128]
                    p.op("tensor", lambda e, sl=sl: e.transpose(out=tr_ps[:], in_=sl, identity=ident[:]),
                         reads=[wld, ident], writes=[tr_ps])
                    dsl = WcT[:, g, k_, cb, ob * 128:(ob + 1) * 128]
                    p.op("scalar", lambda e, dsl=dsl: e.activation(out=dsl, in_=tr_ps[:], func=AF.Copy),
                         reads=[tr_ps], writes=[(WcT, (g, k_, cb, ob))])

    hmask = p.sbuf("hmask_s", [128, 1], F32)
    p.dma("sync", hmask[:], hmask_i, writes=[hmask])
    hhalo = p.sbuf("hhalo", [128, 8, 4], BF16)
    xt = [p.sbuf(f"xt{i}", [128, D], F32) for i in range(2)]
    sq = [p.sbuf(f"sq{i}", [128, D], F32) for i in range(2)]
    lnb = [p.sbuf(f"lnb{i}", [128, D], BF16) for i in range(2)]
    stats = [make_stat(p, f"st{i}") for i in range(2)]
    trb_ps = [p.psum(f"trb_ps{i}", [128, 128], BF16) for i in range(2)]
    hcT = p.sbuf("hcT", [128, 8, CHUNK + H], BF16)
    haT = p.sbuf("haT", [128, 8, CHUNK], BF16)
    g1t = [p.sbuf(f"g1t{i}", [128, D], F32) for i in range(2)]
    qkv_ev = p.sbuf("qkv_ev", [128, CHUNK], BF16)
    cps = [p.psum(f"cps{i}", [128, 512], F32) for i in range(2)]
    mm_ps = [p.psum(f"mm_ps{i}", [128, 512], F32) for i in range(2)]

    g0buf = [p.sbuf(f"g0buf{i}", [128, D], F32) for i in range(2)]
    g0keep = p.sbuf("g0keep", [128, CHUNK // 128, D], F32)

    def ln_tr(b, tok0, ntok, col0, dst, scale_idx, bias_idx, x_from=None, par=0, keep_tt=None):
        X, SQ, LNB, st = xt[par], sq[par], lnb[par], stats[par]
        if x_from is None:
            p.dma("sync", X[0:ntok, :], x_in[b, tok0:tok0 + ntok, :], writes=[X])
            G0 = g0buf[par]
            ln_stats_n(p, X, SQ, st, ntok)
            p.op("scalar", lambda e, X=X, G0=G0, st=st, ntok=ntok: e.activation(
                out=G0[0:ntok, :], in_=X[0:ntok, :], func=AF.Identity,
                bias=st["negmur"][0:ntok, 0:1], scale=st["rstd"][0:ntok, 0:1]),
                 reads=[X, st["negmur"], st["rstd"]], writes=[G0])
            if keep_tt is not None:
                p.op("vector", lambda e, G0=G0, keep_tt=keep_tt: e.tensor_copy(
                    out=g0keep[0:ntok, keep_tt, :], in_=G0[0:ntok, :]),
                     reads=[G0], writes=[(g0keep, keep_tt)])
            src_t = G0
        else:
            src_t = x_from
        ln_stats_n(p, src_t, SQ, st, ntok)
        p.op("scalar", lambda e, src_t=src_t, LNB=LNB, st=st, ntok=ntok: e.activation(
            out=LNB[0:ntok, :], in_=src_t[0:ntok, :], func=AF.Identity,
            bias=st["negmur"][0:ntok, 0:1], scale=st["rstd"][0:ntok, 0:1]),
             reads=[src_t, st["negmur"], st["rstd"]], writes=[LNB])
        for kb in range(8):
            tp = trb_ps[kb % 2]
            idb = identb[0:ntok, 0:ntok]
            p.op("tensor", lambda e, kb=kb, tp=tp, LNB=LNB, ntok=ntok, idb=idb: e.transpose(
                out=tp[:, 0:ntok], in_=LNB[0:ntok, kb * 128:(kb + 1) * 128], identity=idb),
                 reads=[LNB, identb], writes=[tp])
            dsl = dst[:, kb, col0:col0 + ntok]
            p.op("scalar", lambda e, dsl=dsl, tp=tp, kb=kb, ntok=ntok: e.activation(
                out=dsl, in_=tp[:, 0:ntok], func=AF.Identity,
                scale=lncols[:, scale_idx, kb:kb + 1], bias=lncols[:, bias_idx, kb:kb + 1]),
                 reads=[tp, lncols], writes=[(dst, (kb, col0))])

    for b in range(B):
        for ci in range(NCH):
            c0 = ci * CHUNK
            if ci == 0:
                ln_tr(b, 0, H, 0, hhalo, 0, 1)
                p.op("vector", lambda e: e.tensor_scalar(out=hcT[:, :, 0:H], in0=hhalo[:, :, 0:H],
                                                         scalar1=hmask[:, 0:1], scalar2=None, op0=ALU.mult),
                     reads=[hhalo, hmask], writes=[hcT])
            else:
                p.op("vector", lambda e: e.tensor_copy(out=hcT[:, :, 0:H], in_=hcT[:, :, CHUNK:CHUNK + H]),
                     reads=[hcT], writes=[hcT])
            for tt in range(CHUNK // 128):
                ln_tr(b, H + c0 + tt * 128, 128, H + tt * 128, hcT, 0, 1, par=tt % 2, keep_tt=tt)
            for tt in range(CHUNK // 128):
                for g in range(4):
                    ps = cps[g // 2]
                    colr = (g % 2) * 256
                    def fc(e, g=g, tt=tt, ps=ps, colr=colr):
                        e.matmul(out=ps[:, colr:colr + 256], lhsT=ones_row[:],
                                 rhs=cbS[:, g * 256:(g + 1) * 256], start=True, stop=False)
                        last = None
                        for k_ in range(4):
                            for cb in range(2):
                                last = e.matmul(out=ps[:, colr:colr + 256],
                                                lhsT=hcT[:, g * 2 + cb, tt * 128 + k_:tt * 128 + k_ + 128],
                                                rhs=WcT[:, g, k_, cb, :],
                                                start=False, stop=(k_ == 3 and cb == 1))
                        return last
                    p.op("tensor", fc, reads=[hcT, WcT, ones_row, cbS], writes=[(cps[g // 2], g % 2)])
                G1 = g1t[tt % 2]
                for half in range(2):
                    p.op("vector", lambda e, half=half, G1=G1, tt=tt: e.tensor_add(
                        out=G1[:, half * 512:(half + 1) * 512], in0=cps[half][:],
                        in1=g0keep[:, tt, half * 512:(half + 1) * 512]),
                         reads=[(cps[half], 0), (cps[half], 1), (g0keep, tt)], writes=[(G1, half)])
                p.dma("gpsimd", g1_o[b, c0 + tt * 128:c0 + (tt + 1) * 128, :], G1[:],
                      reads=[G1], sem=f"g1st{tt % 2}")
                ln_tr(b, 0, 128, tt * 128, haT, 2, 3, x_from=G1, par=tt % 2)
            for ni, n in enumerate("qkv"):
                dramo = {"q": qT_o, "k": kT_o, "v": vT_o}[n]
                for och in range(8):
                    ps = mm_ps[och % 2]
                    def fq(e, n=n, och=och, ps=ps):
                        last = None
                        for kb in range(8):
                            last = e.matmul(out=ps[:], lhsT=WT[n][:, kb, och * 128:(och + 1) * 128],
                                            rhs=haT[:, kb, :], start=(kb == 0), stop=(kb == 7))
                        return last
                    p.op("tensor", fq, reads=[haT, WT[n]], writes=[ps])
                    p.op("scalar", lambda e, ni=ni, och=och, ps=ps: e.activation(
                        out=qkv_ev[:], in_=ps[:], func=AF.Identity, bias=bqkv[:, ni, och:och + 1]),
                         reads=[ps, bqkv], writes=[qkv_ev])
                    p.dma("gpsimd", dramo[b, och * 128:(och + 1) * 128, c0:c0 + CHUNK], qkv_ev[:],
                          reads=[qkv_ev], sem="qkvst")
    return


def _l1af_io(p):
    return dict(
        x=p.dram("x", [B, TLOC + 3, D], F32, kind="ExternalInput").ap(),
        Wq=p.dram("Wq", [D, D], F32, kind="ExternalInput").ap(),
        Wk=p.dram("Wk", [D, D], F32, kind="ExternalInput").ap(),
        Wv=p.dram("Wv", [D, D], F32, kind="ExternalInput").ap(),
        bqkv=p.dram("bqkv", [128, 3, 8], F32, kind="ExternalInput").ap(),
        cw=p.dram("cw", [4, 1024, 256], F32, kind="ExternalInput").ap(),
        cb=p.dram("cb", [1, 1024], F32, kind="ExternalInput").ap(),
        hmask=p.dram("hmask", [128, 1], F32, kind="ExternalInput").ap(),
        lncols=p.dram("lncols", [128, 6, 8], F32, kind="ExternalInput").ap(),
        ident_in=p.dram("ident_in", [128, 128], F32, kind="ExternalInput").ap(),
        g1=p.dram("g1", [B, TLOC, D], F32, kind="ExternalOutput").ap(),
        qT=p.dram("qT", [B, D, TLOC], BF16, kind="ExternalOutput").ap(),
        kT=p.dram("kT", [B, D, TLOC], BF16, kind="ExternalOutput").ap(),
        vT=p.dram("vT", [B, D, TLOC], BF16, kind="ExternalOutput").ap(),
    )


def build_l1attn_front(tag="l1af"):
    p = Prog(tag)
    io = _l1af_io(p)
    with p.stage("la"):
        emit_l1af(p, io)
    p.finalize()
    return p



def emit_pooled(p, io, T=S):
    PCH = 2048
    NCHP = T // PCH
    qT_i = io["qT"]; kT_i = io["kT"]; vT_i = io["vT"]; rs_i = io["rs"]; pkvT_o = io["pkvT"]

    ones = p.sbuf("ones", [128, PCH], F32)
    p.op("vector", lambda e: e.memset(ones[:], 1.0), writes=[ones])
    rsb = p.sbuf("rsb", [128, PCH], F32)
    rsl = p.sbuf("rsl", [1, PCH], BF16)
    rslf = p.sbuf("rslf", [1, PCH], F32)
    one_col = p.sbuf("one_col", [1, 128], BF16)
    p.op("vector", lambda e: e.memset(one_col[:], 1.0), writes=[one_col])
    rs_ps = p.psum("rs_ps", [128, 512], F32)
    qt = p.sbuf("qt", [128, PCH], BF16)
    kt = p.sbuf("kt", [128, PCH], BF16)
    vt = p.sbuf("vt", [128, PCH], BF16)
    cum = p.sbuf("cum", [128, PCH], F32)
    pq = p.sbuf("pq", [128, PCH], F32)
    pqk = p.sbuf("pqk", [128, PCH], F32)
    cum2 = p.sbuf("cum2", [128, PCH], F32)
    pk = p.sbuf("pk", [128, PCH], F32)
    pkv = p.sbuf("pkv", [128, PCH], BF16)
    qcar = p.sbuf("qcar", [128, 1], F32)
    kcar = p.sbuf("kcar", [128, 1], F32)

    for b in range(B):
        p.op("vector", lambda e: e.memset(qcar[:], 0.0), writes=[qcar])
        p.op("vector", lambda e: e.memset(kcar[:], 0.0), writes=[kcar])
        for ci in range(NCHP):
            c0 = ci * PCH
            p.dma("sync", qt[:], qT_i[b, :, c0:c0 + PCH], writes=[qt])
            p.dma("sync", kt[:], kT_i[b, :, c0:c0 + PCH], writes=[kt])
            p.dma("sync", vt[:], vT_i[b, :, c0:c0 + PCH], writes=[vt])
            rbview = bass.AP(rs_i.tensor, rs_i.offset + c0, [[0, 128], [1, PCH]])
            p.dma("sync", rsb[:], rbview, writes=[rsb])
            p.op("vector", lambda e: e.tensor_tensor_scan(out=cum[:], data0=ones[:], data1=qt[:],
                                                          initial=qcar[:, 0:1], op0=ALU.mult, op1=ALU.add),
                 reads=[ones, qt, qcar], writes=[cum])
            p.op("vector", lambda e: e.tensor_copy(out=qcar[:], in_=cum[:, PCH - 1:PCH]),
                 reads=[cum], writes=[qcar])
            p.op("vector", lambda e: e.tensor_mul(out=pq[:], in0=cum[:], in1=rsb[:]),
                 reads=[cum, rsb], writes=[pq])
            p.op("vector", lambda e: e.tensor_mul(out=pqk[:], in0=pq[:], in1=kt[:]),
                 reads=[pq, kt], writes=[pqk])
            p.op("vector", lambda e: e.tensor_tensor_scan(out=cum2[:], data0=ones[:], data1=pqk[:],
                                                          initial=kcar[:, 0:1], op0=ALU.mult, op1=ALU.add),
                 reads=[ones, pqk, kcar], writes=[cum2])
            p.op("vector", lambda e: e.tensor_copy(out=kcar[:], in_=cum2[:, PCH - 1:PCH]),
                 reads=[cum2], writes=[kcar])
            p.op("vector", lambda e: e.tensor_mul(out=pk[:], in0=cum2[:], in1=rsb[:]),
                 reads=[cum2, rsb], writes=[pk])
            p.op("vector", lambda e: e.tensor_mul(out=pkv[:], in0=pk[:], in1=vt[:]),
                 reads=[pk, vt], writes=[pkv])
            p.dma("gpsimd", pkvT_o[b, :, c0:c0 + PCH], pkv[:], reads=[pkv], sem="pkvst")
    return


def build_pooled(T=S, tag="pooled"):
    p = Prog(tag)
    io = dict(
        qT=p.dram("qT", [B, 128, T], BF16, kind="ExternalInput").ap(),
        kT=p.dram("kT", [B, 128, T], BF16, kind="ExternalInput").ap(),
        vT=p.dram("vT", [B, 128, T], BF16, kind="ExternalInput").ap(),
        rs=p.dram("rs", [1, T], F32, kind="ExternalInput").ap(),
        pkvT=p.dram("pkvT", [B, 128, T], BF16, kind="ExternalOutput").ap(),
    )
    with p.stage("po"):
        emit_pooled(p, io, T=T)
    p.finalize()
    return p



def emit_l1ab(p, io):
    CHI = 256
    NCH = TLOC // CHI
    x_in = io["x"]; g1_i = io["g1"]; pkvT_i = io["pkvT"]
    Wb_i = io["Wb"]; Wu_i = io["Wu"]; bb_i = io["bb"]; bu_i = io["bu"]
    lncols_i = io["lncols"]; ident_in = io["ident_in"]; x_out = io["xo"]

    ident = p.sbuf("ident", [128, 128], F32)
    identb = p.sbuf("identb", [128, 128], BF16)
    p.dma("sync", ident[:], ident_in, writes=[ident])
    p.op("vector", lambda e: e.tensor_copy(out=identb[:], in_=ident[:]), reads=[ident], writes=[identb])
    bbcol = p.sbuf("bbcol", [128, 32], F32)
    p.dma("sync", bbcol[:], bb_i, writes=[bbcol])
    lncols = p.sbuf("lncols_s", [128, 2, 8], F32)
    p.dma("sync", lncols[:], lncols_i, writes=[lncols])
    buS = p.sbuf("buS", [1, D], BF16)
    buf_ = p.sbuf("buf_", [1, D], F32)
    p.dma("sync", buf_[:], bu_i, writes=[buf_])
    p.op("vector", lambda e: e.tensor_copy(out=buS[:], in_=buf_[:]), reads=[buf_], writes=[buS])
    ones_row = p.sbuf("ones_row", [1, 128], BF16)
    p.op("vector", lambda e: e.memset(ones_row[:], 1.0), writes=[ones_row])

    WbT = p.sbuf("WbT", [128, 8, 4096], BF16)
    WuT = p.sbuf("WuT", [128, 32, D], BF16)
    wld = p.sbuf("wld", [128, 1024], F32)
    tr_ps = p.psum("tr_ps", [128, 128], F32)
    for rb in range(32):
        p.dma("sync", wld[:], Wb_i[rb * 128:(rb + 1) * 128, :], writes=[wld])
        for kb in range(8):
            sl = wld[:, kb * 128:(kb + 1) * 128]
            p.op("tensor", lambda e, sl=sl: e.transpose(out=tr_ps[:], in_=sl, identity=ident[:]),
                 reads=[wld, ident], writes=[tr_ps])
            dsl = WbT[:, kb, rb * 128:(rb + 1) * 128]
            p.op("scalar", lambda e, dsl=dsl: e.activation(out=dsl, in_=tr_ps[:], func=AF.Copy),
                 reads=[tr_ps], writes=[(WbT, (kb, rb))])
    for rb in range(8):
        for qc in range(4):
            p.dma("sync", wld[:], Wu_i[rb * 128:(rb + 1) * 128, qc * 1024:(qc + 1) * 1024], writes=[wld])
            for kq in range(8):
                kb = qc * 8 + kq
                sl = wld[:, kq * 128:(kq + 1) * 128]
                p.op("tensor", lambda e, sl=sl: e.transpose(out=tr_ps[:], in_=sl, identity=ident[:]),
                     reads=[wld, ident], writes=[tr_ps])
                dsl = WuT[:, kb, rb * 128:(rb + 1) * 128]
                p.op("scalar", lambda e, dsl=dsl: e.activation(out=dsl, in_=tr_ps[:], func=AF.Copy),
                     reads=[tr_ps], writes=[(WuT, (kb, rb))])

    xt = [p.sbuf(f"xt{i}", [128, D], F32) for i in range(2)]
    sq = [p.sbuf(f"sq{i}", [128, D], F32) for i in range(2)]
    lnb = [p.sbuf(f"lnb{i}", [128, D], BF16) for i in range(2)]
    stats = [make_stat(p, f"st{i}") for i in range(2)]
    trb_ps = [p.psum(f"trb_ps{i}", [128, 128], BF16) for i in range(2)]
    pkvt = p.sbuf("pkvt", [128, 128], BF16)
    g2t = [p.sbuf(f"g2t{i}", [128, D], F32) for i in range(2)]
    g2keep = p.sbuf("g2keep", [128, CHI // 128, D], F32)
    hbT = p.sbuf("hbT", [128, 8, CHI], BF16)
    bhT = p.sbuf("bhT", [128, 32, CHI], BF16)
    mm_ps = [p.psum(f"mm_ps{i}", [128, 512], F32) for i in range(4)]
    xo_t = [p.sbuf("xo_t0", [128, D], F32)] * 2

    def ln_tr_aff(src_t, col0, dst, par):
        SQ, LNB, st = sq[par], lnb[par], stats[par]
        ln_stats_n(p, src_t, SQ, st, 128)
        p.op("scalar", lambda e, src_t=src_t, LNB=LNB, st=st: e.activation(
            out=LNB[:], in_=src_t[:], func=AF.Identity,
            bias=st["negmur"][:, 0:1], scale=st["rstd"][:, 0:1]),
             reads=[src_t, st["negmur"], st["rstd"]], writes=[LNB])
        for kb in range(8):
            tp = trb_ps[kb % 2]
            p.op("tensor", lambda e, kb=kb, tp=tp, LNB=LNB: e.transpose(
                out=tp[:], in_=LNB[:, kb * 128:(kb + 1) * 128], identity=identb[:]),
                 reads=[LNB, identb], writes=[tp])
            dsl = dst[:, kb, col0:col0 + 128]
            p.op("scalar", lambda e, dsl=dsl, tp=tp, kb=kb: e.activation(
                out=dsl, in_=tp[:], func=AF.Identity,
                scale=lncols[:, 0, kb:kb + 1], bias=lncols[:, 1, kb:kb + 1]),
                 reads=[tp, lncols], writes=[(dst, (kb, col0))])

    for b in range(B):
        for ci in range(NCH):
            c0 = ci * CHI
            for tt in range(CHI // 128):
                par = tt % 2
                G2 = g2t[par]
                X = xt[par]
                p.dma("sync", X[:], g1_i[b, c0 + tt * 128:c0 + (tt + 1) * 128, :], writes=[X])
                for kb in range(8):
                    p.dma("sync", pkvt[:], pkvT_i[b, kb * 128:(kb + 1) * 128,
                                                         c0 + tt * 128:c0 + (tt + 1) * 128],
                          writes=[pkvt])
                    tp = trb_ps[kb % 2]
                    p.op("tensor", lambda e, tp=tp: e.transpose(out=tp[:], in_=pkvt[:],
                                                                identity=identb[:]),
                         reads=[pkvt, identb], writes=[tp])
                    p.op("vector", lambda e, kb=kb, tp=tp, G2=G2, X=X: e.tensor_add(
                        out=G2[:, kb * 128:(kb + 1) * 128], in0=tp[:], in1=X[:, kb * 128:(kb + 1) * 128]),
                         reads=[tp, X], writes=[(G2, kb)])
                p.op("vector", lambda e, tt=tt, G2=G2: e.tensor_copy(out=g2keep[:, tt, :], in_=G2[:]),
                     reads=[G2], writes=[(g2keep, tt)])
                ln_tr_aff(G2, tt * 128, hbT, par)
            for och in range(32):
                ps = mm_ps[och % 2]
                def fb1(e, och=och, ps=ps):
                    last = None
                    for kb in range(8):
                        last = e.matmul(out=ps[:, 0:CHI], lhsT=WbT[:, kb, och * 128:(och + 1) * 128],
                                        rhs=hbT[:, kb, :], start=(kb == 0), stop=(kb == 7))
                    return last
                p.op("tensor", fb1, reads=[hbT, WbT], writes=[ps])
                p.op("scalar", lambda e, och=och, ps=ps: e.activation(
                    out=bhT[:, och, :], in_=ps[:, 0:CHI], func=AF.Gelu, bias=bbcol[:, och:och + 1]),
                     reads=[ps, bbcol], writes=[(bhT, och)])
            for tt in range(CHI // 128):
                XO = xo_t[tt % 2]
                X4 = xt[tt % 2]
                p.dma("sync", X4[:], x_in[b, c0 + tt * 128:c0 + (tt + 1) * 128, :], writes=[X4],
                      sem=f"x4ld{tt % 2}")
                for oc in range(2):
                    ps = mm_ps[2 + oc]
                    def fo(e, tt=tt, oc=oc, ps=ps):
                        e.matmul(out=ps[:], lhsT=ones_row[:], rhs=buS[:, oc * 512:(oc + 1) * 512],
                                 start=True, stop=False)
                        last = None
                        for kb in range(32):
                            last = e.matmul(out=ps[:], lhsT=bhT[:, kb, tt * 128:(tt + 1) * 128],
                                            rhs=WuT[:, kb, oc * 512:(oc + 1) * 512],
                                            start=False, stop=(kb == 31))
                        return last
                    p.op("tensor", fo, reads=[bhT, WuT, ones_row, buS], writes=[ps])
                    p.op("vector", lambda e, oc=oc, ps=ps, tt=tt, XO=XO: e.tensor_add(
                        out=XO[:, oc * 512:(oc + 1) * 512], in0=ps[:],
                        in1=g2keep[:, tt, oc * 512:(oc + 1) * 512]),
                         reads=[ps, (g2keep, tt)], writes=[(XO, oc)])
                p.op("vector", lambda e, XO=XO, X4=X4: e.tensor_add(out=XO[:], in0=XO[:], in1=X4[:]),
                     reads=[XO, X4], writes=[XO])
                p.dma("gpsimd", x_out[b, c0 + tt * 128:c0 + (tt + 1) * 128, :], XO[:],
                      reads=[XO], sem=f"xst{tt % 2}")
    return


def build_l1attn_back(tag="l1ab"):
    p = Prog(tag)
    io = dict(
        x=p.dram("x", [B, TLOC, D], F32, kind="ExternalInput").ap(),
        g1=p.dram("g1", [B, TLOC, D], F32, kind="ExternalInput").ap(),
        pkvT=p.dram("pkvT", [B, D, TLOC], BF16, kind="ExternalInput").ap(),
        Wb=p.dram("Wb", [4096, D], F32, kind="ExternalInput").ap(),
        Wu=p.dram("Wu", [D, 4096], F32, kind="ExternalInput").ap(),
        bb=p.dram("bb", [128, 32], F32, kind="ExternalInput").ap(),
        bu=p.dram("bu", [1, D], F32, kind="ExternalInput").ap(),
        lncols=p.dram("lncols", [128, 2, 8], F32, kind="ExternalInput").ap(),
        ident_in=p.dram("ident_in", [128, 128], F32, kind="ExternalInput").ap(),
        xo=p.dram("xo", [B, TLOC, D], F32, kind="ExternalOutput").ap(),
    )
    with p.stage("lb"):
        emit_l1ab(p, io)
    p.finalize()
    return p



def emit_halo(p, tag, body3, dst, nL, nR, selL_ap, selR_ap, ident, q="gpsimd"):
    contrib = p.dram(tag + "_hc", [12, D], F32).ap()
    hag = p.dram(tag + "_hag", [96, D], F32).ap()
    for b in range(B):
        f3, l3 = body3(b)
        p.dma(q, contrib[b * 6:b * 6 + 3, :], f3, sem=tag + "h0")
        p.dma(q, contrib[b * 6 + 3:b * 6 + 6, :], l3, sem=tag + "h0")
    p.coll("AllGather", ALU.bypass, [contrib], [hag],
           reads=[(contrib, None)], writes=[(hag, None)])
    hs = p.sbuf(tag + "_hs", [96, D], F32)
    hb = p.sbuf(tag + "_hb", [96, D], BF16)
    sL = p.sbuf(tag + "_sL", [96, 6], BF16)
    sR = p.sbuf(tag + "_sR", [96, 6], BF16)
    hps = p.psum(tag + "_hps", [8, 512], F32)
    hev = p.sbuf(tag + "_hev", [3, D], F32)
    p.dma("sync", hs[:], hag, reads=[(hag, None)], writes=[hs])
    p.op("vector", lambda e: e.tensor_copy(out=hb[:], in_=hs[:]), reads=[hs], writes=[hb])
    p.dma("sync", sL[:], selL_ap, writes=[sL])
    p.dma("sync", sR[:], selR_ap, writes=[sR])
    for b in range(B):
        for side, n_, sel in (("L", nL, sL), ("R", nR, sR)):
            if n_ == 0:
                continue
            for oc in range(2):
                p.op("tensor", lambda e, b=b, oc=oc, sel=sel: e.matmul(
                    out=hps[0:3, 0:512], lhsT=sel[:, b * 3:(b + 1) * 3],
                    rhs=hb[:, oc * 512:(oc + 1) * 512], start=True, stop=True),
                     reads=[sel, hb], writes=[hps])
                p.op("scalar", lambda e, oc=oc: e.activation(
                    out=hev[:, oc * 512:(oc + 1) * 512], in_=hps[0:3, 0:512], func=AF.Copy),
                     reads=[hps], writes=[(hev, oc)])
            d = dst(b, side)
            p.dma(q, d, hev[3 - n_:3, :] if side == "L" else hev[0:n_, :],
                  reads=[hev], sem=tag + "hw")


def emit_scan_fused(p, io, regs):
    NCHS = S // SCH
    bc_ag = {d_: io[f"bc_ag_{d_}"] for d_ in "fb"}
    dmy = io["dmy"]
    alog_i = io["alog2"]
    sel_i = io["selbc"]
    y_o2 = io["ylocal2"]

    SELB = p.sbuf("SELB", [64, 32, 128], BF16)
    p.dma("sync", SELB[:], sel_i, writes=[SELB])
    acols = p.sbuf("acols", [128, 2, 4, 16], F32)
    atmp = p.sbuf("atmp", [128, 16], F32)
    atmp2 = p.sbuf("atmp2", [128, 16], F32)
    for di in range(2):
        for db in range(4):
            p.dma("sync", atmp[:], alog_i[di, db], writes=[atmp])
            p.op("scalar", lambda e: e.activation(out=atmp2[:], in_=atmp[:], func=AF.Exp),
                 reads=[atmp], writes=[atmp2])
            p.op("vector", lambda e, di=di, db=db: e.tensor_scalar(
                out=acols[:, di, db, :], in0=atmp2[:], scalar1=-1.0, scalar2=None, op0=ALU.mult),
                 reads=[atmp2], writes=[(acols, (di, db))])

    Bb = p.sbuf("Bb", [128, 16, SCH], BF16)
    Cb = p.sbuf("Cb", [128, 16, SCH], BF16)
    bcc = p.sbuf("bcc_s", [64, SCH], BF16)
    bccr = p.sbuf("bccr", [64, SCH], BF16)
    bc_ps = [p.psum(f"bc_ps{i}", [128, SCH], F32) for i in range(2)]
    dA = [p.sbuf(f"dA{i}", [128, 16, SCH], F32) for i in range(2)]
    dBu = [p.sbuf(f"dBu{i}", [128, 16, SCH], BF16) for i in range(2)]
    hb = p.sbuf("hb", [128, 16, SCH], BF16)
    tb = p.sbuf("tb", [128, 4, SCH], BF16)
    tc = p.sbuf("tc", [128, 2, SCH], BF16)
    yv = p.sbuf("yv", [128, SCH], BF16)
    dTt = [p.sbuf(f"dTt{i}", [128, SCH], BF16) for i in range(2)]
    duTt = [p.sbuf(f"duTt{i}", [128, SCH], BF16) for i in range(2)]
    carry = p.sbuf("carry", [128, 4, 16], F32)

    def rev(ap_, n):
        return bass.AP(ap_.tensor, ap_.offset + n - 1, [[ap_.ap[0][0], ap_.ap[0][1]], [-1, n]])

    for di, d_ in enumerate("fb"):
        fwd = (d_ == "f")
        p.op("vector", lambda e: e.memset(carry[:], 0.0), writes=[carry])
        for ci in range(NCHS):
            c0 = ci * SCH if fwd else (NCHS - 1 - ci) * SCH
            r = c0 // TLOC
            col0 = c0 % TLOC
            p.dma("sync", bcc[:], bc_ag[d_][r * 64:(r + 1) * 64, col0:col0 + SCH],
                  reads=[(bc_ag[d_], None)], writes=[bcc])
            if fwd:
                bcu = bcc
            else:
                p.op("vector", lambda e: e.tensor_copy(out=bccr[:], in_=rev(bcc[:], SCH)),
                     reads=[bcc], writes=[bccr])
                bcu = bccr
            for s in range(16):
                ps = bc_ps[s % 2]
                p.op("tensor", lambda e, s=s, ps=ps, bcu=bcu: e.matmul(
                    out=ps[:], lhsT=SELB[:, s, :], rhs=bcu[:], start=True, stop=True),
                     reads=[SELB, bcu], writes=[ps])
                p.op("scalar", lambda e, s=s, ps=ps: e.activation(out=Bb[:, s, :], in_=ps[:], func=AF.Copy),
                     reads=[ps], writes=[(Bb, s)])
                ps2 = bc_ps[(s + 1) % 2]
                p.op("tensor", lambda e, s=s, ps2=ps2, bcu=bcu: e.matmul(
                    out=ps2[:], lhsT=SELB[:, 16 + s, :], rhs=bcu[:], start=True, stop=True),
                     reads=[SELB, bcu], writes=[ps2])
                p.op("scalar", lambda e, s=s, ps2=ps2: e.activation(out=Cb[:, s, :], in_=ps2[:], func=AF.Copy),
                     reads=[ps2], writes=[(Cb, s)])
            for db in range(4):
                par = db % 2
                dt_, du_ = dTt[par], duTt[par]
                p.dma("sync", dt_[:], dmy[0, di, db * 128:(db + 1) * 128, c0:c0 + SCH],
                      reads=[(dmy, None)], writes=[dt_], sem=f"dtl{par}")
                p.dma("sync", du_[:], dmy[1, di, db * 128:(db + 1) * 128, c0:c0 + SCH],
                      reads=[(dmy, None)], writes=[du_], sem=f"dul{par}")
                DA, DB = dA[par], dBu[par]
                dsrc = dt_[:] if fwd else rev(dt_[:], SCH)
                usrc = du_[:] if fwd else rev(du_[:], SCH)
                for s in range(16):
                    p.op("scalar", lambda e, s=s, di=di, db=db, DA=DA, dsrc=dsrc: e.activation(
                        out=DA[:, s, :], in_=dsrc, func=AF.Exp, scale=acols[:, di, db, s:s + 1]),
                         reads=[dt_, (acols, (di, db))], writes=[(DA, s)])
                    p.op("vector", lambda e, s=s, DB=DB, usrc=usrc: e.tensor_mul(
                        out=DB[:, s, :], in0=usrc, in1=Bb[:, s, :]),
                         reads=[du_, (Bb, s)], writes=[(DB, s)])
                for s in range(16):
                    p.op("vector", lambda e, s=s, db=db, DA=DA, DB=DB: e.tensor_tensor_scan(
                        out=hb[:, s, :], data0=DA[:, s, :], data1=DB[:, s, :],
                        initial=carry[:, db, s:s + 1], op0=ALU.mult, op1=ALU.add),
                         reads=[(DA, s), (DB, s), (carry, db)], writes=[(hb, s)])
                p.op("vector", lambda e, db=db: e.tensor_copy(
                    out=carry[:, db, :], in_=bass.AP(hb[:].tensor, hb[:].offset + SCH - 1,
                                                     [[hb[:].ap[0][0], 128], [SCH, 16]])),
                     reads=[hb], writes=[(carry, db)])
                p.op("vector", lambda e, DB=DB: e.tensor_mul(out=DB[:], in0=hb[:], in1=Cb[:]),
                     reads=[hb, Cb], writes=[DB])
                p.op("vector", lambda e, DB=DB: e.tensor_add(out=hb[:, 0:8, :], in0=DB[:, 0:8, :], in1=DB[:, 8:16, :]),
                     reads=[DB], writes=[hb])
                p.op("vector", lambda e: e.tensor_add(out=tb[:], in0=hb[:, 0:4, :], in1=hb[:, 4:8, :]),
                     reads=[hb], writes=[tb])
                p.op("vector", lambda e: e.tensor_add(out=tc[:], in0=tb[:, 0:2, :], in1=tb[:, 2:4, :]),
                     reads=[tb], writes=[tc])
                yout = yv[:] if fwd else rev(yv[:], SCH)
                p.op("vector", lambda e, yout=yout: e.tensor_add(out=yout, in0=tc[:, 0, :], in1=tc[:, 1, :]),
                     reads=[tc], writes=[yv])
                p.dma("gpsimd", y_o2[di][db * 128:(db + 1) * 128, c0:c0 + SCH], yv[:],
                      reads=[yv], sem=f"yst{par}")
        if io.get("_ycoll") is not None:
            io["_ycoll"](di)
    return


def emit_pooled_fused(p, io):
    qT_i, kT_i, vT_i = io["qTl"], io["kTl"], io["vTl"]
    rs_i = io["rss"]
    mask_i = io["maskprev"]
    pkv_o = io["pkvTl"]

    ones = p.sbuf("ones", [128, TLOC], F32)
    p.op("vector", lambda e: e.memset(ones[:], 1.0), writes=[ones])
    rsb = p.sbuf("rsb", [128, TLOC], F32)
    rbv = bass.AP(rs_i.tensor, rs_i.offset, [[0, 128], [1, TLOC]])
    p.dma("sync", rsb[:], rbv, writes=[rsb])
    maskc = p.sbuf("maskc", [16, 8], F32)
    p.dma("sync", maskc[:], mask_i, writes=[maskc])
    ident = p.sbuf("po_id", [128, 128], F32)
    p.dma("sync", ident[:], io["ident_in"], writes=[ident])

    qt = [p.sbuf(f"qt{i}", [128, TLOC], BF16) for i in range(2)]
    kt = [p.sbuf(f"kt{i}", [128, TLOC], BF16) for i in range(2)]
    vt = p.sbuf("vt", [128, TLOC], BF16)
    cq = p.sbuf("cq", [128, TLOC], F32)
    krs = p.sbuf("krs", [128, TLOC], F32)
    KS = p.sbuf("KS", [128, TLOC], F32)
    t2 = p.sbuf("t2", [128, TLOC], F32)
    LS = p.sbuf("LS", [128, TLOC], F32)
    ends = p.sbuf("ends", [128, 16, 3], F32)

    def lastcol(t):
        return bass.AP(t[:].tensor, t[:].offset + TLOC - 1, [[t[:].ap[0][0], 128], [1, 1]])

    def tile_scans(i, b, db, store_ends):
        Q, Kt = qt[i % 2], kt[i % 2]
        p.dma("sync", Q[:], qT_i[b, db * 128:(db + 1) * 128, :], writes=[Q], sem=f"qtl{i % 2}")
        p.dma("sync", Kt[:], kT_i[b, db * 128:(db + 1) * 128, :], writes=[Kt], sem=f"ktl{i % 2}")
        p.op("vector", lambda e, Q=Q: e.tensor_tensor_scan(
            out=cq[:], data0=ones[:], data1=Q[:], initial=0.0, op0=ALU.mult, op1=ALU.add),
             reads=[ones, Q], writes=[cq])
        p.op("vector", lambda e, Kt=Kt: e.tensor_mul(out=krs[:], in0=Kt[:], in1=rsb[:]),
             reads=[Kt, rsb], writes=[krs])
        p.op("vector", lambda e: e.tensor_tensor_scan(
            out=KS[:], data0=ones[:], data1=krs[:], initial=0.0, op0=ALU.mult, op1=ALU.add),
             reads=[ones, krs], writes=[KS])
        p.op("vector", lambda e: e.tensor_mul(out=t2[:], in0=cq[:], in1=krs[:]),
             reads=[cq, krs], writes=[t2])
        p.op("vector", lambda e: e.tensor_tensor_scan(
            out=LS[:], data0=ones[:], data1=t2[:], initial=0.0, op0=ALU.mult, op1=ALU.add),
             reads=[ones, t2], writes=[LS])
        if store_ends:
            for j, t in enumerate((cq, KS, LS)):
                p.op("vector", lambda e, j=j, t=t, i=i: e.tensor_copy(
                    out=ends[:, i, j:j + 1], in_=lastcol(t)),
                     reads=[t], writes=[(ends, (i, j))])

    for b in range(B):
        for db in range(8):
            tile_scans(b * 8 + db, b, db, True)
    sqc = p.dram("po_sqc", [16, 128], F32).ap()
    sag = p.dram("po_sag", [128, 128], F32).ap()
    for i in range(16):
        p.dma("gpsimd", sqc[i:i + 1, :],
              bass.AP(ends[:].tensor, ends[:].offset + i * 3 + 0, [[ends[:].ap[0][0], 128], [1, 1]]),
              reads=[(ends, (i, 0))], sem="sqst")
    p.coll("AllGather", ALU.bypass, [sqc], [sag], reads=[(sqc, None)], writes=[(sag, None)])

    gsl = [p.sbuf(f"gsl{r}", [16, 128], F32) for r in range(2)]
    cacc = [p.sbuf(f"cacc{i}", [16, 128], F32) for i in range(2)]
    cq_ps = p.psum("cq_ps", [128, 16], F32)
    carryq = p.sbuf("carryq", [128, 16], F32)
    carrypk = p.sbuf("carrypk", [128, 16], F32)

    def compose(agap, dst_col, tag):
        p.op("vector", lambda e: e.memset(cacc[0][:], 0.0), writes=[cacc[0]])
        prev = cacc[0]
        for r in range(8):
            g = gsl[r % 2]
            p.dma("sync", g[:], agap[r * 16:(r + 1) * 16, :], reads=[(agap, None)], writes=[g], sem=tag + "g")
            nxt = cacc[(r + 1) % 2] if r < 7 else cacc[(r + 1) % 2]
            p.op("vector", lambda e, g=g, r=r, prev=prev, nxt=nxt: e.scalar_tensor_tensor(
                out=nxt[:], in0=g[:], scalar=maskc[:, r:r + 1], in1=prev[:],
                op0=ALU.mult, op1=ALU.add),
                 reads=[g, maskc, prev], writes=[nxt])
            prev = nxt
        p.op("tensor", lambda e, prev=prev: e.transpose(out=cq_ps[:, 0:16], in_=prev[:],
                                                        identity=ident[0:16, 0:16]),
             reads=[prev, ident], writes=[cq_ps])
        p.op("vector", lambda e, dst_col=dst_col: e.tensor_copy(out=dst_col[:], in_=cq_ps[:, 0:16]),
             reads=[cq_ps], writes=[dst_col])

    compose(sag, carryq, "cg")
    spk = p.dram("po_spk", [16, 128], F32).ap()
    pag = p.dram("po_pag", [128, 128], F32).ap()
    send = p.sbuf("send", [128, 1], F32)
    for i in range(16):
        p.op("vector", lambda e, i=i: e.scalar_tensor_tensor(
            out=send[:], in0=ends[:, i, 1:2], scalar=carryq[:, i:i + 1], in1=ends[:, i, 2:3],
            op0=ALU.mult, op1=ALU.add),
             reads=[(ends, (i, 1)), (ends, (i, 2)), carryq], writes=[send])
        p.dma("gpsimd", spk[i:i + 1, :], send[:], reads=[send], sem="spkst")
    p.coll("AllGather", ALU.bypass, [spk], [pag], reads=[(spk, None)], writes=[(pag, None)])
    compose(pag, carrypk, "cp")
    c2 = p.sbuf("c2", [128, TLOC], F32)
    c3 = p.sbuf("c3", [128, TLOC], F32)
    pk = p.sbuf("pk", [128, TLOC], F32)
    pkv = p.sbuf("pkv", [128, TLOC], BF16)
    for b in range(B):
        for db in range(8):
            i = b * 8 + db
            tile_scans(i, b, db, False)
            p.dma("sync", vt[:], vT_i[b, db * 128:(db + 1) * 128, :], writes=[vt], sem="vtl")
            p.op("vector", lambda e, i=i: e.scalar_tensor_tensor(
                out=c2[:], in0=KS[:], scalar=carryq[:, i:i + 1], in1=LS[:],
                op0=ALU.mult, op1=ALU.add),
                 reads=[KS, LS, carryq], writes=[c2])
            p.op("vector", lambda e, i=i: e.tensor_scalar(
                out=c3[:], in0=c2[:], scalar1=carrypk[:, i:i + 1], scalar2=None, op0=ALU.add),
                 reads=[c2, carrypk], writes=[c3])
            p.op("vector", lambda e: e.tensor_mul(out=pk[:], in0=c3[:], in1=rsb[:]),
                 reads=[c3, rsb], writes=[pk])
            p.op("vector", lambda e: e.tensor_mul(out=pkv[:], in0=pk[:], in1=vt[:]),
                 reads=[pk, vt], writes=[pkv])
            p.dma("gpsimd", pkv_o[b, db * 128:(db + 1) * 128, :], pkv[:], reads=[pkv], sem="pkvst")
    return


def build_fused():
    p = Prog("fused")
    nc = p.nc
    ident_in = p.dram("ident_in", [128, 128], F32, kind="ExternalInput").ap()
    x0 = p.dram("x0", [B, TLOC, D], F32, kind="ExternalInput").ap()
    x6 = p.dram("x6", [B, TLOC, D], F32, kind="ExternalOutput").ap()
    moe_w = dict(
        We=p.dram("We", [8, D, D], F32, kind="ExternalInput").ap(),
        be=p.dram("be", [8, D], F32, kind="ExternalInput").ap(),
        Wr=p.dram("Wr", [8, D], F32, kind="ExternalInput").ap(),
        br=p.dram("br", [8], F32, kind="ExternalInput").ap(),
    )
    mf_io = _mfront_io(p, kind_out="Internal", pre="mf_", make_x=False)
    mb_w = _mback_w_io(p)
    lf_w = dict(
        Wr=p.dram("lf_Wr", [D, D], F32, kind="ExternalInput").ap(),
        Wk=p.dram("lf_Wk", [D, D], F32, kind="ExternalInput").ap(),
        Wv=p.dram("lf_Wv", [D, D], F32, kind="ExternalInput").ap(),
        mur=p.dram("lf_mur", [128, 8], F32, kind="ExternalInput").ap(),
        muk=p.dram("lf_muk", [128, 8], F32, kind="ExternalInput").ap(),
    )
    la_w = dict(
        Wq=p.dram("la_Wq", [D, D], F32, kind="ExternalInput").ap(),
        Wk=p.dram("la_Wk", [D, D], F32, kind="ExternalInput").ap(),
        Wv=p.dram("la_Wv", [D, D], F32, kind="ExternalInput").ap(),
        bqkv=p.dram("la_bqkv", [128, 3, 8], F32, kind="ExternalInput").ap(),
        cw=p.dram("la_cw", [4, 1024, 256], F32, kind="ExternalInput").ap(),
        cb=p.dram("la_cb", [1, 1024], F32, kind="ExternalInput").ap(),
        hmask=p.dram("la_hmask", [128, 1], F32, kind="ExternalInput").ap(),
        lncols=p.dram("la_lncols", [128, 6, 8], F32, kind="ExternalInput").ap(),
    )
    lb_w = dict(
        Wb=p.dram("lb_Wb", [4096, D], F32, kind="ExternalInput").ap(),
        Wu=p.dram("lb_Wu", [D, 4096], F32, kind="ExternalInput").ap(),
        bb=p.dram("lb_bb", [128, 32], F32, kind="ExternalInput").ap(),
        bu=p.dram("lb_bu", [1, D], F32, kind="ExternalInput").ap(),
        lncols=p.dram("lb_lncols", [128, 2, 8], F32, kind="ExternalInput").ap(),
    )
    sels = {h: (p.dram(f"selL_{h}", [96, 6], BF16, kind="ExternalInput").ap(),
                p.dram(f"selR_{h}", [96, 6], BF16, kind="ExternalInput").ap())
            for h in ("h1", "h2", "h3", "h4")}
    selbc = p.dram("selbc", [64, 32, 128], BF16, kind="ExternalInput").ap()
    alog2 = p.dram("alog2", [2, 4, 128, 16], F32, kind="ExternalInput").ap()
    maskprev = p.dram("maskprev", [16, 8], F32, kind="ExternalInput").ap()
    rss = p.dram("rss", [1, TLOC], F32, kind="ExternalInput").ap()
    offtab_d = p.dram("offtab", [1, 1024], mybir.dt.int32, kind="ExternalInput").ap()
    x1h = p.dram("x1h", [B, TLOC + 6, D], F32).ap()
    x2 = p.dram("x2s", [B, TLOC, D], F32).ap()
    x3h = p.dram("x3h", [B, TLOC + 1, D], F32).ap()
    x4h = p.dram("x4h", [B, TLOC + 3, D], F32).ap()
    x5h = p.dram("x5h", [B, TLOC + 1, D], F32).ap()
    ag = {}
    for d_ in "fb":
        for nm in ("dT", "duT"):
            ag[f"{nm}_ag_{d_}"] = p.dram(f"{nm}_ag_{d_}", [8 * 4096, TLOC], BF16).ap()
        ag[f"bc_ag_{d_}"] = p.dram(f"bc_ag_{d_}", [8 * 64, TLOC], BF16).ap()
    dmy = p.dram("dmy", [2, 2, 512, S], BF16).ap()
    ylocal2 = [p.dram(f"ylocal{di}", [512, S], BF16).ap() for di in range(2)]
    y_ag2 = [p.dram(f"y_ag{di}", [8 * 512, S], BF16).ap() for di in range(2)]
    ylocT = p.dram("ylocT", [B, 2, 2048, TLOC], BF16).ap()
    g1s = p.dram("g1s", [B, TLOC, D], F32).ap()
    qkvl = {n: p.dram(n, [B, D, TLOC], BF16).ap() for n in ("qTl", "kTl", "vTl")}
    pkvl = p.dram("pkvTl", [B, D, TLOC], BF16).ap()

    regs = {}
    regs["rs"] = p.stack.enter_context(nc.scalar.register("rs"))
    regs["rg"] = p.stack.enter_context(nc.gpsimd.register("rg"))
    regs["ry"] = p.stack.enter_context(nc.sync.register("ry"))
    offtab_sb = p.sbuf("offtab_sb", [1, 1024], mybir.dt.int32)
    p.dma("sync", offtab_sb[:], offtab_d, writes=[offtab_sb])
    regs["offtab"] = offtab_sb

    with p.stage("s1"):
        io1 = dict(x=x0, ident_in=ident_in, xo=x1h[:, 3:3 + TLOC, :], **moe_w)
        emit_moe(p, io1)
    with p.stage("hx1"):
        emit_halo(p, "h1",
                  lambda b: (x1h[b, 3:6, :], x1h[b, TLOC:TLOC + 3, :]),
                  lambda b, side: (x1h[b, 0:3, :] if side == "L" else x1h[b, TLOC + 3:TLOC + 6, :]),
                  3, 3, sels["h1"][0], sels["h1"][1], None)
    def colls_after_dir(d_):
        di2 = 0 if d_ == "f" else 1
        for nm in ("dT", "duT"):
            srcv = mf_io[f"{nm}_{d_}"].rearrange("b d t -> (b d) t")
            p.coll("AllGather", ALU.bypass, [srcv], [ag[f"{nm}_ag_{d_}"]],
                   reads=[(mf_io[f"{nm}_{d_}"], None)], writes=[(ag[f"{nm}_ag_{d_}"], None)])
        srcv = mf_io[f"bcT_{d_}"].rearrange("b d t -> (b d) t")
        p.coll("AllGather", ALU.bypass, [srcv], [ag[f"bc_ag_{d_}"]],
               reads=[(mf_io[f"bcT_{d_}"], None)], writes=[(ag[f"bc_ag_{d_}"], None)])
        for tn, nm in ((0, "dT"), (1, "duT")):
            for db in range(4):
                idx = (di2 * 4 + db) * 2 + tn
                agt = ag[f"{nm}_ag_{d_}"]
                def fg(e, tn=tn, di2=di2, db=db, idx=idx, agt=agt):
                    e.reg_load(regs["rs"], offtab_sb[0:1, idx:idx + 1])
                    return e.dma_start(out=dmy[tn, di2, db * 128:(db + 1) * 128, :],
                                       in_=bass.AP(agt.tensor, regs["rs"],
                                                   [[TLOC, 128], [4096 * TLOC, 8], [1, TLOC]]))
                p.op("scalar", fg, reads=[(agt, None), (offtab_sb, None)],
                     writes=[(dmy, (tn, di2, db))], dma=True, dma_sem=f"gx{tn}{db % 2}")

    with p.stage("s2"):
        io2 = dict(mf_io)
        io2["x"] = x1h
        io2["ident_in"] = ident_in
        io2["_colls_after_dir"] = colls_after_dir
        emit_mfront(p, io2)
    with p.stage("s3"):
        io3 = dict(ag)
        io3["dmy"] = dmy
        io3["alog2"] = alog2
        io3["selbc"] = selbc
        io3["ylocal2"] = ylocal2

        def _ycoll(di):
            p.coll("AllGather", ALU.bypass, [ylocal2[di]], [y_ag2[di]],
                   reads=[(ylocal2[di], None)], writes=[(y_ag2[di], None)])
        io3["_ycoll"] = _ycoll
        emit_scan_fused(p, io3, regs)
    with p.stage("gy"):
        for b in range(B):
            for di in range(2):
                for kbhi in range(4):
                    idx = 512 + (b * 2 + di) * 4 + kbhi
                    def fg(e, b=b, di=di, kbhi=kbhi, idx=idx):
                        e.reg_load(regs["rg"], offtab_sb[0:1, idx:idx + 1])
                        return e.dma_start(
                            out=ylocT[b, di, kbhi * 512:(kbhi + 1) * 512, :],
                            in_=bass.AP(y_ag2[di].tensor, regs["rg"],
                                        [[S, 512], [1, TLOC]]))
                    p.op("gpsimd", fg, reads=[(y_ag2[di], None), (offtab_sb, None)],
                         writes=[(ylocT, (b, di, kbhi))], dma=True, dma_sem=f"gy{kbhi}")
    with p.stage("s4"):
        io4 = dict(mb_w)
        io4["x"] = x1h[:, 3:3 + TLOC, :]
        io4["ident_in"] = ident_in
        io4["xo"] = x2
        for d_ in "fb":
            io4[f"uT_{d_}"] = mf_io[f"uT_{d_}"]
            io4[f"zT_{d_}"] = mf_io[f"zT_{d_}"]
        for d_, di in (("f", 0), ("b", 1)):
            io4[f"yT_{d_}"] = ylocT[:, di]
        emit_mback(p, io4)
    with p.stage("s5"):
        io5 = dict(x=x2, ident_in=ident_in, xo=x3h[:, 1:1 + TLOC, :], **moe_w)
        emit_moe(p, io5)
    with p.stage("hx2"):
        emit_halo(p, "h2",
                  lambda b: (x3h[b, 1:4, :], x3h[b, TLOC - 2:TLOC + 1, :]),
                  lambda b, side: (x3h[b, 0:1, :] if side == "L" else None),
                  1, 0, sels["h2"][0], sels["h2"][1], None)
    with p.stage("s6"):
        io6 = dict(x=x3h, ident_in=ident_in, xo=x4h[:, 3:3 + TLOC, :], **lf_w)
        emit_l1ffn(p, io6)
    with p.stage("hx3"):
        emit_halo(p, "h3",
                  lambda b: (x4h[b, 3:6, :], x4h[b, TLOC:TLOC + 3, :]),
                  lambda b, side: (x4h[b, 0:3, :] if side == "L" else None),
                  3, 0, sels["h3"][0], sels["h3"][1], None)
    with p.stage("s7"):
        io7 = dict(x=x4h, ident_in=ident_in, g1=g1s, qT=qkvl["qTl"], kT=qkvl["kTl"],
                   vT=qkvl["vTl"], **la_w)
        emit_l1af(p, io7)
    with p.stage("s8"):
        io8 = dict(qTl=qkvl["qTl"], kTl=qkvl["kTl"], vTl=qkvl["vTl"], rss=rss,
                   maskprev=maskprev, ident_in=ident_in, pkvTl=pkvl)
        emit_pooled_fused(p, io8)
    with p.stage("s9"):
        io9 = dict(x=x4h[:, 3:3 + TLOC, :], g1=g1s, pkvT=pkvl, ident_in=ident_in,
                   xo=x5h[:, 1:1 + TLOC, :], **lb_w)
        emit_l1ab(p, io9)
    with p.stage("hx4"):
        emit_halo(p, "h4",
                  lambda b: (x5h[b, 1:4, :], x5h[b, TLOC - 2:TLOC + 1, :]),
                  lambda b, side: (x5h[b, 0:1, :] if side == "L" else None),
                  1, 0, sels["h4"][0], sels["h4"][1], None)
    with p.stage("s10"):
        io10 = dict(x=x5h, ident_in=ident_in, xo=x6, **lf_w)
        emit_l1ffn(p, io10)
    p.finalize()
    return p


def kernel_fused(inputs):
    ins = {k: np.asarray(v) for k, v in inputs.items()}
    p = _PROG_CACHE.get("fused")
    if p is None:
        p = _PROG_CACHE["fused"] = build_fused()
    x0f = np.ascontiguousarray(np.asarray(ins["L0_attn_x"], np.float32))
    in_maps = []
    for c in range(N_CORES):
        lo, hi = c * TLOC, (c + 1) * TLOC
        d = dict(
            ident_in=_ident_np(),
            x0=np.ascontiguousarray(x0f[:, lo:hi]),
            We=ins["L0_ffn_W_experts"], be=ins["L0_ffn_b_experts"],
            Wr=ins["L0_ffn_W_router"], br=ins["L0_ffn_b_router"],
            fw=ins['L0_attn_fusion_w'], fb=np.asarray(ins['L0_attn_fusion_b'])[None, :],
            Dv_f=_col16(ins['L0_attn_fwd_D']), Dv_b=_col16(ins['L0_attn_bwd_D']),
            outw_f=ins['L0_attn_fwd_out_proj_w'], outw_b=ins['L0_attn_bwd_out_proj_w'],
            lf_Wr=ins['L1_ffn_W_r'], lf_Wk=ins['L1_ffn_W_k'], lf_Wv=ins['L1_ffn_W_v'],
            lf_mur=_col8(ins['L1_ffn_mu_r']), lf_muk=_col8(ins['L1_ffn_mu_k']),
            la_Wq=ins['L1_attn_Wq'], la_Wk=ins['L1_attn_Wk'], la_Wv=ins['L1_attn_Wv'],
            la_cw=np.ascontiguousarray(np.asarray(ins['L1_attn_conv_w']).transpose(2, 0, 1)),
            la_cb=np.asarray(ins['L1_attn_conv_b'])[None, :],
            la_hmask=np.full((128, 1), 0.0 if lo == 0 else 1.0, np.float32),
            lb_Wb=ins['L1_attn_Wb'], lb_Wu=ins['L1_attn_Wu'],
            lb_bb=_col32(ins['L1_attn_bb']), lb_bu=np.asarray(ins['L1_attn_bu'])[None, :],
        )
        for d_, pref in (("f", "fwd_"), ("b", "bwd_")):
            d[f"mf_Win_{d_}"] = ins[f"L0_attn_{pref}in_proj_w"]
            d[f"mf_convw_{d_}"] = np.ascontiguousarray(
                np.asarray(ins[f"L0_attn_{pref}conv_w"])[:, 0, :].reshape(16, 128, 4).transpose(1, 0, 2))
            d[f"mf_convb_{d_}"] = _col16(ins[f"L0_attn_{pref}conv_b"])
            d[f"mf_xproj_{d_}"] = ins[f"L0_attn_{pref}x_proj_w"]
            d[f"mf_dtw_{d_}"] = ins[f"L0_attn_{pref}dt_proj_w"]
            d[f"mf_dtb_{d_}"] = _col16(ins[f"L0_attn_{pref}dt_proj_b"])
        lncols = np.zeros((128, 6, 8), np.float32)
        lncols[:, 0] = _col8(ins['L1_attn_ln_conv_g']); lncols[:, 1] = _col8(ins['L1_attn_ln_conv_b'])
        lncols[:, 2] = _col8(ins['L1_attn_ln_attn_g']); lncols[:, 3] = _col8(ins['L1_attn_ln_attn_b'])
        d["la_lncols"] = lncols
        bqkv = np.zeros((128, 3, 8), np.float32)
        bqkv[:, 0] = _col8(ins['L1_attn_bq']); bqkv[:, 1] = _col8(ins['L1_attn_bk'])
        bqkv[:, 2] = _col8(ins['L1_attn_bv'])
        d["la_bqkv"] = bqkv
        lncols2 = np.zeros((128, 2, 8), np.float32)
        lncols2[:, 0] = _col8(ins['L1_attn_ln_boom_g']); lncols2[:, 1] = _col8(ins['L1_attn_ln_boom_b'])
        d["lb_lncols"] = lncols2
        for h in ("h1", "h2", "h3", "h4"):
            sl = np.zeros((96, 6), np.float32)
            sr = np.zeros((96, 6), np.float32)
            if c > 0:
                for b in range(B):
                    for j in range(3):
                        sl[(c - 1) * 12 + b * 6 + 3 + j, b * 3 + j] = 1.0
            if c < 7:
                for b in range(B):
                    for j in range(3):
                        sr[(c + 1) * 12 + b * 6 + j, b * 3 + j] = 1.0
            d[f"selL_{h}"] = sl.astype(BF16NP)
            d[f"selR_{h}"] = sr.astype(BF16NP)
        sb_, sq_ = c // 4, c % 4
        selbc = np.zeros((64, 32, 128), np.float32)
        for j in range(32):
            selbc[sb_ * 32 + j, j, :] = 1.0
        d["selbc"] = selbc.astype(BF16NP)
        al = np.zeros((2, 4, 128, 16), np.float32)
        for di, pref in enumerate(("fwd_", "bwd_")):
            alog = np.asarray(ins[f"L0_attn_{pref}A_log"], np.float32)
            al[di] = alog[sq_ * 512:(sq_ + 1) * 512].reshape(4, 128, 16)
        d["alog2"] = al
        mp = np.zeros((16, 8), np.float32)
        mp[:, :c] = 1.0
        d["maskprev"] = mp
        d["rss"] = (1.0 / np.arange(lo + 1, hi + 1, dtype=np.float32))[None, :]
        offtab = np.zeros((1, 1024), np.int32)
        base_row = sb_ * 2048 + sq_ * 512
        for di in range(2):
            for db in range(4):
                off = (base_row + db * 128) * TLOC
                offtab[0, (di * 4 + db) * 2] = off
                offtab[0, (di * 4 + db) * 2 + 1] = off
            offtab[0, 32 + di] = sb_ * 32 * TLOC
        for b in range(B):
            for di in range(2):
                for kbhi in range(4):
                    off = (b * 4 + kbhi) * 512 * S + c * TLOC
                    offtab[0, 512 + (b * 2 + di) * 4 + kbhi] = off
        d["offtab"] = offtab
        in_maps.append(d)
    res = run_bass_kernel_spmd(p.nc, in_maps, core_ids=list(range(N_CORES)))
    out = np.empty((B, S, D), np.float32)
    for c in range(N_CORES):
        out[:, c * TLOC:(c + 1) * TLOC, :] = res.results[c]["x6"]
    return out



_PROG_CACHE = {}

def _prog(name, builder):
    if name not in _PROG_CACHE:
        _PROG_CACHE[name] = builder()
    return _PROG_CACHE[name]

def _ident_np():
    return np.eye(128, dtype=np.float32)

def _col8(v):
    return np.ascontiguousarray(np.asarray(v).reshape(8, 128).T)

def _col16(v):
    return np.ascontiguousarray(np.asarray(v).reshape(16, 128).T)

def _col32(v):
    return np.ascontiguousarray(np.asarray(v).reshape(32, 128).T)

def _sel_np():
    sel = np.zeros((32, 32, 128), np.float32)
    for j in range(32):
        sel[j, j, :] = 1.0
    return sel.astype(BF16NP)

def _run(p, in_maps):
    return run_bass_kernel_spmd(p.nc, in_maps, core_ids=list(range(N_CORES)))


def run_moe(x_full, We, be, Wr, br):
    p = _prog("moe", build_moe)
    in_maps = []
    for c in range(N_CORES):
        sl = x_full[:, c * TLOC:(c + 1) * TLOC, :]
        in_maps.append(dict(x=np.ascontiguousarray(sl), We=We, be=be, Wr=Wr, br=br,
                            ident_in=_ident_np()))
    res = _run(p, in_maps)
    out = np.empty_like(x_full)
    for c in range(N_CORES):
        out[:, c * TLOC:(c + 1) * TLOC, :] = res.results[c]["xo"]
    return out


def run_mfront(x1, ins):
    p = _prog("mfront", build_mamba_front)
    in_maps = []
    for c in range(N_CORES):
        lo, hi = c * TLOC, (c + 1) * TLOC
        xh = np.zeros((B, TLOC + 6, D), np.float32)
        xh[:, 3:3 + TLOC] = x1[:, lo:hi]
        if lo > 0:
            xh[:, 0:3] = x1[:, lo - 3:lo]
        if hi < S:
            xh[:, 3 + TLOC:] = x1[:, hi:hi + 3]
        d = dict(x=xh, ident_in=_ident_np())
        for d_, pref in (("f", "fwd_"), ("b", "bwd_")):
            d[f"Win_{d_}"] = ins[f"L0_attn_{pref}in_proj_w"]
            d[f"convw_{d_}"] = np.ascontiguousarray(
                ins[f"L0_attn_{pref}conv_w"][:, 0, :].reshape(16, 128, 4).transpose(1, 0, 2))
            d[f"convb_{d_}"] = _col16(ins[f"L0_attn_{pref}conv_b"])
            d[f"xproj_{d_}"] = ins[f"L0_attn_{pref}x_proj_w"]
            d[f"dtw_{d_}"] = ins[f"L0_attn_{pref}dt_proj_w"]
            d[f"dtb_{d_}"] = _col16(ins[f"L0_attn_{pref}dt_proj_b"])
        in_maps.append(d)
    res = _run(p, in_maps)
    st = {}
    for name in ("uT", "zT", "dT", "duT", "bcT"):
        for d_ in "fb":
            st[f"{name}_{d_}"] = np.concatenate(
                [np.asarray(res.results[c][f"{name}_{d_}"]) for c in range(N_CORES)], axis=2)
    return st


def run_scan(st, ins):
    p = _prog("scan", lambda: build_scan(T=S))
    in_maps, meta = [], []
    for d_ in "fb":
        alog = np.asarray(ins[f'L0_attn_{"fwd" if d_ == "f" else "bwd"}_A_log'], np.float32)
        for b in range(B):
            for half in range(2):
                dsl = slice(half * 1024, (half + 1) * 1024)
                dT = st[f'dT_{d_}'][b, dsl, :]
                duT = st[f'duT_{d_}'][b, dsl, :]
                bcT = st[f'bcT_{d_}'][b]
                if d_ == "b":
                    dT = dT[:, ::-1]; duT = duT[:, ::-1]; bcT = bcT[:, ::-1]
                in_maps.append(dict(dT=np.ascontiguousarray(dT), duT=np.ascontiguousarray(duT),
                                    bcT=np.ascontiguousarray(bcT),
                                    alog=np.ascontiguousarray(alog[dsl].reshape(8, 128, 16)),
                                    sel=_sel_np()))
                meta.append((d_, b, half))
    res = _run(p, in_maps)
    out = {d_: np.zeros((B, 2048, S), BF16NP) for d_ in "fb"}
    for c, (d_, b, half) in enumerate(meta):
        y = np.asarray(res.results[c]["yT"])
        if d_ == "b":
            y = y[:, ::-1]
        out[d_][b, half * 1024:(half + 1) * 1024, :] = y
    return out


def run_mback(x1, st, ys, ins):
    p = _prog("mback", build_mamba_back)
    in_maps = []
    for c in range(N_CORES):
        lo, hi = c * TLOC, (c + 1) * TLOC
        d = dict(x=np.ascontiguousarray(x1[:, lo:hi]),
                 outw_f=ins['L0_attn_fwd_out_proj_w'], outw_b=ins['L0_attn_bwd_out_proj_w'],
                 fw=ins['L0_attn_fusion_w'], fb=np.asarray(ins['L0_attn_fusion_b'])[None, :],
                 ident_in=_ident_np(),
                 Dv_f=_col16(ins['L0_attn_fwd_D']), Dv_b=_col16(ins['L0_attn_bwd_D']))
        for d_ in "fb":
            d[f"yT_{d_}"] = np.ascontiguousarray(ys[d_][:, :, lo:hi])
            d[f"uT_{d_}"] = np.ascontiguousarray(st[f'uT_{d_}'][:, :, lo:hi])
            d[f"zT_{d_}"] = np.ascontiguousarray(st[f'zT_{d_}'][:, :, lo:hi])
        in_maps.append(d)
    res = _run(p, in_maps)
    out = np.empty_like(x1)
    for c in range(N_CORES):
        out[:, c * TLOC:(c + 1) * TLOC, :] = res.results[c]["xo"]
    return out


def run_l1ffn_launch(x_full, ins):
    p = _prog("l1ffn", build_l1ffn)
    in_maps = []
    for c in range(N_CORES):
        lo, hi = c * TLOC, (c + 1) * TLOC
        xh = np.zeros((B, TLOC + 1, D), np.float32)
        xh[:, 1:] = x_full[:, lo:hi]
        if lo > 0:
            xh[:, 0] = x_full[:, lo - 1]
        in_maps.append(dict(x=xh, Wr=ins['L1_ffn_W_r'], Wk=ins['L1_ffn_W_k'], Wv=ins['L1_ffn_W_v'],
                            mur=_col8(ins['L1_ffn_mu_r']), muk=_col8(ins['L1_ffn_mu_k']),
                            ident_in=_ident_np()))
    res = _run(p, in_maps)
    out = np.empty_like(x_full)
    for c in range(N_CORES):
        out[:, c * TLOC:(c + 1) * TLOC, :] = res.results[c]["xo"]
    return out


def run_l1af(x4, ins):
    p = _prog("l1af", build_l1attn_front)
    lncols = np.zeros((128, 6, 8), np.float32)
    lncols[:, 0] = _col8(ins['L1_attn_ln_conv_g']); lncols[:, 1] = _col8(ins['L1_attn_ln_conv_b'])
    lncols[:, 2] = _col8(ins['L1_attn_ln_attn_g']); lncols[:, 3] = _col8(ins['L1_attn_ln_attn_b'])
    bqkv = np.zeros((128, 3, 8), np.float32)
    bqkv[:, 0] = _col8(ins['L1_attn_bq']); bqkv[:, 1] = _col8(ins['L1_attn_bk'])
    bqkv[:, 2] = _col8(ins['L1_attn_bv'])
    in_maps = []
    for c in range(N_CORES):
        lo, hi = c * TLOC, (c + 1) * TLOC
        xh = np.zeros((B, TLOC + 3, D), np.float32)
        xh[:, 3:] = x4[:, lo:hi]
        if lo > 0:
            xh[:, 0:3] = x4[:, lo - 3:lo]
        in_maps.append(dict(x=xh, Wq=ins['L1_attn_Wq'], Wk=ins['L1_attn_Wk'], Wv=ins['L1_attn_Wv'],
                            bqkv=bqkv, cw=np.ascontiguousarray(np.asarray(ins['L1_attn_conv_w']).transpose(2, 0, 1)),
                            cb=np.asarray(ins['L1_attn_conv_b'])[None, :], lncols=lncols,
                            ident_in=_ident_np(),
                            hmask=np.full((128, 1), 0.0 if lo == 0 else 1.0, np.float32)))
    res = _run(p, in_maps)
    g1 = np.concatenate([np.asarray(res.results[c]["g1"]) for c in range(N_CORES)], axis=1)
    qT = np.concatenate([np.asarray(res.results[c]["qT"]) for c in range(N_CORES)], axis=2)
    kT = np.concatenate([np.asarray(res.results[c]["kT"]) for c in range(N_CORES)], axis=2)
    vT = np.concatenate([np.asarray(res.results[c]["vT"]) for c in range(N_CORES)], axis=2)
    return g1, qT, kT, vT


def run_pooled(qT, kT, vT):
    p = _prog("pooled", lambda: build_pooled(T=S))
    rs = (1.0 / np.arange(1, S + 1, dtype=np.float32))[None, :]
    in_maps = [dict(qT=np.ascontiguousarray(qT[:, c * 128:(c + 1) * 128]),
                    kT=np.ascontiguousarray(kT[:, c * 128:(c + 1) * 128]),
                    vT=np.ascontiguousarray(vT[:, c * 128:(c + 1) * 128]), rs=rs)
               for c in range(N_CORES)]
    res = _run(p, in_maps)
    return np.concatenate([np.asarray(res.results[c]["pkvT"]) for c in range(N_CORES)], axis=1)


def run_l1ab(x4, g1, pkvT, ins):
    p = _prog("l1ab", build_l1attn_back)
    lncols = np.zeros((128, 2, 8), np.float32)
    lncols[:, 0] = _col8(ins['L1_attn_ln_boom_g']); lncols[:, 1] = _col8(ins['L1_attn_ln_boom_b'])
    in_maps = []
    for c in range(N_CORES):
        lo, hi = c * TLOC, (c + 1) * TLOC
        in_maps.append(dict(x=np.ascontiguousarray(x4[:, lo:hi]),
                            g1=np.ascontiguousarray(g1[:, lo:hi]),
                            pkvT=np.ascontiguousarray(pkvT[:, :, lo:hi]),
                            Wb=ins['L1_attn_Wb'], Wu=ins['L1_attn_Wu'],
                            bb=_col32(ins['L1_attn_bb']),
                            bu=np.asarray(ins['L1_attn_bu'])[None, :], lncols=lncols,
                            ident_in=_ident_np()))
    res = _run(p, in_maps)
    out = np.empty_like(x4)
    for c in range(N_CORES):
        out[:, c * TLOC:(c + 1) * TLOC, :] = res.results[c]["xo"]
    return out


FUSED = True


def kernel(**inputs):
    if FUSED:
        return kernel_fused(inputs)
    return kernel_unfused(**inputs)


def kernel_unfused(**inputs):
    x0 = np.ascontiguousarray(np.asarray(inputs["L0_attn_x"], np.float32))
    moe_args = (np.asarray(inputs["L0_ffn_W_experts"], np.float32),
                np.asarray(inputs["L0_ffn_b_experts"], np.float32),
                np.asarray(inputs["L0_ffn_W_router"], np.float32),
                np.asarray(inputs["L0_ffn_b_router"], np.float32))
    ins = {k: np.asarray(v) for k, v in inputs.items()}
    x1 = run_moe(x0, *moe_args)
    st = run_mfront(x1, ins)
    ys = run_scan(st, ins)
    x2 = run_mback(x1, st, ys, ins)
    x3 = run_moe(x2, *moe_args)
    x4 = run_l1ffn_launch(x3, ins)
    g1, qT, kT, vT = run_l1af(x4, ins)
    pkvT = run_pooled(qT, kT, vT)
    x5 = run_l1ab(x4, g1, pkvT, ins)
    x6 = run_l1ffn_launch(x5, ins)
    return x6


# version 09 (final)
# speedup vs baseline: 1.2299x; 1.0401x over previous
import contextlib
import numpy as np
import ml_dtypes
import concourse.bass as bass
import concourse.mybir as mybir
from concourse.bass_utils import run_bass_kernel_spmd

F32 = mybir.dt.float32
BF16 = mybir.dt.bfloat16
AF = mybir.ActivationFunctionType
ALU = mybir.AluOpType
AX = mybir.AxisListType
BF16NP = ml_dtypes.bfloat16

N_CORES = 8
B = 2
S = 16384
D = 1024
TLOC = S // N_CORES
NT = TLOC // 128


ENGINES = ("tensor", "vector", "scalar", "gpsimd", "sync")


class Prog:
    def __init__(self, name="prog"):
        self.nc = bass.Bass("TRN2", target_bir_lowering=False)
        self.name = name
        self.stack = contextlib.ExitStack()
        self.ops = []
        self.counts = {e: 0 for e in ENGINES}
        self.dma_counts = {}
        self.deps = {}
        self.waited = {e: {} for e in ENGINES}
        self.sem_handles = {}
        self._coll_sems = set()
        self._finalized = False

    def dram(self, name, shape, dtype, kind="Internal"):
        t = self.nc.dram_tensor(name, list(shape), dtype, kind=kind)
        return t

    def sbuf(self, name, shape, dtype):
        pre = getattr(self, "_stage_name", "")
        return self.stack.enter_context(self.nc.sbuf_tensor(pre + name, list(shape), dtype))

    def psum(self, name, shape, dtype=F32):
        pre = getattr(self, "_stage_name", "")
        return self.stack.enter_context(self.nc.psum_tensor(pre + name, list(shape), dtype))

    @staticmethod
    def _key(item):
        if isinstance(item, tuple):
            h, sk = item
        else:
            h, sk = item, None
        name = getattr(h, "name", None)
        if name is None:
            name = h.tensor.name if hasattr(h, "tensor") else str(h)
        return (name, sk)

    def _entry(self, name):
        return self.deps.setdefault(name, {"w": {}, "r": {}})

    def _conflicts(self, table, sk):
        out = {}
        for (sk2, sem), val in table.items():
            if sk is None or sk2 is None or sk == sk2:
                if out.get(sem, 0) < val:
                    out[sem] = val
        return out

    def _record(self, table, name, sk, sem, val):
        k = (sk, sem)
        if table.get(k, 0) < val:
            table[k] = val

    def op(self, engine, fn, reads=(), writes=(), dma=False, dma_sem=None):
        needed = {}
        rkeys = [self._key(r) for r in reads]
        wkeys = [self._key(w) for w in writes]
        for (name, sk) in rkeys:
            e = self._entry(name)
            for sem, val in self._conflicts(e["w"], sk).items():
                if needed.get(sem, 0) < val:
                    needed[sem] = val
        for (name, sk) in wkeys:
            e = self._entry(name)
            for tab in (e["w"], e["r"]):
                for sem, val in self._conflicts(tab, sk).items():
                    if needed.get(sem, 0) < val:
                        needed[sem] = val
        waits = []
        wt = self.waited[engine]
        for sem, val in sorted(needed.items()):
            if sem.startswith("eng_" + engine) and not dma:
                pass
            if wt.get(sem, 0) < val:
                waits.append((sem, val))
                wt[sem] = val
        if dma:
            if dma_sem == "__coll":
                sem = "dcoll"
                amt = 1
            else:
                slots = getattr(self, "_dma_slots", None)
                if slots is None:
                    self._dma_slots = slots = {}
                if dma_sem not in slots:
                    slots[dma_sem] = f"ds{len(slots)}"
                sem = slots[dma_sem]
                amt = 16
            self.dma_counts[sem] = self.dma_counts.get(sem, 0) + amt
            val_after = self.dma_counts[sem]
        else:
            sem = "eng_" + engine
            amt = 1
            self.counts[engine] += 1
            val_after = self.counts[engine]
        incs = [(sem, amt)]
        for (name, sk) in rkeys:
            self._record(self._entry(name)["r"], name, sk, sem, val_after)
        for (name, sk) in wkeys:
            self._record(self._entry(name)["w"], name, sk, sem, val_after)
        self.ops.append((engine, fn, waits, incs))
        return (sem, val_after)

    def dma(self, queue, out, in_, reads=(), writes=(), sem=None):
        if sem is None:
            pool = list(writes) or list(reads)
            sem = self._key(pool[0])[0]
        return self.op(queue, lambda e: e.dma_start(out=out, in_=in_),
                       reads=reads, writes=writes, dma=True, dma_sem=sem)

    def barrier_wait(self, engine, token):
        sem, val = token
        wt = self.waited[engine]
        if wt.get(sem, 0) >= val:
            return
        wt[sem] = val
        self.ops.append((engine, None, [(sem, val)], []))

    def barrier(self):
        toks = [("eng_" + e, self.counts[e]) for e in ENGINES if self.counts[e]]
        toks += [(s, v) for s, v in self.dma_counts.items() if s not in self._coll_sems]
        for e in ENGINES:
            for t in toks:
                self.barrier_wait(e, t)

    @contextlib.contextmanager
    def stage(self, name):
        old_stack = self.stack
        self.stack = contextlib.ExitStack()
        old_slot = getattr(self, "_dma_slots", None)
        old_name = getattr(self, "_stage_name", "")
        self._dma_slots = {}
        self._stage_name = name + "_"
        try:
            yield
        finally:
            self.barrier()
            self.stack.close()
            self.stack = old_stack
            self._dma_slots = old_slot
            self._stage_name = old_name

    def dyn_dma(self, queue, out, in_, reg, reads=(), writes=(), sem=None):
        return self.dma(queue, out, in_, reads=reads, writes=writes, sem=sem)

    def coll(self, kind, op_, ins_ap, outs_ap, reads=(), writes=()):
        def f(e):
            return e.collective_compute(kind, op_, replica_groups=[list(range(N_CORES))],
                                        ins=[a.opt() for a in ins_ap], outs=[a.opt() for a in outs_ap])
        tok = self.op("gpsimd", f, reads=reads, writes=writes, dma=True, dma_sem="__coll")
        self._coll_sems.add(tok[0])
        return tok

    def finalize(self, final_tokens=()):
        assert not self._finalized
        self._finalized = True
        nc = self.nc
        sems = {}
        for e in ENGINES:
            if self.counts[e]:
                sems["eng_" + e] = self.stack.enter_context(nc.semaphore("eng_" + e))
        for sname in self.dma_counts:
            sems[sname] = self.stack.enter_context(nc.semaphore(sname))
        self.sem_handles = sems
        block = self.stack.enter_context(nc.Block())
        per_engine = {e: [] for e in ENGINES}
        for rec in self.ops:
            per_engine[rec[0]].append(rec)
        fin_engine = "gpsimd"
        final_tokens = list(final_tokens) + [(s, v) for s, v in self.dma_counts.items()]
        for e in ENGINES:
            eng_ops = per_engine[e]
            fins = [t for t in final_tokens] if e == fin_engine else []
            if not eng_ops and not fins:
                continue

            def make(eng_name, eng_ops, fins):
                def body(eng):
                    for (_, fn, waits, incs) in eng_ops:
                        for (sem, val) in waits:
                            if sem in sems:
                                eng.wait_ge(sems[sem], val)
                        if fn is not None:
                            inst = fn(eng)
                            for (sem, amt) in incs:
                                inst.then_inc(sems[sem], amt)
                    for (sem, val) in fins:
                        if sem in sems:
                            eng.wait_ge(sems[sem], val)
                return body

            getattr(block, e)(make(e, eng_ops, fins))
        return nc

    def close(self):
        self.stack.close()



def ln_stats(p, x_tile, sq_scratch, stat, n_feat=D, eps=1e-5):
    v, s = "vector", "scalar"
    p.op(v, lambda e: e.reduce_sum(out=stat["sum"][:], in_=x_tile[:], axis=AX.X),
         reads=[x_tile], writes=[stat["sum"]])
    p.op(s, lambda e: e.activation(out=sq_scratch[:], in_=x_tile[:], func=AF.Square,
                                   accum_out=stat["sumsq"][:]),
         reads=[x_tile], writes=[sq_scratch, stat["sumsq"]])
    p.op(v, lambda e: e.tensor_scalar(out=stat["mean"][:], in0=stat["sum"][:],
                                      scalar1=1.0 / n_feat, scalar2=None, op0=ALU.mult),
         reads=[stat["sum"]], writes=[stat["mean"]])
    p.op(v, lambda e: e.tensor_mul(out=stat["m2"][:], in0=stat["mean"][:], in1=stat["mean"][:]),
         reads=[stat["mean"]], writes=[stat["m2"]])
    p.op(v, lambda e: e.tensor_scalar(out=stat["nm2"][:], in0=stat["m2"][:], scalar1=-1.0,
                                      scalar2=None, op0=ALU.mult),
         reads=[stat["m2"]], writes=[stat["nm2"]])
    p.op(v, lambda e: e.scalar_tensor_tensor(out=stat["var"][:], in0=stat["sumsq"][:],
                                             scalar=1.0 / n_feat, in1=stat["nm2"][:],
                                             op0=ALU.mult, op1=ALU.add),
         reads=[stat["sumsq"], stat["nm2"]], writes=[stat["var"]])
    p.op(v, lambda e: e.tensor_scalar(out=stat["vare"][:], in0=stat["var"][:], scalar1=eps,
                                      scalar2=None, op0=ALU.add),
         reads=[stat["var"]], writes=[stat["vare"]])
    p.op(s, lambda e: e.activation(out=stat["std"][:], in_=stat["vare"][:], func=AF.Ln),
         reads=[stat["vare"]], writes=[stat["std"]])
    p.op(s, lambda e: e.activation(out=stat["rstd"][:], in_=stat["std"][:], func=AF.Exp, scale=-0.5),
         reads=[stat["std"]], writes=[stat["rstd"]])
    p.op(v, lambda e: e.tensor_mul(out=stat["mr"][:], in0=stat["mean"][:], in1=stat["rstd"][:]),
         reads=[stat["mean"], stat["rstd"]], writes=[stat["mr"]])
    p.op(v, lambda e: e.tensor_scalar(out=stat["negmur"][:], in0=stat["mr"][:], scalar1=-1.0,
                                      scalar2=None, op0=ALU.mult),
         reads=[stat["mr"]], writes=[stat["negmur"]])


def make_stat(p, tag):
    names = ("sum", "sumsq", "mean", "m2", "nm2", "var", "vare", "std", "rstd", "mr", "negmur")
    return {n: p.sbuf(f"{tag}_{n}", [128, 1], F32) for n in names}



def emit_moe(p, io):
    x_in = io["x"]; We = io["We"]; be = io["be"]; Wr = io["Wr"]; br = io["br"]
    ident_in = io["ident_in"]; x_out = io["xo"]

    ident = p.sbuf("ident", [128, 128], F32)
    identb = p.sbuf("identb", [128, 128], BF16)
    ones_row = p.sbuf("ones_row", [1, 128], BF16)
    p.dma("sync", ident[:], ident_in, writes=[ident])
    p.op("vector", lambda e: e.tensor_copy(out=identb[:], in_=ident[:]), reads=[ident], writes=[identb])
    p.op("vector", lambda e: e.memset(ones_row[:], 1.0), writes=[ones_row])

    WeT = p.sbuf("WeT", [128, 8, 8, 1024], BF16)
    wld = [p.sbuf(f"wld{i}", [128, 1024], F32) for i in range(2)]
    ex_ps = [p.psum(f"ex_ps{i}", [128, 512], F32) for i in range(4)]
    tr_ps = [ex_ps[0], ex_ps[1]]
    for e_i in range(8):
        for oc in range(8):
            w = wld[(e_i * 8 + oc) % 2]
            p.dma("sync", w[:], We[e_i, oc * 128:(oc + 1) * 128, :], writes=[w])
            for kb in range(8):
                tp = tr_ps[kb % 2]
                sl = w[:, kb * 128:(kb + 1) * 128]
                p.op("tensor", lambda e, sl=sl, tp=tp: e.transpose(out=tp[:, 0:128], in_=sl, identity=ident[:]),
                     reads=[w, ident], writes=[tp])
                dsl = WeT[:, e_i, kb, oc * 128:(oc + 1) * 128]
                p.op("scalar", lambda e, dsl=dsl, tp=tp: e.activation(out=dsl, in_=tp[:, 0:128], func=AF.Copy),
                     reads=[tp], writes=[(WeT, (e_i, kb, oc))])
    WrT = p.sbuf("WrT", [128, 8, 8], BF16)
    wr_s = p.sbuf("wr_s", [8, 1024], F32)
    p.dma("sync", wr_s[:], Wr, writes=[wr_s])
    for kb in range(8):
        tp = tr_ps[kb % 2]
        sl = wr_s[:, kb * 128:(kb + 1) * 128]
        p.op("tensor", lambda e, sl=sl, tp=tp: e.transpose(out=tp[:, 0:8], in_=sl, identity=ident[0:8, 0:8]),
             reads=[wr_s, ident], writes=[tp])
        dsl = WrT[:, kb, :]
        p.op("scalar", lambda e, dsl=dsl, tp=tp: e.activation(out=dsl, in_=tp[:, 0:8], func=AF.Copy),
             reads=[tp], writes=[(WrT, kb)])
    beS = p.sbuf("beS", [8, 1024], BF16)
    be_f = p.sbuf("be_f", [8, 1024], F32)
    p.dma("sync", be_f[:], be, writes=[be_f])
    p.op("vector", lambda e: e.tensor_copy(out=beS[:], in_=be_f[:]), reads=[be_f], writes=[beS])
    brS = p.sbuf("brS", [1, 8], BF16)
    br_f = p.sbuf("br_f", [1, 8], F32)
    p.dma("sync", br_f[:], br.unsqueeze(0), writes=[br_f])
    p.op("vector", lambda e: e.tensor_copy(out=brS[:], in_=br_f[:]), reads=[br_f], writes=[brS])

    NPAR = 2
    xt = [p.sbuf(f"xt{i}", [128, D], F32) for i in range(NPAR)]
    sq = [p.sbuf(f"sq{i}", [128, D], F32) for i in range(NPAR)]
    lnb = [p.sbuf(f"lnb{i}", [128, D], BF16) for i in range(NPAR)]
    lnT = [p.sbuf(f"lnT{i}", [128, 8, 128], BF16) for i in range(NPAR)]
    stats = [make_stat(p, f"st{i}") for i in range(NPAR)]
    tr_bps = [p.psum(f"trb{i}", [128, 128], BF16) for i in range(2)]
    rt_ps = p.psum("rt_ps", [128, 8], F32)
    rw_p = [p.sbuf(f"rw{i}", [128, 8], F32) for i in range(NPAR)]
    rwm_p = [p.sbuf(f"rwm{i}", [128, 8], F32) for i in range(NPAR)]
    m1_p = [p.sbuf(f"m1{i}", [128, 1], F32) for i in range(NPAR)]
    m2_p = [p.sbuf(f"m2{i}", [128, 1], F32) for i in range(NPAR)]
    t8a_p = [p.sbuf(f"t8a{i}", [128, 8], F32) for i in range(NPAR)]
    t8b_p = [p.sbuf(f"t8b{i}", [128, 8], F32) for i in range(NPAR)]
    t8c_p = [p.sbuf(f"t8c{i}", [128, 8], F32) for i in range(NPAR)]
    ssum_p = [p.sbuf(f"ssum{i}", [128, 1], F32) for i in range(NPAR)]
    srec_p = [p.sbuf(f"srec{i}", [128, 1], F32) for i in range(NPAR)]
    rwT_ps = p.psum("rwT_ps", [8, 128], F32)
    rwT_p = [p.sbuf(f"rwT{i}", [8, 128], BF16) for i in range(NPAR)]
    acc = [p.sbuf(f"acc{i}", [128, D], F32) for i in range(3)]
    out_t = [p.sbuf(f"out_t{i}", [128, D], F32) for i in range(NPAR)]
    out_tokens = []

    for b in range(B):
        for ti in range(NT):
            par = (b * NT + ti) % NPAR
            rw, rwm, m1, m2 = rw_p[par], rwm_p[par], m1_p[par], m2_p[par]
            t8a, t8b, t8c, ssum, srec, rwT = (t8a_p[par], t8b_p[par], t8c_p[par],
                                              ssum_p[par], srec_p[par], rwT_p[par])
            X = xt[par]
            p.dma("sync", X[:], x_in[b, ti * 128:(ti + 1) * 128, :], writes=[X])
            st = stats[par]
            ln_stats(p, X, sq[par], st)
            LNB = lnb[par]
            p.op("scalar", lambda e, X=X, LNB=LNB, st=st, rw=rw, rwm=rwm, m1=m1, m2=m2, t8a=t8a, t8b=t8b, t8c=t8c, ssum=ssum, srec=srec, rwT=rwT:
                 e.activation(out=LNB[:], in_=X[:], func=AF.Identity,
                              bias=st["negmur"][:, 0:1], scale=st["rstd"][:, 0:1]),
                 reads=[X, st["negmur"], st["rstd"]], writes=[LNB])
            LT = lnT[par]
            for kb in range(8):
                TPS = tr_bps[kb % 2]
                sl = LNB[:, kb * 128:(kb + 1) * 128]
                p.op("tensor", lambda e, sl=sl, TPS=TPS, rw=rw, rwm=rwm, m1=m1, m2=m2, t8a=t8a, t8b=t8b, t8c=t8c, ssum=ssum, srec=srec, rwT=rwT: e.transpose(out=TPS[:], in_=sl, identity=identb[:]),
                     reads=[LNB, identb], writes=[TPS])
                dsl = LT[:, kb, :]
                p.op("scalar", lambda e, dsl=dsl, TPS=TPS, rw=rw, rwm=rwm, m1=m1, m2=m2, t8a=t8a, t8b=t8b, t8c=t8c, ssum=ssum, srec=srec, rwT=rwT: e.activation(out=dsl, in_=TPS[:], func=AF.Copy),
                     reads=[TPS], writes=[(LT, kb)])
            def rt_mm(e, LT=LT):
                e.matmul(out=rt_ps[:], lhsT=ones_row[:], rhs=brS[:], start=True, stop=False)
                last = None
                for kb in range(8):
                    last = e.matmul(out=rt_ps[:], lhsT=LT[:, kb, :], rhs=WrT[:, kb, :],
                                    start=False, stop=(kb == 7))
                return last
            p.op("tensor", rt_mm, reads=[ones_row, brS, LT, WrT], writes=[rt_ps])
            p.op("vector", lambda e, rw=rw, rwm=rwm, m1=m1, m2=m2, t8a=t8a, t8b=t8b, t8c=t8c, ssum=ssum, srec=srec, rwT=rwT: e.tensor_reduce(out=m1[:], in_=rt_ps[:], axis=AX.X, op=ALU.max, negate=True),
                 reads=[rt_ps], writes=[m1])
            p.op("scalar", lambda e, rw=rw, rwm=rwm, m1=m1, m2=m2, t8a=t8a, t8b=t8b, t8c=t8c, ssum=ssum, srec=srec, rwT=rwT: e.activation(out=rw[:], in_=rt_ps[:], func=AF.Exp, bias=m1[:, 0:1]),
                 reads=[rt_ps, m1], writes=[rw])
            p.op("vector", lambda e, rw=rw, rwm=rwm, m1=m1, m2=m2, t8a=t8a, t8b=t8b, t8c=t8c, ssum=ssum, srec=srec, rwT=rwT: e.reduce_sum(out=ssum[:], in_=rw[:], axis=AX.X),
                 reads=[rw], writes=[ssum])
            p.op("vector", lambda e, rw=rw, rwm=rwm, m1=m1, m2=m2, t8a=t8a, t8b=t8b, t8c=t8c, ssum=ssum, srec=srec, rwT=rwT: e.reciprocal(out=srec[:], in_=ssum[:]),
                 reads=[ssum], writes=[srec])
            p.op("vector", lambda e, rw=rw, rwm=rwm, m1=m1, m2=m2, t8a=t8a, t8b=t8b, t8c=t8c, ssum=ssum, srec=srec, rwT=rwT: e.tensor_reduce(out=m1[:], in_=rw[:], axis=AX.X, op=ALU.max, negate=True),
                 reads=[rw], writes=[m1])
            p.op("vector", lambda e, rw=rw, rwm=rwm, m1=m1, m2=m2, t8a=t8a, t8b=t8b, t8c=t8c, ssum=ssum, srec=srec, rwT=rwT: e.tensor_scalar(out=t8a[:], in0=rw[:], scalar1=m1[:, 0:1],
                                                     scalar2=None, op0=ALU.add),
                 reads=[rw, m1], writes=[t8a])
            p.op("vector", lambda e, rw=rw, rwm=rwm, m1=m1, m2=m2, t8a=t8a, t8b=t8b, t8c=t8c, ssum=ssum, srec=srec, rwT=rwT: e.tensor_scalar(out=t8b[:], in0=t8a[:], scalar1=0.0,
                                                     scalar2=None, op0=ALU.is_lt),
                 reads=[t8a], writes=[t8b])
            p.op("vector", lambda e, rw=rw, rwm=rwm, m1=m1, m2=m2, t8a=t8a, t8b=t8b, t8c=t8c, ssum=ssum, srec=srec, rwT=rwT: e.tensor_mul(out=t8c[:], in0=t8b[:], in1=rw[:]),
                 reads=[t8b, rw], writes=[t8c])
            p.op("vector", lambda e, rw=rw, rwm=rwm, m1=m1, m2=m2, t8a=t8a, t8b=t8b, t8c=t8c, ssum=ssum, srec=srec, rwT=rwT: e.tensor_reduce(out=m2[:], in_=t8c[:], axis=AX.X, op=ALU.max, negate=True),
                 reads=[t8c], writes=[m2])
            p.op("vector", lambda e, rw=rw, rwm=rwm, m1=m1, m2=m2, t8a=t8a, t8b=t8b, t8c=t8c, ssum=ssum, srec=srec, rwT=rwT: e.tensor_scalar(out=t8a[:], in0=rw[:], scalar1=m2[:, 0:1],
                                                     scalar2=None, op0=ALU.add),
                 reads=[rw, m2], writes=[t8a])
            p.op("vector", lambda e, rw=rw, rwm=rwm, m1=m1, m2=m2, t8a=t8a, t8b=t8b, t8c=t8c, ssum=ssum, srec=srec, rwT=rwT: e.tensor_scalar(out=t8b[:], in0=t8a[:], scalar1=0.0,
                                                     scalar2=None, op0=ALU.is_ge),
                 reads=[t8a], writes=[t8b])
            p.op("vector", lambda e, rw=rw, rwm=rwm, m1=m1, m2=m2, t8a=t8a, t8b=t8b, t8c=t8c, ssum=ssum, srec=srec, rwT=rwT: e.tensor_mul(out=rwm[:], in0=t8b[:], in1=rw[:]),
                 reads=[t8b, rw], writes=[rwm])
            p.op("vector", lambda e, rw=rw, rwm=rwm, m1=m1, m2=m2, t8a=t8a, t8b=t8b, t8c=t8c, ssum=ssum, srec=srec, rwT=rwT: e.tensor_scalar(out=rw[:], in0=rwm[:], scalar1=srec[:, 0:1],
                                                     scalar2=None, op0=ALU.mult),
                 reads=[rwm, srec], writes=[rw])
            p.op("tensor", lambda e, rw=rw, rwm=rwm, m1=m1, m2=m2, t8a=t8a, t8b=t8b, t8c=t8c, ssum=ssum, srec=srec, rwT=rwT: e.transpose(out=rwT_ps[:, :], in_=rw[:], identity=ident[:]),
                 reads=[rw, ident], writes=[rwT_ps])
            p.op("scalar", lambda e, rw=rw, rwm=rwm, m1=m1, m2=m2, t8a=t8a, t8b=t8b, t8c=t8c, ssum=ssum, srec=srec, rwT=rwT: e.activation(out=rwT[:], in_=rwT_ps[:], func=AF.Copy),
                 reads=[rwT_ps], writes=[rwT])
            for oc in range(2):
                pse = ex_ps[oc * 2]
                p.op("tensor", lambda e, oc=oc, pse=pse, rw=rw, rwm=rwm, m1=m1, m2=m2, t8a=t8a, t8b=t8b, t8c=t8c, ssum=ssum, srec=srec, rwT=rwT: e.matmul(out=pse[:], lhsT=rwT[:],
                                                                  rhs=beS[:, oc * 512:(oc + 1) * 512],
                                                                  start=True, stop=True),
                     reads=[rwT, beS], writes=[pse])
                p.op("vector", lambda e, oc=oc, pse=pse, rw=rw, rwm=rwm, m1=m1, m2=m2, t8a=t8a, t8b=t8b, t8c=t8c, ssum=ssum, srec=srec, rwT=rwT: e.tensor_copy(out=acc[0][:, oc * 512:(oc + 1) * 512],
                                                                       in_=pse[:]),
                     reads=[pse], writes=[(acc[0], oc)])
            prev = acc[0]
            for e_i in range(8):
                nxt = acc[1 + (e_i % 2)]
                for oc in range(2):
                    pse = ex_ps[oc * 2 + (e_i % 2)]
                    def emm(e, e_i=e_i, oc=oc, LT=LT, pse=pse):
                        last = None
                        for kb in range(8):
                            last = e.matmul(out=pse[:], lhsT=LT[:, kb, :],
                                            rhs=WeT[:, e_i, kb, oc * 512:(oc + 1) * 512],
                                            start=(kb == 0), stop=(kb == 7))
                        return last
                    p.op("tensor", emm, reads=[LT, WeT], writes=[pse])
                    p.op("vector", lambda e, oc=oc, pse=pse, prev=prev, nxt=nxt, e_i=e_i, rw=rw, rwm=rwm, m1=m1, m2=m2, t8a=t8a, t8b=t8b, t8c=t8c, ssum=ssum, srec=srec, rwT=rwT:
                         e.scalar_tensor_tensor(out=nxt[:, oc * 512:(oc + 1) * 512],
                                                in0=pse[:], scalar=rw[:, e_i:e_i + 1],
                                                in1=prev[:, oc * 512:(oc + 1) * 512],
                                                op0=ALU.mult, op1=ALU.add),
                         reads=[pse, rw, (prev, oc)], writes=[(nxt, oc)])
                prev = nxt
            OT = out_t[par]
            p.op("vector", lambda e, OT=OT, prev=prev, X=X, rw=rw, rwm=rwm, m1=m1, m2=m2, t8a=t8a, t8b=t8b, t8c=t8c, ssum=ssum, srec=srec, rwT=rwT:
                 e.scalar_tensor_tensor(out=OT[:], in0=prev[:], scalar=0.5, in1=X[:],
                                        op0=ALU.mult, op1=ALU.add),
                 reads=[prev, X], writes=[OT])
            tok = p.dma("gpsimd", x_out[b, ti * 128:(ti + 1) * 128, :], OT[:], reads=[OT])
            out_tokens.append(tok)

    return


def build_moe(tag="moe"):
    p = Prog(tag)
    io = dict(
        x=p.dram("x", [B, TLOC, D], F32, kind="ExternalInput").ap(),
        We=p.dram("We", [8, D, D], F32, kind="ExternalInput").ap(),
        be=p.dram("be", [8, D], F32, kind="ExternalInput").ap(),
        Wr=p.dram("Wr", [8, D], F32, kind="ExternalInput").ap(),
        br=p.dram("br", [8], F32, kind="ExternalInput").ap(),
        ident_in=p.dram("ident_in", [128, 128], F32, kind="ExternalInput").ap(),
        xo=p.dram("xo", [B, TLOC, D], F32, kind="ExternalOutput").ap(),
    )
    with p.stage("moe"):
        emit_moe(p, io)
    p.finalize()
    return p



CHUNK = 512

def emit_mfront(p, io):
    H = 3
    TE = TLOC + 2 * H
    NCH = TLOC // CHUNK
    x_in = io["x"]
    Win = {d_: io[f"Win_{d_}"] for d_ in "fb"}
    convw = {d_: io[f"convw_{d_}"] for d_ in "fb"}
    convb = {d_: io[f"convb_{d_}"] for d_ in "fb"}
    xproj = {d_: io[f"xproj_{d_}"] for d_ in "fb"}
    dtw = {d_: io[f"dtw_{d_}"] for d_ in "fb"}
    dtb = {d_: io[f"dtb_{d_}"] for d_ in "fb"}
    ident_in = io["ident_in"]
    uT_o = {d_: io[f"uT_{d_}"] for d_ in "fb"}
    zT_o = {d_: io[f"zT_{d_}"] for d_ in "fb"}
    dT_o = {d_: io[f"dT_{d_}"] for d_ in "fb"}
    duT_o = {d_: io[f"duT_{d_}"] for d_ in "fb"}
    rT_o = {d_: io.get(f"rT_{d_}") for d_ in "fb"}
    no_dT = bool(io.get("no_dT"))
    bcT_o = {d_: io[f"bcT_{d_}"] for d_ in "fb"}

    ident = p.sbuf("ident", [128, 128], F32)
    identb = p.sbuf("identb", [128, 128], BF16)
    p.dma("sync", ident[:], ident_in, writes=[ident])
    p.op("vector", lambda e: e.tensor_copy(out=identb[:], in_=ident[:]), reads=[ident], writes=[identb])

    lnT = p.sbuf("lnT", [128, 8, TE + 2], BF16)
    WinT = p.sbuf("WinT", [128, 8, 2048], BF16)
    xprojT = p.sbuf("xprojT", [128, 16, 96], BF16)
    dtwT = p.sbuf("dtwT", [64, 2048], BF16)
    cwcol = p.sbuf("cwcol", [128, 16, 4], F32)
    cbcol = p.sbuf("cbcol", [128, 16], F32)
    dtbcol = p.sbuf("dtbcol", [128, 16], F32)
    xsT = p.sbuf("xsT", [128, 16, CHUNK + H], BF16)
    uT = p.sbuf("uTs", [128, 16, CHUNK], BF16)
    dTc = p.sbuf("dTc", [128, 16, CHUNK], BF16)
    duTc = p.sbuf("duTc", [128, 16, CHUNK], BF16)
    conv_a = p.sbuf("conv_a", [128, CHUNK], BF16)
    conv_b2 = p.sbuf("conv_b2", [128, CHUNK], BF16)
    conv_c = p.sbuf("conv_c", [128, CHUNK], BF16)
    conv_d2 = p.sbuf("conv_d2", [128, CHUNK], BF16)
    zev = [p.sbuf(f"zev{i}", [128, CHUNK], BF16) for i in range(2)]
    xdb = p.sbuf("xdb", [96, CHUNK], BF16)
    bcs = p.sbuf("bcs", [32, CHUNK], BF16)
    sp0 = p.sbuf("sp0", [128, CHUNK], F32)
    sp1 = p.sbuf("sp1", [128, CHUNK], F32)
    sp2 = p.sbuf("sp2", [128, CHUNK], F32)
    xt = [p.sbuf(f"xt{i}", [128, D], F32) for i in range(2)]
    sq = [p.sbuf(f"sq{i}", [128, D], F32) for i in range(2)]
    lnb = [p.sbuf(f"lnb{i}", [128, D], BF16) for i in range(2)]
    stats = [make_stat(p, f"st{i}") for i in range(2)]
    wld = [p.sbuf(f"wld{i}", [128, 1024], F32) for i in range(2)]
    wsm = p.sbuf("wsm", [96, 2048], F32)
    mm_ps = [p.psum(f"mm_ps{i}", [128, 512], F32) for i in range(4)]
    _tr0 = p.psum("tr_ps0", [128, 128], F32)
    tr_ps = [_tr0, _tr0]
    trb_ps = [p.psum(f"trb_ps{i}", [128, 128], BF16) for i in range(2)]
    xd_ps = p.psum("xd_ps", [96, 512], F32)

    def wprep_win(d_, half):
        for rb in range(16):
            w = wld[rb % 2]
            p.dma("sync", w[:], Win[d_][half * 2048 + rb * 128:half * 2048 + (rb + 1) * 128, :], writes=[w])
            for kb in range(8):
                tp = tr_ps[kb % 2]
                sl = w[:, kb * 128:(kb + 1) * 128]
                p.op("tensor", lambda e, sl=sl, tp=tp: e.transpose(out=tp[:], in_=sl, identity=ident[:]),
                     reads=[w, ident], writes=[tp])
                dsl = WinT[:, kb, rb * 128:(rb + 1) * 128]
                p.op("scalar", lambda e, dsl=dsl, tp=tp: e.activation(out=dsl, in_=tp[:], func=AF.Copy),
                     reads=[tp], writes=[(WinT, rb)])
    def wprep_dir(d_):
        p.dma("sync", wsm[:], xproj[d_], writes=[wsm])
        for kb in range(16):
            tp = tr_ps[kb % 2]
            sl = wsm[:, kb * 128:(kb + 1) * 128]
            p.op("tensor", lambda e, sl=sl, tp=tp: e.transpose(out=tp[:, 0:96], in_=sl,
                                                               identity=ident[0:96, 0:96]),
                 reads=[wsm, ident], writes=[tp])
            dsl = xprojT[:, kb, :]
            p.op("scalar", lambda e, dsl=dsl, tp=tp: e.activation(out=dsl, in_=tp[:, 0:96], func=AF.Copy),
                 reads=[tp], writes=[(xprojT, kb)])
        for rb in range(16):
            w = wld[rb % 2]
            p.dma("sync", w[:, 0:64], dtw[d_][rb * 128:(rb + 1) * 128, :], writes=[w])
            tp = tr_ps[rb % 2]
            p.op("tensor", lambda e, w=w, tp=tp: e.transpose(out=tp[0:64, 0:128], in_=w[:, 0:64],
                                                             identity=ident[:]),
                 reads=[w, ident], writes=[tp])
            dsl = dtwT[:, rb * 128:(rb + 1) * 128]
            p.op("scalar", lambda e, dsl=dsl, tp=tp: e.activation(out=dsl, in_=tp[0:64, 0:128], func=AF.Copy),
                 reads=[tp], writes=[(dtwT, rb)])
        p.dma("sync", cwcol[:], convw[d_], writes=[cwcol])
        p.dma("sync", cbcol[:], convb[d_], writes=[cbcol])
        p.dma("sync", dtbcol[:], dtb[d_], writes=[dtbcol])

    def phase1_b(b):
        def do_tile(tok0, ntok, col0):
            par = (tok0 // 128) % 2
            X, SQ, LNB, st = xt[par], sq[par], lnb[par], stats[par]
            p.dma("sync", X[0:ntok, :], x_in[b, tok0:tok0 + ntok, :], writes=[X])
            ln_stats_n(p, X, SQ, st, ntok)
            p.op("scalar", lambda e, X=X, LNB=LNB, st=st, ntok=ntok:
                 e.activation(out=LNB[0:ntok, :], in_=X[0:ntok, :], func=AF.Identity,
                              bias=st["negmur"][0:ntok, 0:1], scale=st["rstd"][0:ntok, 0:1]),
                 reads=[X, st["negmur"], st["rstd"]], writes=[LNB])
            for kb in range(8):
                tp = trb_ps[kb % 2]
                sl = LNB[0:ntok, kb * 128:(kb + 1) * 128]
                idb = identb[0:ntok, 0:ntok]
                p.op("tensor", lambda e, sl=sl, tp=tp, ntok=ntok, idb=idb:
                     e.transpose(out=tp[:, 0:ntok], in_=sl, identity=idb),
                     reads=[LNB, identb], writes=[tp])
                dsl = lnT[:, kb, col0:col0 + ntok]
                p.op("scalar", lambda e, dsl=dsl, tp=tp, ntok=ntok:
                     e.activation(out=dsl, in_=tp[:, 0:ntok], func=AF.Copy),
                     reads=[tp], writes=[(lnT, (kb, col0))])
        do_tile(0, H, 0)
        for ti in range(NT):
            do_tile(H + ti * 128, 128, H + ti * 128)
        do_tile(H + TLOC, H, H + TLOC)

    def inproj_cols(b, col0, n, ech_list, dst, dst_off, psum_sel):
        for i, ech in enumerate(ech_list):
            ps = mm_ps[psum_sel(i)]
            def f(e, ech=ech, ps=ps, col0=col0, n=n):
                last = None
                for kb in range(8):
                    last = e.matmul(out=ps[:, 0:n], lhsT=WinT[:, kb, ech * 128:(ech + 1) * 128],
                                    rhs=lnT[:, kb, col0:col0 + n], start=(kb == 0), stop=(kb == 7))
                return last
            p.op("tensor", f, reads=[WinT, lnT], writes=[ps])
            yield i, ech, ps

    out_sems = []

    colls_after_dir = io.get("_colls_after_dir")
    for d_ in "fb":
        for b in range(B):
            phase1_b(b)
            wprep_dir(d_)
            wprep_win(d_, 0)
            fwd = (d_ == "f")
            chunk_order = range(NCH) if fwd else range(NCH - 1, -1, -1)
            first = True
            for ci in chunk_order:
                c0 = ci * CHUNK
                if fwd:
                    if first:
                        for i, ech, ps in inproj_cols(b, c0, H, range(16), xsT, 0, lambda i: i % 2):
                            dsl = xsT[:, ech, 0:H]
                            p.op("vector", lambda e, dsl=dsl, ps=ps: e.tensor_copy(out=dsl, in_=ps[:, 0:H]),
                                 reads=[ps], writes=[(xsT, ("h", ech))])
                    else:
                        p.op("vector", lambda e: e.tensor_copy(out=xsT[:, :, 0:H],
                                                               in_=xsT[:, :, CHUNK:CHUNK + H]),
                             reads=[xsT], writes=[xsT])
                    body_off = H
                else:
                    if first:
                        for i, ech, ps in inproj_cols(b, c0 + CHUNK + H, H, range(16), xsT, 0, lambda i: i % 2):
                            dsl = xsT[:, ech, CHUNK:CHUNK + H]
                            p.op("vector", lambda e, dsl=dsl, ps=ps: e.tensor_copy(out=dsl, in_=ps[:, 0:H]),
                                 reads=[ps], writes=[(xsT, ("h", ech))])
                    else:
                        p.op("vector", lambda e: e.tensor_copy(out=xsT[:, :, CHUNK:CHUNK + H],
                                                               in_=xsT[:, :, 0:H]),
                             reads=[xsT], writes=[xsT])
                    body_off = 0
                first = False
                for i, ech, ps in inproj_cols(b, H + c0, CHUNK, range(16), xsT, body_off, lambda i: i % 2):
                    dsl = xsT[:, ech, body_off:body_off + CHUNK]
                    p.op("vector", lambda e, dsl=dsl, ps=ps: e.tensor_copy(out=dsl, in_=ps[:]),
                         reads=[ps], writes=[(xsT, ("b", ech))])
                bufs = [conv_a, conv_b2, conv_c, conv_d2]
                for kb in range(16):
                    ks = [0, 1, 2, 3] if fwd else [3, 2, 1, 0]
                    p.op("vector", lambda e, kb=kb, ks=ks, b0=bufs[0]: e.tensor_scalar(
                        out=b0[:], in0=xsT[:, kb, ks[0]:ks[0] + CHUNK],
                        scalar1=cwcol[:, kb, 0:1], scalar2=None, op0=ALU.mult),
                         reads=[xsT, cwcol], writes=[bufs[0]])
                    for j in range(1, 4):
                        p.op("vector", lambda e, kb=kb, j=j, ks=ks, bj=bufs[j], bp=bufs[j - 1]:
                             e.scalar_tensor_tensor(
                                 out=bj[:], in0=xsT[:, kb, ks[j]:ks[j] + CHUNK],
                                 scalar=cwcol[:, kb, j:j + 1], in1=bp[:],
                                 op0=ALU.mult, op1=ALU.add),
                             reads=[xsT, cwcol, bufs[j - 1]], writes=[bufs[j]])
                    p.op("scalar", lambda e, kb=kb: e.activation(
                        out=uT[:, kb, :], in_=conv_d2[:], func=AF.Silu, bias=cbcol[:, kb:kb + 1]),
                         reads=[conv_d2, cbcol], writes=[(uT, kb)])
                    p.dma("gpsimd", uT_o[d_][b, kb * 128:(kb + 1) * 128, c0:c0 + CHUNK], uT[:, kb, :],
                          reads=[(uT, kb)], sem=f"ust{kb}")
                def xp(e):
                    last = None
                    for kb in range(16):
                        last = e.matmul(out=xd_ps[:], lhsT=xprojT[:, kb, :], rhs=uT[:, kb, :],
                                        start=(kb == 0), stop=(kb == 15))
                    return last
                p.op("tensor", xp, reads=[uT, xprojT], writes=[xd_ps])
                p.op("vector", lambda e: e.tensor_copy(out=xdb[:], in_=xd_ps[:]),
                     reads=[xd_ps], writes=[xdb])
                p.op("vector", lambda e: e.tensor_copy(out=bcs[:], in_=xd_ps[64:96, :]),
                     reads=[xd_ps], writes=[bcs])
                p.dma("gpsimd", bcT_o[d_][b, :, c0:c0 + CHUNK], bcs[:], reads=[bcs], sem="bcst")
                if rT_o[d_] is not None:
                    p.dma("gpsimd", rT_o[d_][b, :, c0:c0 + CHUNK], xdb[0:64, :], reads=[xdb], sem="rst")
                for dch in range(16):
                    ps = mm_ps[dch % 2]
                    p.op("tensor", lambda e, dch=dch, ps=ps: e.matmul(
                        out=ps[:], lhsT=dtwT[:, dch * 128:(dch + 1) * 128], rhs=xdb[0:64, :],
                        start=True, stop=True),
                         reads=[dtwT, xdb], writes=[ps])
                    p.op("scalar", lambda e, ps=ps, dch=dch: e.activation(
                        out=sp0[:], in_=ps[:], func=AF.Identity, bias=dtbcol[:, dch:dch + 1]),
                         reads=[ps, dtbcol], writes=[sp0])
                    p.op("scalar", lambda e: e.activation(out=sp1[:], in_=sp0[:], func=AF.Abs),
                         reads=[sp0], writes=[sp1])
                    p.op("scalar", lambda e: e.activation(out=sp2[:], in_=sp1[:], func=AF.Exp, scale=-1.0),
                         reads=[sp1], writes=[sp2])
                    p.op("scalar", lambda e: e.activation(out=sp1[:], in_=sp2[:], func=AF.Ln, bias=1.0),
                         reads=[sp2], writes=[sp1])
                    p.op("vector", lambda e: e.tensor_scalar(out=sp2[:], in0=sp0[:], scalar1=0.0,
                                                             scalar2=None, op0=ALU.max),
                         reads=[sp0], writes=[sp2])
                    p.op("vector", lambda e, dch=dch: e.tensor_add(out=dTc[:, dch, :], in0=sp2[:], in1=sp1[:]),
                         reads=[sp2, sp1], writes=[(dTc, dch)])
                    p.op("vector", lambda e, dch=dch: e.tensor_mul(out=duTc[:, dch, :], in0=dTc[:, dch, :],
                                                                   in1=uT[:, dch, :]),
                         reads=[(dTc, dch), (uT, dch)], writes=[(duTc, dch)])
                    if not no_dT:
                        p.dma("gpsimd", dT_o[d_][b, dch * 128:(dch + 1) * 128, c0:c0 + CHUNK], dTc[:, dch, :],
                              reads=[(dTc, dch)], sem=f"dst{dch}")
                    p.dma("gpsimd", duT_o[d_][b, dch * 128:(dch + 1) * 128, c0:c0 + CHUNK], duTc[:, dch, :],
                          reads=[(duTc, dch)], sem=f"dust{dch}")
            wprep_win(d_, 1)
            for ci in range(NCH):
                c0 = ci * CHUNK
                for i, ech, ps in inproj_cols(b, H + c0, CHUNK, range(16), None, 0, lambda i: 2 + i % 2):
                    zv = zev[i % 2]
                    p.op("vector", lambda e, zv=zv, ps=ps: e.tensor_copy(out=zv[:], in_=ps[:]),
                         reads=[ps], writes=[zv])
                    p.dma("gpsimd", zT_o[d_][b, ech * 128:(ech + 1) * 128, c0:c0 + CHUNK], zv[:],
                          reads=[zv], sem=f"zst{i % 2}")
        if colls_after_dir is not None:
            colls_after_dir(d_)
    return


def _mfront_io(p, kind_in="ExternalInput", kind_out="ExternalOutput", pre="", make_x=True):
    io = {}
    if make_x:
        io["x"] = p.dram(pre + "x", [B, TLOC + 6, D], F32, kind=kind_in).ap()
        io["ident_in"] = p.dram(pre + "ident_in", [128, 128], F32, kind="ExternalInput").ap()
    for d_ in "fb":
        io[f"Win_{d_}"] = p.dram(pre + f"Win_{d_}", [4096, D], F32, kind="ExternalInput").ap()
        io[f"convw_{d_}"] = p.dram(pre + f"convw_{d_}", [128, 16, 4], F32, kind="ExternalInput").ap()
        io[f"convb_{d_}"] = p.dram(pre + f"convb_{d_}", [128, 16], F32, kind="ExternalInput").ap()
        io[f"xproj_{d_}"] = p.dram(pre + f"xproj_{d_}", [96, 2048], F32, kind="ExternalInput").ap()
        io[f"dtw_{d_}"] = p.dram(pre + f"dtw_{d_}", [2048, 64], F32, kind="ExternalInput").ap()
        io[f"dtb_{d_}"] = p.dram(pre + f"dtb_{d_}", [128, 16], F32, kind="ExternalInput").ap()
        for nm in ("uT", "zT", "dT", "duT"):
            io[f"{nm}_{d_}"] = p.dram(pre + f"{nm}_{d_}", [B, 2048, TLOC], BF16, kind=kind_out).ap()
        io[f"bcT_{d_}"] = p.dram(pre + f"bcT_{d_}", [B, 32, TLOC], BF16, kind=kind_out).ap()
        io[f"rT_{d_}"] = p.dram(pre + f"rT_{d_}", [B, 64, TLOC], BF16, kind=kind_out).ap()
    return io


def build_mamba_front(tag="mfront"):
    p = Prog(tag)
    io = _mfront_io(p)
    with p.stage("mf"):
        emit_mfront(p, io)
    p.finalize()
    return p


def ln_stats_n(p, x_tile, sq_scratch, stat, ntok, n_feat=D, eps=1e-5):
    v, s = "vector", "scalar"
    p.op(v, lambda e: e.reduce_sum(out=stat["sum"][0:ntok, :], in_=x_tile[0:ntok, :], axis=AX.X),
         reads=[x_tile], writes=[stat["sum"]])
    p.op(s, lambda e: e.activation(out=sq_scratch[0:ntok, :], in_=x_tile[0:ntok, :], func=AF.Square,
                                   accum_out=stat["sumsq"][0:ntok, :]),
         reads=[x_tile], writes=[sq_scratch, stat["sumsq"]])
    p.op(v, lambda e: e.tensor_scalar(out=stat["mean"][0:ntok, :], in0=stat["sum"][0:ntok, :],
                                      scalar1=1.0 / n_feat, scalar2=None, op0=ALU.mult),
         reads=[stat["sum"]], writes=[stat["mean"]])
    p.op(v, lambda e: e.tensor_mul(out=stat["m2"][0:ntok, :], in0=stat["mean"][0:ntok, :],
                                   in1=stat["mean"][0:ntok, :]),
         reads=[stat["mean"]], writes=[stat["m2"]])
    p.op(v, lambda e: e.tensor_scalar(out=stat["nm2"][0:ntok, :], in0=stat["m2"][0:ntok, :], scalar1=-1.0,
                                      scalar2=None, op0=ALU.mult),
         reads=[stat["m2"]], writes=[stat["nm2"]])
    p.op(v, lambda e: e.scalar_tensor_tensor(out=stat["var"][0:ntok, :], in0=stat["sumsq"][0:ntok, :],
                                             scalar=1.0 / n_feat, in1=stat["nm2"][0:ntok, :],
                                             op0=ALU.mult, op1=ALU.add),
         reads=[stat["sumsq"], stat["nm2"]], writes=[stat["var"]])
    p.op(v, lambda e: e.tensor_scalar(out=stat["vare"][0:ntok, :], in0=stat["var"][0:ntok, :], scalar1=eps,
                                      scalar2=None, op0=ALU.add),
         reads=[stat["var"]], writes=[stat["vare"]])
    p.op(s, lambda e: e.activation(out=stat["std"][0:ntok, :], in_=stat["vare"][0:ntok, :], func=AF.Ln),
         reads=[stat["vare"]], writes=[stat["std"]])
    p.op(s, lambda e: e.activation(out=stat["rstd"][0:ntok, :], in_=stat["std"][0:ntok, :], func=AF.Exp,
                                   scale=-0.5),
         reads=[stat["std"]], writes=[stat["rstd"]])
    p.op(v, lambda e: e.tensor_mul(out=stat["mr"][0:ntok, :], in0=stat["mean"][0:ntok, :],
                                   in1=stat["rstd"][0:ntok, :]),
         reads=[stat["mean"], stat["rstd"]], writes=[stat["mr"]])
    p.op(v, lambda e: e.tensor_scalar(out=stat["negmur"][0:ntok, :], in0=stat["mr"][0:ntok, :], scalar1=-1.0,
                                      scalar2=None, op0=ALU.mult),
         reads=[stat["mr"]], writes=[stat["negmur"]])



SCH = 512

def emit_scan(p, io, T=S):
    NCHS = T // SCH
    dT_i = io["dT"]; duT_i = io["duT"]; bcT_i = io["bcT"]
    alog_i = io["alog"]; sel_i = io["sel"]; yT_o = io["yT"]

    SEL = p.sbuf("SEL", [32, 32, 128], BF16)
    p.dma("sync", SEL[:], sel_i, writes=[SEL])
    acols = p.sbuf("acols", [128, 8, 16], F32)
    atmp = p.sbuf("atmp", [128, 16], F32)
    atmp2 = p.sbuf("atmp2", [128, 16], F32)
    for db in range(8):
        p.dma("sync", atmp[:], alog_i[db], writes=[atmp])
        p.op("scalar", lambda e: e.activation(out=atmp2[:], in_=atmp[:], func=AF.Exp),
             reads=[atmp], writes=[atmp2])
        p.op("vector", lambda e, db=db: e.tensor_scalar(out=acols[:, db, :], in0=atmp2[:], scalar1=-1.0,
                                                        scalar2=None, op0=ALU.mult),
             reads=[atmp2], writes=[(acols, db)])

    bcc = p.sbuf("bcc", [32, SCH], BF16)
    Bb = p.sbuf("Bb", [128, 16, SCH], BF16)
    Cb = p.sbuf("Cb", [128, 16, SCH], BF16)
    bc_ps = [p.psum(f"bc_ps{i}", [128, SCH], F32) for i in range(2)]
    dA = [p.sbuf(f"dA{i}", [128, 16, SCH], F32) for i in range(2)]
    dBu = [p.sbuf(f"dBu{i}", [128, 16, SCH], BF16) for i in range(2)]
    hb = p.sbuf("hb", [128, 16, SCH], BF16)
    ta = p.sbuf("ta", [128, 8, SCH], BF16)
    tb = p.sbuf("tb", [128, 4, SCH], BF16)
    tc = p.sbuf("tc", [128, 2, SCH], BF16)
    yv = p.sbuf("yv", [128, SCH], BF16)
    dTt = [p.sbuf(f"dTt{i}", [128, SCH], BF16) for i in range(2)]
    duTt = [p.sbuf(f"duTt{i}", [128, SCH], BF16) for i in range(2)]
    carry = p.sbuf("carry", [128, 8, 16], F32)
    p.op("vector", lambda e: e.memset(carry[:], 0.0), writes=[carry])

    for ci in range(NCHS):
        c0 = ci * SCH
        p.dma("sync", bcc[:], bcT_i[:, c0:c0 + SCH], writes=[bcc])
        for s in range(16):
            ps = bc_ps[s % 2]
            p.op("tensor", lambda e, s=s, ps=ps: e.matmul(out=ps[:], lhsT=SEL[:, s, :], rhs=bcc[:],
                                                          start=True, stop=True),
                 reads=[SEL, bcc], writes=[ps])
            p.op("scalar", lambda e, s=s, ps=ps: e.activation(out=Bb[:, s, :], in_=ps[:], func=AF.Copy),
                 reads=[ps], writes=[(Bb, s)])
            ps2 = bc_ps[(s + 1) % 2]
            p.op("tensor", lambda e, s=s, ps2=ps2: e.matmul(out=ps2[:], lhsT=SEL[:, 16 + s, :], rhs=bcc[:],
                                                            start=True, stop=True),
                 reads=[SEL, bcc], writes=[ps2])
            p.op("scalar", lambda e, s=s, ps2=ps2: e.activation(out=Cb[:, s, :], in_=ps2[:], func=AF.Copy),
                 reads=[ps2], writes=[(Cb, s)])
        for db in range(8):
            par = db % 2
            dt_, du_ = dTt[par], duTt[par]
            p.dma("sync", dt_[:], dT_i[db * 128:(db + 1) * 128, c0:c0 + SCH], writes=[dt_])
            p.dma("sync", du_[:], duT_i[db * 128:(db + 1) * 128, c0:c0 + SCH], writes=[du_])
            DA, DB = dA[par], dBu[par]
            for s in range(16):
                p.op("scalar", lambda e, s=s, db=db, DA=DA, dt_=dt_: e.activation(
                    out=DA[:, s, :], in_=dt_[:], func=AF.Exp, scale=acols[:, db, s:s + 1]),
                     reads=[dt_, (acols, db)], writes=[(DA, s)])
                p.op("vector", lambda e, s=s, DB=DB, du_=du_: e.tensor_mul(
                    out=DB[:, s, :], in0=du_[:], in1=Bb[:, s, :]),
                     reads=[du_, (Bb, s)], writes=[(DB, s)])
            for s in range(16):
                p.op("vector", lambda e, s=s, db=db, DA=DA, DB=DB: e.tensor_tensor_scan(
                    out=hb[:, s, :], data0=DA[:, s, :], data1=DB[:, s, :],
                    initial=carry[:, db, s:s + 1], op0=ALU.mult, op1=ALU.add),
                     reads=[(DA, s), (DB, s), (carry, db)], writes=[(hb, s)])
            p.op("vector", lambda e, db=db: e.tensor_copy(
                out=carry[:, db, :], in_=bass.AP(hb[:].tensor, hb[:].offset + SCH - 1,
                                                 [[hb[:].ap[0][0], 128], [SCH, 16]])),
                 reads=[hb], writes=[(carry, db)])
            p.op("vector", lambda e, DB=DB: e.tensor_mul(out=DB[:], in0=hb[:], in1=Cb[:]),
                 reads=[hb, Cb], writes=[DB])
            p.op("vector", lambda e, DB=DB: e.tensor_add(out=ta[:], in0=DB[:, 0:8, :], in1=DB[:, 8:16, :]),
                 reads=[DB], writes=[ta])
            p.op("vector", lambda e: e.tensor_add(out=tb[:], in0=ta[:, 0:4, :], in1=ta[:, 4:8, :]),
                 reads=[ta], writes=[tb])
            p.op("vector", lambda e: e.tensor_add(out=tc[:], in0=tb[:, 0:2, :], in1=tb[:, 2:4, :]),
                 reads=[tb], writes=[tc])
            p.op("vector", lambda e: e.tensor_add(out=yv[:], in0=tc[:, 0, :], in1=tc[:, 1, :]),
                 reads=[tc], writes=[yv])
            p.dma("gpsimd", yT_o[db * 128:(db + 1) * 128, c0:c0 + SCH], yv[:], reads=[yv], sem=f"yst{par}")
    return


def build_scan(T=S, tag="scan"):
    p = Prog(tag)
    io = dict(
        dT=p.dram("dT", [1024, T], BF16, kind="ExternalInput").ap(),
        duT=p.dram("duT", [1024, T], BF16, kind="ExternalInput").ap(),
        bcT=p.dram("bcT", [32, T], BF16, kind="ExternalInput").ap(),
        alog=p.dram("alog", [8, 128, 16], F32, kind="ExternalInput").ap(),
        sel=p.dram("sel", [32, 32, 128], BF16, kind="ExternalInput").ap(),
        yT=p.dram("yT", [1024, T], BF16, kind="ExternalOutput").ap(),
    )
    with p.stage("sc"):
        emit_scan(p, io, T=T)
    p.finalize()
    return p



def emit_mback(p, io, yT_colf=None, y_prefetch=None):
    NCH = TLOC // CHUNK
    x_in = io["x"]
    yT_i = {d_: io.get(f"yT_{d_}") for d_ in "fb"}
    uT_i = {d_: io[f"uT_{d_}"] for d_ in "fb"}
    zT_i = {d_: io[f"zT_{d_}"] for d_ in "fb"}
    Dv = {d_: io[f"Dv_{d_}"] for d_ in "fb"}
    outw = {d_: io[f"outw_{d_}"] for d_ in "fb"}
    fw = io["fw"]; fb_ = io["fb"]; ident_in = io["ident_in"]; x_out = io["xo"]

    ident = p.sbuf("ident", [128, 128], F32)
    p.dma("sync", ident[:], ident_in, writes=[ident])
    ones_row = p.sbuf("ones_row", [1, 128], BF16)
    p.op("vector", lambda e: e.memset(ones_row[:], 1.0), writes=[ones_row])
    fbS = p.sbuf("fbS", [1, 1024], BF16)
    fbf = p.sbuf("fbf", [1, 1024], F32)
    p.dma("sync", fbf[:], fb_, writes=[fbf])
    p.op("vector", lambda e: e.tensor_copy(out=fbS[:], in_=fbf[:]), reads=[fbf], writes=[fbS])
    Dcol = {}
    for d_ in "fb":
        Dcol[d_] = p.sbuf(f"Dcol{d_}", [128, 16], F32)
        p.dma("sync", Dcol[d_][:], Dv[d_], writes=[Dcol[d_]])

    owT = {d_: p.sbuf(f"owT{d_}", [128, 16, 1024], BF16) for d_ in "fb"}
    fwT = p.sbuf("fwT", [128, 16, 1024], BF16)
    wld = p.sbuf("wld", [128, 2048], F32)
    tr_ps = p.psum("tr_ps", [128, 128], F32)
    def prep_T(dst, W, nrow_blocks, ncol_blocks):
        for rb in range(nrow_blocks):
            p.dma("sync", wld[:, 0:ncol_blocks * 128], W[rb * 128:(rb + 1) * 128, :], writes=[wld])
            for kb in range(ncol_blocks):
                sl = wld[:, kb * 128:(kb + 1) * 128]
                p.op("tensor", lambda e, sl=sl: e.transpose(out=tr_ps[:], in_=sl, identity=ident[:]),
                     reads=[wld, ident], writes=[tr_ps])
                dsl = dst[:, kb, rb * 128:(rb + 1) * 128]
                p.op("scalar", lambda e, dsl=dsl: e.activation(out=dsl, in_=tr_ps[:], func=AF.Copy),
                     reads=[tr_ps], writes=[(dst, (kb, rb))])
    for d_ in "fb":
        prep_T(owT[d_], outw[d_], 8, 16)
    prep_T(fwT, fw, 8, 16)

    yt_p = [p.sbuf(f"yt{i}", [128, CHUNK], BF16) for i in range(2)]
    ut_p = [p.sbuf(f"ut{i}", [128, CHUNK], BF16) for i in range(2)]
    zt_p = [p.sbuf(f"zt{i}", [128, CHUNK], BF16) for i in range(2)]
    sz_p = [p.sbuf(f"sz{i}", [128, CHUNK], F32) for i in range(2)]
    g1_p = [p.sbuf(f"g1{i}", [128, CHUNK], F32) for i in range(2)]
    yg = p.sbuf("yg", [128, 16, CHUNK], BF16)
    oT = p.sbuf("oT", [128, 16, CHUNK], BF16)
    xr = [p.sbuf(f"xr{i}", [128, D], F32) for i in range(2)]
    xo_t = [p.sbuf(f"xo_t{i}", [128, D], F32) for i in range(2)]
    mm_ps = [p.psum(f"mm_ps{i}", [128, 512], F32) for i in range(4)]

    kb0 = 0
    for b in range(B):
        for ci in range(NCH):
            c0 = ci * CHUNK
            if y_prefetch is not None:
                y_prefetch(b, ci)
            for di, d_ in enumerate("fb"):
                for kb in range(16):
                    yt, ut, zt, sz, g1 = (yt_p[kb % 2], ut_p[kb % 2], zt_p[kb % 2],
                                          sz_p[kb % 2], g1_p[kb % 2])
                    if yT_colf is not None:
                        ysl = yT_colf(d_, b, kb, c0)
                        p.op("vector", lambda e, ysl=ysl, yt=yt, ut=ut, zt=zt, sz=sz, g1=g1: e.tensor_copy(out=yt[:], in_=ysl),
                             reads=[("yslab", None)], writes=[yt])
                    else:
                        p.dma("sync", yt[:], yT_i[d_][b, kb * 128:(kb + 1) * 128, c0:c0 + CHUNK],
                              writes=[yt])
                    p.dma("sync", ut[:], uT_i[d_][b, kb * 128:(kb + 1) * 128, c0:c0 + CHUNK], writes=[ut])
                    p.dma("sync", zt[:], zT_i[d_][b, kb * 128:(kb + 1) * 128, c0:c0 + CHUNK], writes=[zt])
                    p.op("scalar", lambda e, yt=yt, ut=ut, zt=zt, sz=sz, g1=g1: e.activation(out=sz[:], in_=zt[:], func=AF.Silu),
                         reads=[zt], writes=[sz])
                    p.op("vector", lambda e, kb=kb, d_=d_, yt=yt, ut=ut, zt=zt, sz=sz, g1=g1: e.scalar_tensor_tensor(
                        out=g1[:], in0=ut[:], scalar=Dcol[d_][:, kb:kb + 1], in1=yt[:],
                        op0=ALU.mult, op1=ALU.add),
                         reads=[ut, Dcol[d_], yt], writes=[g1])
                    p.op("vector", lambda e, kb=kb, yt=yt, ut=ut, zt=zt, sz=sz, g1=g1: e.tensor_mul(out=yg[:, kb, :], in0=g1[:], in1=sz[:]),
                         reads=[g1, sz], writes=[(yg, kb)])
                for och in range(8):
                    ps = mm_ps[och % 2]
                    def f(e, och=och, d_=d_, ps=ps):
                        last = None
                        for kb in range(16):
                            last = e.matmul(out=ps[:], lhsT=owT[d_][:, kb, och * 128:(och + 1) * 128],
                                            rhs=yg[:, kb, :], start=(kb == 0), stop=(kb == 15))
                        return last
                    p.op("tensor", f, reads=[yg, owT[d_]], writes=[ps])
                    p.op("scalar", lambda e, och=och, di=di, ps=ps, yt=yt, ut=ut, zt=zt, sz=sz, g1=g1: e.activation(
                        out=oT[:, di * 8 + och, :], in_=ps[:], func=AF.Copy),
                         reads=[ps], writes=[(oT, di * 8 + och)])
            for tt in range(CHUNK // 128):
                X = xr[tt % 2]
                XO = xo_t[tt % 2]
                p.dma("sync", X[:], x_in[b, c0 + tt * 128:c0 + (tt + 1) * 128, :], writes=[X])
                for oc in range(2):
                    ps = mm_ps[2 + oc]
                    def f2(e, tt=tt, oc=oc, ps=ps):
                        e.matmul(out=ps[:], lhsT=ones_row[:], rhs=fbS[:, oc * 512:(oc + 1) * 512],
                                 start=True, stop=False)
                        last = None
                        for kb in range(16):
                            last = e.matmul(out=ps[:], lhsT=oT[:, kb, tt * 128:(tt + 1) * 128],
                                            rhs=fwT[:, kb, oc * 512:(oc + 1) * 512],
                                            start=False, stop=(kb == 15))
                        return last
                    p.op("tensor", f2, reads=[oT, fwT, ones_row, fbS], writes=[ps])
                    p.op("vector", lambda e, oc=oc, ps=ps, X=X, XO=XO, yt=yt, ut=ut, zt=zt, sz=sz, g1=g1: e.tensor_add(
                        out=XO[:, oc * 512:(oc + 1) * 512], in0=ps[:], in1=X[:, oc * 512:(oc + 1) * 512]),
                         reads=[ps, X], writes=[(XO, oc)])
                p.dma("gpsimd", x_out[b, c0 + tt * 128:c0 + (tt + 1) * 128, :], XO[:],
                      reads=[XO], sem=f"xst{tt % 2}")
    return


def _mback_w_io(p):
    io = dict(
        fw=p.dram("fw", [1024, 2048], F32, kind="ExternalInput").ap(),
        fb=p.dram("fb", [1, 1024], F32, kind="ExternalInput").ap(),
    )
    for d_ in "fb":
        io[f"Dv_{d_}"] = p.dram(f"Dv_{d_}", [128, 16], F32, kind="ExternalInput").ap()
        io[f"outw_{d_}"] = p.dram(f"outw_{d_}", [1024, 2048], F32, kind="ExternalInput").ap()
    return io


def build_mamba_back(tag="mback"):
    p = Prog(tag)
    io = _mback_w_io(p)
    io["x"] = p.dram("x", [B, TLOC, D], F32, kind="ExternalInput").ap()
    io["ident_in"] = p.dram("ident_in", [128, 128], F32, kind="ExternalInput").ap()
    io["xo"] = p.dram("xo", [B, TLOC, D], F32, kind="ExternalOutput").ap()
    for d_ in "fb":
        for nm in ("yT", "uT", "zT"):
            io[f"{nm}_{d_}"] = p.dram(f"{nm}_{d_}", [B, 2048, TLOC], BF16, kind="ExternalInput").ap()
    with p.stage("mb"):
        emit_mback(p, io)
    p.finalize()
    return p



def emit_l1ffn(p, io):
    NCH = TLOC // CHUNK
    x_in = io["x"]
    Wr_i = io["Wr"]; Wk_i = io["Wk"]; Wv_i = io["Wv"]
    mur_i = io["mur"]; muk_i = io["muk"]
    ident_in = io["ident_in"]; x_out = io["xo"]

    ident = p.sbuf("ident", [128, 128], F32)
    identb = p.sbuf("identb", [128, 128], BF16)
    p.dma("sync", ident[:], ident_in, writes=[ident])
    p.op("vector", lambda e: e.tensor_copy(out=identb[:], in_=ident[:]), reads=[ident], writes=[identb])
    mur = p.sbuf("mur_s", [128, 8], F32)
    muk = p.sbuf("muk_s", [128, 8], F32)
    murm = p.sbuf("murm", [128, 8], F32)
    mukm = p.sbuf("mukm", [128, 8], F32)
    p.dma("sync", mur[:], mur_i, writes=[mur])
    p.dma("sync", muk[:], muk_i, writes=[muk])
    for src_, dst_ in ((mur, murm), (muk, mukm)):
        p.op("vector", lambda e, src_=src_, dst_=dst_: e.tensor_scalar(
            out=dst_[:], in0=src_[:], scalar1=-1.0, scalar2=1.0, op0=ALU.mult, op1=ALU.add),
             reads=[src_], writes=[dst_])

    WT = {n: p.sbuf(f"WT{n}", [128, 8, D], BF16) for n in "rkv"}
    wld = p.sbuf("wld", [128, D], F32)
    tr_ps = p.psum("tr_ps", [128, 128], F32)
    for n, W in (("r", Wr_i), ("k", Wk_i), ("v", Wv_i)):
        for rb in range(8):
            p.dma("sync", wld[:], W[rb * 128:(rb + 1) * 128, :], writes=[wld])
            for kb in range(8):
                sl = wld[:, kb * 128:(kb + 1) * 128]
                p.op("tensor", lambda e, sl=sl: e.transpose(out=tr_ps[:], in_=sl, identity=ident[:]),
                     reads=[wld, ident], writes=[tr_ps])
                dsl = WT[n][:, kb, rb * 128:(rb + 1) * 128]
                p.op("scalar", lambda e, dsl=dsl: e.activation(out=dsl, in_=tr_ps[:], func=AF.Copy),
                     reads=[tr_ps], writes=[(WT[n], (kb, rb))])

    xt = [p.sbuf(f"xt{i}", [128, D], F32) for i in range(2)]
    sq = [p.sbuf(f"sq{i}", [128, D], F32) for i in range(2)]
    lnb = [p.sbuf(f"lnb{i}", [128, D], BF16) for i in range(2)]
    stats = [make_stat(p, f"st{i}") for i in range(2)]
    trb_ps = [p.psum(f"trb_ps{i}", [128, 128], BF16) for i in range(2)]
    lnT = p.sbuf("lnT", [128, 8, CHUNK + 1], BF16)
    xrT = p.sbuf("xrT", [128, 8, CHUNK], BF16)
    xkT = p.sbuf("xkT", [128, 8, CHUNK], BF16)
    t1 = p.sbuf("t1", [128, CHUNK], BF16)
    kaT = p.sbuf("kaT", [128, 8, CHUNK], BF16)
    krel = p.sbuf("krel", [128, CHUNK], F32)
    rS = p.sbuf("rS", [128, 1024], F32)
    xc = [p.sbuf(f"xc{i}", [128, D], F32) for i in range(2)]
    vo = p.sbuf("vo", [128, D], F32)
    xo_t = [p.sbuf(f"xo_t{i}", [128, D], F32) for i in range(2)]
    mm_ps = [p.psum(f"mm_ps{i}", [128, 512], F32) for i in range(4)]

    for b in range(B):
        for ci in range(NCH):
            c0 = ci * CHUNK
            if ci == 0:
                X = xt[0]
                p.dma("sync", X[0:1, :], x_in[b, 0:1, :], writes=[X])
                st0 = stats[0]
                ln_stats_n(p, X, sq[0], st0, 1)
                p.op("scalar", lambda e, X=X, st0=st0: e.activation(
                    out=lnb[0][0:1, :], in_=X[0:1, :], func=AF.Identity,
                    bias=st0["negmur"][0:1, 0:1], scale=st0["rstd"][0:1, 0:1]),
                     reads=[X, st0["negmur"], st0["rstd"]], writes=[lnb[0]])
                for kb in range(8):
                    tp = trb_ps[kb % 2]
                    p.op("tensor", lambda e, kb=kb, tp=tp: e.transpose(
                        out=tp[:, 0:1], in_=lnb[0][0:1, kb * 128:(kb + 1) * 128], identity=identb[0:1, 0:1]),
                         reads=[lnb[0], identb], writes=[tp])
                    p.op("scalar", lambda e, kb=kb, tp=tp: e.activation(
                        out=lnT[:, kb, 0:1], in_=tp[:, 0:1], func=AF.Copy),
                         reads=[tp], writes=[(lnT, ("h", kb))])
            else:
                p.op("vector", lambda e: e.tensor_copy(out=lnT[:, :, 0:1], in_=lnT[:, :, CHUNK:CHUNK + 1]),
                     reads=[lnT], writes=[lnT])
            for tt in range(CHUNK // 128):
                par = tt % 2
                X, SQ, LNB, st = xt[par], sq[par], lnb[par], stats[par]
                p.dma("sync", X[:], x_in[b, 1 + c0 + tt * 128:1 + c0 + (tt + 1) * 128, :], writes=[X])
                ln_stats_n(p, X, SQ, st, 128)
                p.op("scalar", lambda e, X=X, LNB=LNB, st=st: e.activation(
                    out=LNB[:], in_=X[:], func=AF.Identity,
                    bias=st["negmur"][:, 0:1], scale=st["rstd"][:, 0:1]),
                     reads=[X, st["negmur"], st["rstd"]], writes=[LNB])
                for kb in range(8):
                    tp = trb_ps[kb % 2]
                    p.op("tensor", lambda e, kb=kb, tp=tp, LNB=LNB: e.transpose(
                        out=tp[:], in_=LNB[:, kb * 128:(kb + 1) * 128], identity=identb[:]),
                         reads=[LNB, identb], writes=[tp])
                    p.op("scalar", lambda e, kb=kb, tp=tp, tt=tt: e.activation(
                        out=lnT[:, kb, 1 + tt * 128:1 + (tt + 1) * 128], in_=tp[:], func=AF.Copy),
                         reads=[tp], writes=[(lnT, (kb, tt))])
            for kb in range(8):
                p.op("vector", lambda e, kb=kb: e.tensor_scalar(
                    out=t1[:], in0=lnT[:, kb, 1:1 + CHUNK], scalar1=mur[:, kb:kb + 1],
                    scalar2=None, op0=ALU.mult),
                     reads=[lnT, mur], writes=[t1])
                p.op("vector", lambda e, kb=kb: e.scalar_tensor_tensor(
                    out=xrT[:, kb, :], in0=lnT[:, kb, 0:CHUNK], scalar=murm[:, kb:kb + 1],
                    in1=t1[:], op0=ALU.mult, op1=ALU.add),
                     reads=[lnT, murm, t1], writes=[(xrT, kb)])
                p.op("vector", lambda e, kb=kb: e.tensor_scalar(
                    out=t1[:], in0=lnT[:, kb, 1:1 + CHUNK], scalar1=muk[:, kb:kb + 1],
                    scalar2=None, op0=ALU.mult),
                     reads=[lnT, muk], writes=[t1])
                p.op("vector", lambda e, kb=kb: e.scalar_tensor_tensor(
                    out=xkT[:, kb, :], in0=lnT[:, kb, 0:CHUNK], scalar=mukm[:, kb:kb + 1],
                    in1=t1[:], op0=ALU.mult, op1=ALU.add),
                     reads=[lnT, mukm, t1], writes=[(xkT, kb)])
            for och in range(8):
                ps = mm_ps[och % 2]
                def fk(e, och=och, ps=ps):
                    last = None
                    for kb in range(8):
                        last = e.matmul(out=ps[:], lhsT=WT["k"][:, kb, och * 128:(och + 1) * 128],
                                        rhs=xkT[:, kb, :], start=(kb == 0), stop=(kb == 7))
                    return last
                p.op("tensor", fk, reads=[xkT, WT["k"]], writes=[ps])
                p.op("scalar", lambda e, ps=ps: e.activation(out=krel[:], in_=ps[:], func=AF.Relu),
                     reads=[ps], writes=[krel])
                p.op("scalar", lambda e, och=och: e.activation(out=kaT[:, och, :], in_=krel[:], func=AF.Square),
                     reads=[krel], writes=[(kaT, och)])
            for tt in range(CHUNK // 128):
                for oc in range(2):
                    ps = mm_ps[oc]
                    def fr(e, tt=tt, oc=oc, ps=ps):
                        last = None
                        for kb in range(8):
                            last = e.matmul(out=ps[:], lhsT=xrT[:, kb, tt * 128:(tt + 1) * 128],
                                            rhs=WT["r"][:, kb, oc * 512:(oc + 1) * 512],
                                            start=(kb == 0), stop=(kb == 7))
                        return last
                    p.op("tensor", fr, reads=[xrT, WT["r"]], writes=[ps])
                    p.op("scalar", lambda e, oc=oc, ps=ps: e.activation(
                        out=rS[:, oc * 512:(oc + 1) * 512], in_=ps[:], func=AF.Sigmoid),
                         reads=[ps], writes=[(rS, oc)])
                for oc in range(2):
                    ps = mm_ps[2 + oc]
                    def fv(e, tt=tt, oc=oc, ps=ps):
                        last = None
                        for kb in range(8):
                            last = e.matmul(out=ps[:], lhsT=kaT[:, kb, tt * 128:(tt + 1) * 128],
                                            rhs=WT["v"][:, kb, oc * 512:(oc + 1) * 512],
                                            start=(kb == 0), stop=(kb == 7))
                        return last
                    p.op("tensor", fv, reads=[kaT, WT["v"]], writes=[ps])
                    p.op("vector", lambda e, oc=oc, ps=ps: e.tensor_mul(
                        out=vo[:, oc * 512:(oc + 1) * 512],
                        in0=rS[:, oc * 512:(oc + 1) * 512], in1=ps[:]),
                         reads=[(rS, oc), ps], writes=[(vo, oc)])
                XO = xo_t[tt % 2]
                X2 = xc[tt % 2]
                p.dma("sync", X2[:], x_in[b, 1 + c0 + tt * 128:1 + c0 + (tt + 1) * 128, :], writes=[X2])
                p.op("vector", lambda e, XO=XO, X2=X2: e.scalar_tensor_tensor(
                    out=XO[:], in0=vo[:], scalar=0.5, in1=X2[:], op0=ALU.mult, op1=ALU.add),
                     reads=[vo, X2], writes=[XO])
                p.dma("gpsimd", x_out[b, c0 + tt * 128:c0 + (tt + 1) * 128, :], XO[:],
                      reads=[XO], sem=f"xst{tt % 2}")
    return


def build_l1ffn(tag="l1ffn"):
    p = Prog(tag)
    io = dict(
        x=p.dram("x", [B, TLOC + 1, D], F32, kind="ExternalInput").ap(),
        Wr=p.dram("Wr", [D, D], F32, kind="ExternalInput").ap(),
        Wk=p.dram("Wk", [D, D], F32, kind="ExternalInput").ap(),
        Wv=p.dram("Wv", [D, D], F32, kind="ExternalInput").ap(),
        mur=p.dram("mur", [128, 8], F32, kind="ExternalInput").ap(),
        muk=p.dram("muk", [128, 8], F32, kind="ExternalInput").ap(),
        ident_in=p.dram("ident_in", [128, 128], F32, kind="ExternalInput").ap(),
        xo=p.dram("xo", [B, TLOC, D], F32, kind="ExternalOutput").ap(),
    )
    with p.stage("lf"):
        emit_l1ffn(p, io)
    p.finalize()
    return p



def emit_l1af(p, io):
    H = 3
    NCH = TLOC // CHUNK
    x_in = io["x"]
    Wq_i = io["Wq"]; Wk_i = io["Wk"]; Wv_i = io["Wv"]
    bqkv_i = io["bqkv"]; cw_i = io["cw"]; cb_i = io["cb"]; hmask_i = io["hmask"]
    lncols_i = io["lncols"]; ident_in = io["ident_in"]
    g1_o = io["g1"]; qT_o = io["qT"]; kT_o = io["kT"]; vT_o = io["vT"]

    ident = p.sbuf("ident", [128, 128], F32)
    identb = p.sbuf("identb", [128, 128], BF16)
    p.dma("sync", ident[:], ident_in, writes=[ident])
    p.op("vector", lambda e: e.tensor_copy(out=identb[:], in_=ident[:]), reads=[ident], writes=[identb])
    bqkv = p.sbuf("bqkv_s", [128, 3, 8], F32)
    p.dma("sync", bqkv[:], bqkv_i, writes=[bqkv])
    lncols = p.sbuf("lncols_s", [128, 6, 8], F32)
    p.dma("sync", lncols[:], lncols_i, writes=[lncols])
    cbS = p.sbuf("cbS", [1, 1024], BF16)
    cbf = p.sbuf("cbf", [1, 1024], F32)
    p.dma("sync", cbf[:], cb_i, writes=[cbf])
    p.op("vector", lambda e: e.tensor_copy(out=cbS[:], in_=cbf[:]), reads=[cbf], writes=[cbS])
    ones_row = p.sbuf("ones_row", [1, 128], BF16)
    p.op("vector", lambda e: e.memset(ones_row[:], 1.0), writes=[ones_row])

    WT = {n: p.sbuf(f"WT{n}", [128, 8, D], BF16) for n in "qkv"}
    wld = p.sbuf("wld", [128, D], F32)
    tr_ps = p.psum("tr_ps", [128, 128], F32)
    for n, W in (("q", Wq_i), ("k", Wk_i), ("v", Wv_i)):
        for rb in range(8):
            p.dma("sync", wld[:], W[rb * 128:(rb + 1) * 128, :], writes=[wld])
            for kb in range(8):
                sl = wld[:, kb * 128:(kb + 1) * 128]
                p.op("tensor", lambda e, sl=sl: e.transpose(out=tr_ps[:], in_=sl, identity=ident[:]),
                     reads=[wld, ident], writes=[tr_ps])
                dsl = WT[n][:, kb, rb * 128:(rb + 1) * 128]
                p.op("scalar", lambda e, dsl=dsl: e.activation(out=dsl, in_=tr_ps[:], func=AF.Copy),
                     reads=[tr_ps], writes=[(WT[n], (kb, rb))])
    WcT = p.sbuf("WcT", [128, 4, 4, 2, 256], BF16)
    for g in range(4):
        for ob in range(2):
            for k_ in range(4):
                p.dma("sync", wld[:, 0:256], cw_i[k_, g * 256 + ob * 128:g * 256 + (ob + 1) * 128, :],
                      writes=[wld], sem="wld_c")
                for cb in range(2):
                    sl = wld[:, cb * 128:(cb + 1) * 128]
                    p.op("tensor", lambda e, sl=sl: e.transpose(out=tr_ps[:], in_=sl, identity=ident[:]),
                         reads=[wld, ident], writes=[tr_ps])
                    dsl = WcT[:, g, k_, cb, ob * 128:(ob + 1) * 128]
                    p.op("scalar", lambda e, dsl=dsl: e.activation(out=dsl, in_=tr_ps[:], func=AF.Copy),
                         reads=[tr_ps], writes=[(WcT, (g, k_, cb, ob))])

    hmask = p.sbuf("hmask_s", [128, 1], F32)
    p.dma("sync", hmask[:], hmask_i, writes=[hmask])
    hhalo = p.sbuf("hhalo", [128, 8, 4], BF16)
    xt = [p.sbuf(f"xt{i}", [128, D], F32) for i in range(2)]
    sq = [p.sbuf(f"sq{i}", [128, D], F32) for i in range(2)]
    lnb = [p.sbuf(f"lnb{i}", [128, D], BF16) for i in range(2)]
    stats = [make_stat(p, f"st{i}") for i in range(2)]
    trb_ps = [p.psum(f"trb_ps{i}", [128, 128], BF16) for i in range(2)]
    hcT = p.sbuf("hcT", [128, 8, CHUNK + H], BF16)
    haT = p.sbuf("haT", [128, 8, CHUNK], BF16)
    g1t = [p.sbuf(f"g1t{i}", [128, D], F32) for i in range(2)]
    qkv_ev = p.sbuf("qkv_ev", [128, CHUNK], BF16)
    cps = [p.psum(f"cps{i}", [128, 512], F32) for i in range(2)]
    mm_ps = [p.psum(f"mm_ps{i}", [128, 512], F32) for i in range(2)]

    g0buf = [p.sbuf(f"g0buf{i}", [128, D], F32) for i in range(2)]
    g0keep = p.sbuf("g0keep", [128, CHUNK // 128, D], F32)

    def ln_tr(b, tok0, ntok, col0, dst, scale_idx, bias_idx, x_from=None, par=0, keep_tt=None):
        X, SQ, LNB, st = xt[par], sq[par], lnb[par], stats[par]
        if x_from is None:
            p.dma("sync", X[0:ntok, :], x_in[b, tok0:tok0 + ntok, :], writes=[X])
            G0 = g0buf[par]
            ln_stats_n(p, X, SQ, st, ntok)
            p.op("scalar", lambda e, X=X, G0=G0, st=st, ntok=ntok: e.activation(
                out=G0[0:ntok, :], in_=X[0:ntok, :], func=AF.Identity,
                bias=st["negmur"][0:ntok, 0:1], scale=st["rstd"][0:ntok, 0:1]),
                 reads=[X, st["negmur"], st["rstd"]], writes=[G0])
            if keep_tt is not None:
                p.op("vector", lambda e, G0=G0, keep_tt=keep_tt: e.tensor_copy(
                    out=g0keep[0:ntok, keep_tt, :], in_=G0[0:ntok, :]),
                     reads=[G0], writes=[(g0keep, keep_tt)])
            src_t = G0
        else:
            src_t = x_from
        ln_stats_n(p, src_t, SQ, st, ntok)
        p.op("scalar", lambda e, src_t=src_t, LNB=LNB, st=st, ntok=ntok: e.activation(
            out=LNB[0:ntok, :], in_=src_t[0:ntok, :], func=AF.Identity,
            bias=st["negmur"][0:ntok, 0:1], scale=st["rstd"][0:ntok, 0:1]),
             reads=[src_t, st["negmur"], st["rstd"]], writes=[LNB])
        for kb in range(8):
            tp = trb_ps[kb % 2]
            idb = identb[0:ntok, 0:ntok]
            p.op("tensor", lambda e, kb=kb, tp=tp, LNB=LNB, ntok=ntok, idb=idb: e.transpose(
                out=tp[:, 0:ntok], in_=LNB[0:ntok, kb * 128:(kb + 1) * 128], identity=idb),
                 reads=[LNB, identb], writes=[tp])
            dsl = dst[:, kb, col0:col0 + ntok]
            p.op("scalar", lambda e, dsl=dsl, tp=tp, kb=kb, ntok=ntok: e.activation(
                out=dsl, in_=tp[:, 0:ntok], func=AF.Identity,
                scale=lncols[:, scale_idx, kb:kb + 1], bias=lncols[:, bias_idx, kb:kb + 1]),
                 reads=[tp, lncols], writes=[(dst, (kb, col0))])

    for b in range(B):
        for ci in range(NCH):
            c0 = ci * CHUNK
            if ci == 0:
                ln_tr(b, 0, H, 0, hhalo, 0, 1)
                p.op("vector", lambda e: e.tensor_scalar(out=hcT[:, :, 0:H], in0=hhalo[:, :, 0:H],
                                                         scalar1=hmask[:, 0:1], scalar2=None, op0=ALU.mult),
                     reads=[hhalo, hmask], writes=[hcT])
            else:
                p.op("vector", lambda e: e.tensor_copy(out=hcT[:, :, 0:H], in_=hcT[:, :, CHUNK:CHUNK + H]),
                     reads=[hcT], writes=[hcT])
            for tt in range(CHUNK // 128):
                ln_tr(b, H + c0 + tt * 128, 128, H + tt * 128, hcT, 0, 1, par=tt % 2, keep_tt=tt)
            for tt in range(CHUNK // 128):
                for g in range(4):
                    ps = cps[g // 2]
                    colr = (g % 2) * 256
                    def fc(e, g=g, tt=tt, ps=ps, colr=colr):
                        e.matmul(out=ps[:, colr:colr + 256], lhsT=ones_row[:],
                                 rhs=cbS[:, g * 256:(g + 1) * 256], start=True, stop=False)
                        last = None
                        for k_ in range(4):
                            for cb in range(2):
                                last = e.matmul(out=ps[:, colr:colr + 256],
                                                lhsT=hcT[:, g * 2 + cb, tt * 128 + k_:tt * 128 + k_ + 128],
                                                rhs=WcT[:, g, k_, cb, :],
                                                start=False, stop=(k_ == 3 and cb == 1))
                        return last
                    p.op("tensor", fc, reads=[hcT, WcT, ones_row, cbS], writes=[(cps[g // 2], g % 2)])
                G1 = g1t[tt % 2]
                for half in range(2):
                    p.op("vector", lambda e, half=half, G1=G1, tt=tt: e.tensor_add(
                        out=G1[:, half * 512:(half + 1) * 512], in0=cps[half][:],
                        in1=g0keep[:, tt, half * 512:(half + 1) * 512]),
                         reads=[(cps[half], 0), (cps[half], 1), (g0keep, tt)], writes=[(G1, half)])
                p.dma("gpsimd", g1_o[b, c0 + tt * 128:c0 + (tt + 1) * 128, :], G1[:],
                      reads=[G1], sem=f"g1st{tt % 2}")
                ln_tr(b, 0, 128, tt * 128, haT, 2, 3, x_from=G1, par=tt % 2)
            for ni, n in enumerate("qkv"):
                dramo = {"q": qT_o, "k": kT_o, "v": vT_o}[n]
                for och in range(8):
                    ps = mm_ps[och % 2]
                    def fq(e, n=n, och=och, ps=ps):
                        last = None
                        for kb in range(8):
                            last = e.matmul(out=ps[:], lhsT=WT[n][:, kb, och * 128:(och + 1) * 128],
                                            rhs=haT[:, kb, :], start=(kb == 0), stop=(kb == 7))
                        return last
                    p.op("tensor", fq, reads=[haT, WT[n]], writes=[ps])
                    p.op("scalar", lambda e, ni=ni, och=och, ps=ps: e.activation(
                        out=qkv_ev[:], in_=ps[:], func=AF.Identity, bias=bqkv[:, ni, och:och + 1]),
                         reads=[ps, bqkv], writes=[qkv_ev])
                    p.dma("gpsimd", dramo[b, och * 128:(och + 1) * 128, c0:c0 + CHUNK], qkv_ev[:],
                          reads=[qkv_ev], sem="qkvst")
    return


def _l1af_io(p):
    return dict(
        x=p.dram("x", [B, TLOC + 3, D], F32, kind="ExternalInput").ap(),
        Wq=p.dram("Wq", [D, D], F32, kind="ExternalInput").ap(),
        Wk=p.dram("Wk", [D, D], F32, kind="ExternalInput").ap(),
        Wv=p.dram("Wv", [D, D], F32, kind="ExternalInput").ap(),
        bqkv=p.dram("bqkv", [128, 3, 8], F32, kind="ExternalInput").ap(),
        cw=p.dram("cw", [4, 1024, 256], F32, kind="ExternalInput").ap(),
        cb=p.dram("cb", [1, 1024], F32, kind="ExternalInput").ap(),
        hmask=p.dram("hmask", [128, 1], F32, kind="ExternalInput").ap(),
        lncols=p.dram("lncols", [128, 6, 8], F32, kind="ExternalInput").ap(),
        ident_in=p.dram("ident_in", [128, 128], F32, kind="ExternalInput").ap(),
        g1=p.dram("g1", [B, TLOC, D], F32, kind="ExternalOutput").ap(),
        qT=p.dram("qT", [B, D, TLOC], BF16, kind="ExternalOutput").ap(),
        kT=p.dram("kT", [B, D, TLOC], BF16, kind="ExternalOutput").ap(),
        vT=p.dram("vT", [B, D, TLOC], BF16, kind="ExternalOutput").ap(),
    )


def build_l1attn_front(tag="l1af"):
    p = Prog(tag)
    io = _l1af_io(p)
    with p.stage("la"):
        emit_l1af(p, io)
    p.finalize()
    return p



def emit_pooled(p, io, T=S):
    PCH = 2048
    NCHP = T // PCH
    qT_i = io["qT"]; kT_i = io["kT"]; vT_i = io["vT"]; rs_i = io["rs"]; pkvT_o = io["pkvT"]

    ones = p.sbuf("ones", [128, PCH], F32)
    p.op("vector", lambda e: e.memset(ones[:], 1.0), writes=[ones])
    rsb = p.sbuf("rsb", [128, PCH], F32)
    rsl = p.sbuf("rsl", [1, PCH], BF16)
    rslf = p.sbuf("rslf", [1, PCH], F32)
    one_col = p.sbuf("one_col", [1, 128], BF16)
    p.op("vector", lambda e: e.memset(one_col[:], 1.0), writes=[one_col])
    rs_ps = p.psum("rs_ps", [128, 512], F32)
    qt = p.sbuf("qt", [128, PCH], BF16)
    kt = p.sbuf("kt", [128, PCH], BF16)
    vt = p.sbuf("vt", [128, PCH], BF16)
    cum = p.sbuf("cum", [128, PCH], F32)
    pq = p.sbuf("pq", [128, PCH], F32)
    pqk = p.sbuf("pqk", [128, PCH], F32)
    cum2 = p.sbuf("cum2", [128, PCH], F32)
    pk = p.sbuf("pk", [128, PCH], F32)
    pkv = p.sbuf("pkv", [128, PCH], BF16)
    qcar = p.sbuf("qcar", [128, 1], F32)
    kcar = p.sbuf("kcar", [128, 1], F32)

    for b in range(B):
        p.op("vector", lambda e: e.memset(qcar[:], 0.0), writes=[qcar])
        p.op("vector", lambda e: e.memset(kcar[:], 0.0), writes=[kcar])
        for ci in range(NCHP):
            c0 = ci * PCH
            p.dma("sync", qt[:], qT_i[b, :, c0:c0 + PCH], writes=[qt])
            p.dma("sync", kt[:], kT_i[b, :, c0:c0 + PCH], writes=[kt])
            p.dma("sync", vt[:], vT_i[b, :, c0:c0 + PCH], writes=[vt])
            rbview = bass.AP(rs_i.tensor, rs_i.offset + c0, [[0, 128], [1, PCH]])
            p.dma("sync", rsb[:], rbview, writes=[rsb])
            p.op("vector", lambda e: e.tensor_tensor_scan(out=cum[:], data0=ones[:], data1=qt[:],
                                                          initial=qcar[:, 0:1], op0=ALU.mult, op1=ALU.add),
                 reads=[ones, qt, qcar], writes=[cum])
            p.op("vector", lambda e: e.tensor_copy(out=qcar[:], in_=cum[:, PCH - 1:PCH]),
                 reads=[cum], writes=[qcar])
            p.op("vector", lambda e: e.tensor_mul(out=pq[:], in0=cum[:], in1=rsb[:]),
                 reads=[cum, rsb], writes=[pq])
            p.op("vector", lambda e: e.tensor_mul(out=pqk[:], in0=pq[:], in1=kt[:]),
                 reads=[pq, kt], writes=[pqk])
            p.op("vector", lambda e: e.tensor_tensor_scan(out=cum2[:], data0=ones[:], data1=pqk[:],
                                                          initial=kcar[:, 0:1], op0=ALU.mult, op1=ALU.add),
                 reads=[ones, pqk, kcar], writes=[cum2])
            p.op("vector", lambda e: e.tensor_copy(out=kcar[:], in_=cum2[:, PCH - 1:PCH]),
                 reads=[cum2], writes=[kcar])
            p.op("vector", lambda e: e.tensor_mul(out=pk[:], in0=cum2[:], in1=rsb[:]),
                 reads=[cum2, rsb], writes=[pk])
            p.op("vector", lambda e: e.tensor_mul(out=pkv[:], in0=pk[:], in1=vt[:]),
                 reads=[pk, vt], writes=[pkv])
            p.dma("gpsimd", pkvT_o[b, :, c0:c0 + PCH], pkv[:], reads=[pkv], sem="pkvst")
    return


def build_pooled(T=S, tag="pooled"):
    p = Prog(tag)
    io = dict(
        qT=p.dram("qT", [B, 128, T], BF16, kind="ExternalInput").ap(),
        kT=p.dram("kT", [B, 128, T], BF16, kind="ExternalInput").ap(),
        vT=p.dram("vT", [B, 128, T], BF16, kind="ExternalInput").ap(),
        rs=p.dram("rs", [1, T], F32, kind="ExternalInput").ap(),
        pkvT=p.dram("pkvT", [B, 128, T], BF16, kind="ExternalOutput").ap(),
    )
    with p.stage("po"):
        emit_pooled(p, io, T=T)
    p.finalize()
    return p



def emit_l1ab(p, io):
    CHI = 256
    NCH = TLOC // CHI
    x_in = io["x"]; g1_i = io["g1"]; pkvT_i = io["pkvT"]
    Wb_i = io["Wb"]; Wu_i = io["Wu"]; bb_i = io["bb"]; bu_i = io["bu"]
    lncols_i = io["lncols"]; ident_in = io["ident_in"]; x_out = io["xo"]

    ident = p.sbuf("ident", [128, 128], F32)
    identb = p.sbuf("identb", [128, 128], BF16)
    p.dma("sync", ident[:], ident_in, writes=[ident])
    p.op("vector", lambda e: e.tensor_copy(out=identb[:], in_=ident[:]), reads=[ident], writes=[identb])
    bbcol = p.sbuf("bbcol", [128, 32], F32)
    p.dma("sync", bbcol[:], bb_i, writes=[bbcol])
    lncols = p.sbuf("lncols_s", [128, 2, 8], F32)
    p.dma("sync", lncols[:], lncols_i, writes=[lncols])
    buS = p.sbuf("buS", [1, D], BF16)
    buf_ = p.sbuf("buf_", [1, D], F32)
    p.dma("sync", buf_[:], bu_i, writes=[buf_])
    p.op("vector", lambda e: e.tensor_copy(out=buS[:], in_=buf_[:]), reads=[buf_], writes=[buS])
    ones_row = p.sbuf("ones_row", [1, 128], BF16)
    p.op("vector", lambda e: e.memset(ones_row[:], 1.0), writes=[ones_row])

    WbT = p.sbuf("WbT", [128, 8, 4096], BF16)
    WuT = p.sbuf("WuT", [128, 32, D], BF16)
    wld = p.sbuf("wld", [128, 1024], F32)
    tr_ps = p.psum("tr_ps", [128, 128], F32)
    for rb in range(32):
        p.dma("sync", wld[:], Wb_i[rb * 128:(rb + 1) * 128, :], writes=[wld])
        for kb in range(8):
            sl = wld[:, kb * 128:(kb + 1) * 128]
            p.op("tensor", lambda e, sl=sl: e.transpose(out=tr_ps[:], in_=sl, identity=ident[:]),
                 reads=[wld, ident], writes=[tr_ps])
            dsl = WbT[:, kb, rb * 128:(rb + 1) * 128]
            p.op("scalar", lambda e, dsl=dsl: e.activation(out=dsl, in_=tr_ps[:], func=AF.Copy),
                 reads=[tr_ps], writes=[(WbT, (kb, rb))])
    for rb in range(8):
        for qc in range(4):
            p.dma("sync", wld[:], Wu_i[rb * 128:(rb + 1) * 128, qc * 1024:(qc + 1) * 1024], writes=[wld])
            for kq in range(8):
                kb = qc * 8 + kq
                sl = wld[:, kq * 128:(kq + 1) * 128]
                p.op("tensor", lambda e, sl=sl: e.transpose(out=tr_ps[:], in_=sl, identity=ident[:]),
                     reads=[wld, ident], writes=[tr_ps])
                dsl = WuT[:, kb, rb * 128:(rb + 1) * 128]
                p.op("scalar", lambda e, dsl=dsl: e.activation(out=dsl, in_=tr_ps[:], func=AF.Copy),
                     reads=[tr_ps], writes=[(WuT, (kb, rb))])

    xt = [p.sbuf(f"xt{i}", [128, D], F32) for i in range(2)]
    sq = [p.sbuf(f"sq{i}", [128, D], F32) for i in range(2)]
    lnb = [p.sbuf(f"lnb{i}", [128, D], BF16) for i in range(2)]
    stats = [make_stat(p, f"st{i}") for i in range(2)]
    trb_ps = [p.psum(f"trb_ps{i}", [128, 128], BF16) for i in range(2)]
    pkvt = p.sbuf("pkvt", [128, 128], BF16)
    g2t = [p.sbuf(f"g2t{i}", [128, D], F32) for i in range(2)]
    g2keep = p.sbuf("g2keep", [128, CHI // 128, D], F32)
    hbT = p.sbuf("hbT", [128, 8, CHI], BF16)
    bhT = p.sbuf("bhT", [128, 32, CHI], BF16)
    mm_ps = [p.psum(f"mm_ps{i}", [128, 512], F32) for i in range(4)]
    xo_t = [p.sbuf("xo_t0", [128, D], F32)] * 2

    def ln_tr_aff(src_t, col0, dst, par):
        SQ, LNB, st = sq[par], lnb[par], stats[par]
        ln_stats_n(p, src_t, SQ, st, 128)
        p.op("scalar", lambda e, src_t=src_t, LNB=LNB, st=st: e.activation(
            out=LNB[:], in_=src_t[:], func=AF.Identity,
            bias=st["negmur"][:, 0:1], scale=st["rstd"][:, 0:1]),
             reads=[src_t, st["negmur"], st["rstd"]], writes=[LNB])
        for kb in range(8):
            tp = trb_ps[kb % 2]
            p.op("tensor", lambda e, kb=kb, tp=tp, LNB=LNB: e.transpose(
                out=tp[:], in_=LNB[:, kb * 128:(kb + 1) * 128], identity=identb[:]),
                 reads=[LNB, identb], writes=[tp])
            dsl = dst[:, kb, col0:col0 + 128]
            p.op("scalar", lambda e, dsl=dsl, tp=tp, kb=kb: e.activation(
                out=dsl, in_=tp[:], func=AF.Identity,
                scale=lncols[:, 0, kb:kb + 1], bias=lncols[:, 1, kb:kb + 1]),
                 reads=[tp, lncols], writes=[(dst, (kb, col0))])

    for b in range(B):
        for ci in range(NCH):
            c0 = ci * CHI
            for tt in range(CHI // 128):
                par = tt % 2
                G2 = g2t[par]
                X = xt[par]
                p.dma("sync", X[:], g1_i[b, c0 + tt * 128:c0 + (tt + 1) * 128, :], writes=[X])
                for kb in range(8):
                    p.dma("sync", pkvt[:], pkvT_i[b, kb * 128:(kb + 1) * 128,
                                                         c0 + tt * 128:c0 + (tt + 1) * 128],
                          writes=[pkvt])
                    tp = trb_ps[kb % 2]
                    p.op("tensor", lambda e, tp=tp: e.transpose(out=tp[:], in_=pkvt[:],
                                                                identity=identb[:]),
                         reads=[pkvt, identb], writes=[tp])
                    p.op("vector", lambda e, kb=kb, tp=tp, G2=G2, X=X: e.tensor_add(
                        out=G2[:, kb * 128:(kb + 1) * 128], in0=tp[:], in1=X[:, kb * 128:(kb + 1) * 128]),
                         reads=[tp, X], writes=[(G2, kb)])
                p.op("vector", lambda e, tt=tt, G2=G2: e.tensor_copy(out=g2keep[:, tt, :], in_=G2[:]),
                     reads=[G2], writes=[(g2keep, tt)])
                ln_tr_aff(G2, tt * 128, hbT, par)
            for och in range(32):
                ps = mm_ps[och % 2]
                def fb1(e, och=och, ps=ps):
                    last = None
                    for kb in range(8):
                        last = e.matmul(out=ps[:, 0:CHI], lhsT=WbT[:, kb, och * 128:(och + 1) * 128],
                                        rhs=hbT[:, kb, :], start=(kb == 0), stop=(kb == 7))
                    return last
                p.op("tensor", fb1, reads=[hbT, WbT], writes=[ps])
                p.op("scalar", lambda e, och=och, ps=ps: e.activation(
                    out=bhT[:, och, :], in_=ps[:, 0:CHI], func=AF.Gelu, bias=bbcol[:, och:och + 1]),
                     reads=[ps, bbcol], writes=[(bhT, och)])
            for tt in range(CHI // 128):
                XO = xo_t[tt % 2]
                X4 = xt[tt % 2]
                p.dma("sync", X4[:], x_in[b, c0 + tt * 128:c0 + (tt + 1) * 128, :], writes=[X4],
                      sem=f"x4ld{tt % 2}")
                for oc in range(2):
                    ps = mm_ps[2 + oc]
                    def fo(e, tt=tt, oc=oc, ps=ps):
                        e.matmul(out=ps[:], lhsT=ones_row[:], rhs=buS[:, oc * 512:(oc + 1) * 512],
                                 start=True, stop=False)
                        last = None
                        for kb in range(32):
                            last = e.matmul(out=ps[:], lhsT=bhT[:, kb, tt * 128:(tt + 1) * 128],
                                            rhs=WuT[:, kb, oc * 512:(oc + 1) * 512],
                                            start=False, stop=(kb == 31))
                        return last
                    p.op("tensor", fo, reads=[bhT, WuT, ones_row, buS], writes=[ps])
                    p.op("vector", lambda e, oc=oc, ps=ps, tt=tt, XO=XO: e.tensor_add(
                        out=XO[:, oc * 512:(oc + 1) * 512], in0=ps[:],
                        in1=g2keep[:, tt, oc * 512:(oc + 1) * 512]),
                         reads=[ps, (g2keep, tt)], writes=[(XO, oc)])
                p.op("vector", lambda e, XO=XO, X4=X4: e.tensor_add(out=XO[:], in0=XO[:], in1=X4[:]),
                     reads=[XO, X4], writes=[XO])
                p.dma("gpsimd", x_out[b, c0 + tt * 128:c0 + (tt + 1) * 128, :], XO[:],
                      reads=[XO], sem=f"xst{tt % 2}")
    return


def build_l1attn_back(tag="l1ab"):
    p = Prog(tag)
    io = dict(
        x=p.dram("x", [B, TLOC, D], F32, kind="ExternalInput").ap(),
        g1=p.dram("g1", [B, TLOC, D], F32, kind="ExternalInput").ap(),
        pkvT=p.dram("pkvT", [B, D, TLOC], BF16, kind="ExternalInput").ap(),
        Wb=p.dram("Wb", [4096, D], F32, kind="ExternalInput").ap(),
        Wu=p.dram("Wu", [D, 4096], F32, kind="ExternalInput").ap(),
        bb=p.dram("bb", [128, 32], F32, kind="ExternalInput").ap(),
        bu=p.dram("bu", [1, D], F32, kind="ExternalInput").ap(),
        lncols=p.dram("lncols", [128, 2, 8], F32, kind="ExternalInput").ap(),
        ident_in=p.dram("ident_in", [128, 128], F32, kind="ExternalInput").ap(),
        xo=p.dram("xo", [B, TLOC, D], F32, kind="ExternalOutput").ap(),
    )
    with p.stage("lb"):
        emit_l1ab(p, io)
    p.finalize()
    return p



def emit_halo(p, tag, body3, dst, nL, nR, selL_ap, selR_ap, ident, q="gpsimd"):
    contrib = p.dram(tag + "_hc", [12, D], F32).ap()
    hag = p.dram(tag + "_hag", [96, D], F32).ap()
    for b in range(B):
        f3, l3 = body3(b)
        p.dma(q, contrib[b * 6:b * 6 + 3, :], f3, sem=tag + "h0")
        p.dma(q, contrib[b * 6 + 3:b * 6 + 6, :], l3, sem=tag + "h0")
    p.coll("AllGather", ALU.bypass, [contrib], [hag],
           reads=[(contrib, None)], writes=[(hag, None)])
    hs = p.sbuf(tag + "_hs", [96, D], F32)
    hb = p.sbuf(tag + "_hb", [96, D], BF16)
    sL = p.sbuf(tag + "_sL", [96, 6], BF16)
    sR = p.sbuf(tag + "_sR", [96, 6], BF16)
    hps = p.psum(tag + "_hps", [8, 512], F32)
    hev = p.sbuf(tag + "_hev", [3, D], F32)
    p.dma("sync", hs[:], hag, reads=[(hag, None)], writes=[hs])
    p.op("vector", lambda e: e.tensor_copy(out=hb[:], in_=hs[:]), reads=[hs], writes=[hb])
    p.dma("sync", sL[:], selL_ap, writes=[sL])
    p.dma("sync", sR[:], selR_ap, writes=[sR])
    for b in range(B):
        for side, n_, sel in (("L", nL, sL), ("R", nR, sR)):
            if n_ == 0:
                continue
            for oc in range(2):
                p.op("tensor", lambda e, b=b, oc=oc, sel=sel: e.matmul(
                    out=hps[0:3, 0:512], lhsT=sel[:, b * 3:(b + 1) * 3],
                    rhs=hb[:, oc * 512:(oc + 1) * 512], start=True, stop=True),
                     reads=[sel, hb], writes=[hps])
                p.op("scalar", lambda e, oc=oc: e.activation(
                    out=hev[:, oc * 512:(oc + 1) * 512], in_=hps[0:3, 0:512], func=AF.Copy),
                     reads=[hps], writes=[(hev, oc)])
            d = dst(b, side)
            p.dma(q, d, hev[3 - n_:3, :] if side == "L" else hev[0:n_, :],
                  reads=[hev], sem=tag + "hw")


def emit_scan_fused(p, io, regs):
    NCHS = S // SCH
    bc_ag = {d_: io[f"bc_ag_{d_}"] for d_ in "fb"}
    dmy = io["dmy"]
    alog_i = io["alog2"]
    sel_i = io["selbc"]
    y_o2 = io["ylocal2"]

    SELB = p.sbuf("SELB", [64, 32, 128], BF16)
    p.dma("sync", SELB[:], sel_i, writes=[SELB])
    acols = p.sbuf("acols", [128, 2, 4, 16], F32)
    atmp = p.sbuf("atmp", [128, 16], F32)
    atmp2 = p.sbuf("atmp2", [128, 16], F32)
    for di in range(2):
        for db in range(4):
            p.dma("sync", atmp[:], alog_i[di, db], writes=[atmp])
            p.op("scalar", lambda e: e.activation(out=atmp2[:], in_=atmp[:], func=AF.Exp),
                 reads=[atmp], writes=[atmp2])
            p.op("vector", lambda e, di=di, db=db: e.tensor_scalar(
                out=acols[:, di, db, :], in0=atmp2[:], scalar1=-1.0, scalar2=None, op0=ALU.mult),
                 reads=[atmp2], writes=[(acols, (di, db))])

    Bb = p.sbuf("Bb", [128, 16, SCH], BF16)
    Cb = p.sbuf("Cb", [128, 16, SCH], BF16)
    bcc = p.sbuf("bcc_s", [64, SCH], BF16)
    bccr = p.sbuf("bccr", [64, SCH], BF16)
    bc_ps = [p.psum(f"bc_ps{i}", [128, SCH], F32) for i in range(2)]
    dA = [p.sbuf(f"dA{i}", [128, 16, SCH], F32) for i in range(2)]
    dBu = [p.sbuf(f"dBu{i}", [128, 16, SCH], BF16) for i in range(2)]
    hb = p.sbuf("hb", [128, 16, SCH], BF16)
    tb = p.sbuf("tb", [128, 4, SCH], BF16)
    tc = p.sbuf("tc", [128, 2, SCH], BF16)
    yv = p.sbuf("yv", [128, SCH], BF16)
    dTt = [p.sbuf(f"dTt{i}", [128, SCH], BF16) for i in range(2)]
    duTt = [p.sbuf(f"duTt{i}", [128, SCH], BF16) for i in range(2)]
    carry = p.sbuf("carry", [128, 4, 16], F32)
    rloc = p.dram("rloc", [2, 64, S], BF16).ap()
    rtile = [p.sbuf(f"rtile{i}", [64, SCH], BF16) for i in range(2)]
    dtwf = p.sbuf("dtwf", [64, 2, 512], F32)
    dtwS = p.sbuf("dtwS", [64, 2, 512], BF16)
    dtbS = p.sbuf("dtbS", [128, 2, 4], F32)
    spZ = p.sbuf("spZ", [128, SCH], F32)
    spA = p.sbuf("spA", [128, SCH], F32)
    spB = p.sbuf("spB", [128, SCH], F32)
    dt_ps = [p.psum(f"dt_ps{i}", [128, SCH], F32) for i in range(2)]
    p.dma("sync", dtwf[:], io["dtw2"], writes=[dtwf])
    p.dma("sync", dtbS[:], io["dtb2"], writes=[dtbS])
    for di_ in range(2):
        p.op("vector", lambda e, di_=di_: e.tensor_copy(out=dtwS[:, di_, :], in_=dtwf[:, di_, :]),
             reads=[dtwf], writes=[(dtwS, di_)])

    def rev(ap_, n):
        return bass.AP(ap_.tensor, ap_.offset + n - 1, [[ap_.ap[0][0], ap_.ap[0][1]], [-1, n]])

    for di, d_ in enumerate("fb"):
        fwd = (d_ == "f")
        p.op("vector", lambda e: e.memset(carry[:], 0.0), writes=[carry])
        agr = io[f"r_ag_{d_}"]
        def fr(e, agr=agr, di=di):
            e.reg_load(regs["ry"], regs["offtab"][0:1, 40:41])
            return e.dma_start(out=rloc[di], in_=bass.AP(
                agr.tensor, regs["ry"], [[TLOC, 64], [128 * TLOC, 8], [1, TLOC]]))
        p.op("sync", fr, reads=[(agr, None), (regs["offtab"], None)], writes=[(rloc, di)],
             dma=True, dma_sem="rgl")
        for ci in range(NCHS):
            c0 = ci * SCH if fwd else (NCHS - 1 - ci) * SCH
            r = c0 // TLOC
            col0 = c0 % TLOC
            p.dma("sync", bcc[:], bc_ag[d_][r * 64:(r + 1) * 64, col0:col0 + SCH],
                  reads=[(bc_ag[d_], None)], writes=[bcc])
            rt = rtile[ci % 2]
            p.dma("sync", rt[:], rloc[di, :, c0:c0 + SCH],
                  reads=[(rloc, di)], writes=[rt], sem=f"rtl{ci % 2}")
            if fwd:
                bcu = bcc
            else:
                p.op("vector", lambda e: e.tensor_copy(out=bccr[:], in_=rev(bcc[:], SCH)),
                     reads=[bcc], writes=[bccr])
                bcu = bccr
            for s in range(16):
                ps = bc_ps[s % 2]
                p.op("tensor", lambda e, s=s, ps=ps, bcu=bcu: e.matmul(
                    out=ps[:], lhsT=SELB[:, s, :], rhs=bcu[:], start=True, stop=True),
                     reads=[SELB, bcu], writes=[ps])
                p.op("scalar", lambda e, s=s, ps=ps: e.activation(out=Bb[:, s, :], in_=ps[:], func=AF.Copy),
                     reads=[ps], writes=[(Bb, s)])
                ps2 = bc_ps[(s + 1) % 2]
                p.op("tensor", lambda e, s=s, ps2=ps2, bcu=bcu: e.matmul(
                    out=ps2[:], lhsT=SELB[:, 16 + s, :], rhs=bcu[:], start=True, stop=True),
                     reads=[SELB, bcu], writes=[ps2])
                p.op("scalar", lambda e, s=s, ps2=ps2: e.activation(out=Cb[:, s, :], in_=ps2[:], func=AF.Copy),
                     reads=[ps2], writes=[(Cb, s)])
            for db in range(4):
                par = db % 2
                dt_, du_ = dTt[par], duTt[par]
                dps = dt_ps[par]
                p.op("tensor", lambda e, di=di, db=db, rt=rt, dps=dps: e.matmul(
                    out=dps[:], lhsT=dtwS[:, di, db * 128:(db + 1) * 128],
                    rhs=rt[:], start=True, stop=True),
                     reads=[dtwS, rt], writes=[dps])
                p.op("scalar", lambda e, di=di, db=db, dps=dps: e.activation(
                    out=spZ[:], in_=dps[:], func=AF.Identity, bias=dtbS[:, di, db:db + 1]),
                     reads=[dps, dtbS], writes=[spZ])
                p.op("scalar", lambda e: e.activation(out=spA[:], in_=spZ[:], func=AF.Abs),
                     reads=[spZ], writes=[spA])
                p.op("scalar", lambda e: e.activation(out=spB[:], in_=spA[:], func=AF.Exp, scale=-1.0),
                     reads=[spA], writes=[spB])
                p.op("scalar", lambda e: e.activation(out=spA[:], in_=spB[:], func=AF.Ln, bias=1.0),
                     reads=[spB], writes=[spA])
                p.op("vector", lambda e: e.tensor_scalar(out=spB[:], in0=spZ[:], scalar1=0.0,
                                                         scalar2=None, op0=ALU.max),
                     reads=[spZ], writes=[spB])
                p.op("vector", lambda e, dt_=dt_: e.tensor_add(out=dt_[:], in0=spB[:], in1=spA[:]),
                     reads=[spB, spA], writes=[dt_])
                p.dma("sync", du_[:], dmy[1, di, db * 128:(db + 1) * 128, c0:c0 + SCH],
                      reads=[(dmy, None)], writes=[du_], sem=f"dul{par}")
                DA, DB = dA[par], dBu[par]
                dsrc = dt_[:] if fwd else rev(dt_[:], SCH)
                usrc = du_[:] if fwd else rev(du_[:], SCH)
                for s in range(16):
                    p.op("scalar", lambda e, s=s, di=di, db=db, DA=DA, dsrc=dsrc: e.activation(
                        out=DA[:, s, :], in_=dsrc, func=AF.Exp, scale=acols[:, di, db, s:s + 1]),
                         reads=[dt_, (acols, (di, db))], writes=[(DA, s)])
                    p.op("vector", lambda e, s=s, DB=DB, usrc=usrc: e.tensor_mul(
                        out=DB[:, s, :], in0=usrc, in1=Bb[:, s, :]),
                         reads=[du_, (Bb, s)], writes=[(DB, s)])
                for s in range(16):
                    p.op("vector", lambda e, s=s, db=db, DA=DA, DB=DB: e.tensor_tensor_scan(
                        out=hb[:, s, :], data0=DA[:, s, :], data1=DB[:, s, :],
                        initial=carry[:, db, s:s + 1], op0=ALU.mult, op1=ALU.add),
                         reads=[(DA, s), (DB, s), (carry, db)], writes=[(hb, s)])
                p.op("vector", lambda e, db=db: e.tensor_copy(
                    out=carry[:, db, :], in_=bass.AP(hb[:].tensor, hb[:].offset + SCH - 1,
                                                     [[hb[:].ap[0][0], 128], [SCH, 16]])),
                     reads=[hb], writes=[(carry, db)])
                p.op("vector", lambda e, DB=DB: e.tensor_mul(out=DB[:], in0=hb[:], in1=Cb[:]),
                     reads=[hb, Cb], writes=[DB])
                p.op("vector", lambda e, DB=DB: e.tensor_add(out=hb[:, 0:8, :], in0=DB[:, 0:8, :], in1=DB[:, 8:16, :]),
                     reads=[DB], writes=[hb])
                p.op("vector", lambda e: e.tensor_add(out=tb[:], in0=hb[:, 0:4, :], in1=hb[:, 4:8, :]),
                     reads=[hb], writes=[tb])
                p.op("vector", lambda e: e.tensor_add(out=tc[:], in0=tb[:, 0:2, :], in1=tb[:, 2:4, :]),
                     reads=[tb], writes=[tc])
                yout = yv[:] if fwd else rev(yv[:], SCH)
                p.op("vector", lambda e, yout=yout: e.tensor_add(out=yout, in0=tc[:, 0, :], in1=tc[:, 1, :]),
                     reads=[tc], writes=[yv])
                p.dma("gpsimd", y_o2[di][db * 128:(db + 1) * 128, c0:c0 + SCH], yv[:],
                      reads=[yv], sem=f"yst{par}")
        if io.get("_ycoll") is not None:
            io["_ycoll"](di)
    return


def emit_pooled_fused(p, io):
    qT_i, kT_i, vT_i = io["qTl"], io["kTl"], io["vTl"]
    rs_i = io["rss"]
    mask_i = io["maskprev"]
    pkv_o = io["pkvTl"]

    ones = p.sbuf("ones", [128, TLOC], F32)
    p.op("vector", lambda e: e.memset(ones[:], 1.0), writes=[ones])
    rsb = p.sbuf("rsb", [128, TLOC], F32)
    rbv = bass.AP(rs_i.tensor, rs_i.offset, [[0, 128], [1, TLOC]])
    p.dma("sync", rsb[:], rbv, writes=[rsb])
    maskc = p.sbuf("maskc", [16, 8], F32)
    p.dma("sync", maskc[:], mask_i, writes=[maskc])
    ident = p.sbuf("po_id", [128, 128], F32)
    p.dma("sync", ident[:], io["ident_in"], writes=[ident])

    qt = [p.sbuf(f"qt{i}", [128, TLOC], BF16) for i in range(2)]
    kt = [p.sbuf(f"kt{i}", [128, TLOC], BF16) for i in range(2)]
    vt = p.sbuf("vt", [128, TLOC], BF16)
    cq = p.sbuf("cq", [128, TLOC], F32)
    krs = p.sbuf("krs", [128, TLOC], F32)
    KS = p.sbuf("KS", [128, TLOC], F32)
    t2 = p.sbuf("t2", [128, TLOC], F32)
    LS = p.sbuf("LS", [128, TLOC], F32)
    ends = p.sbuf("ends", [128, 16, 3], F32)

    def lastcol(t):
        return bass.AP(t[:].tensor, t[:].offset + TLOC - 1, [[t[:].ap[0][0], 128], [1, 1]])

    def tile_scans(i, b, db, store_ends):
        Q, Kt = qt[i % 2], kt[i % 2]
        p.dma("sync", Q[:], qT_i[b, db * 128:(db + 1) * 128, :], writes=[Q], sem=f"qtl{i % 2}")
        p.dma("sync", Kt[:], kT_i[b, db * 128:(db + 1) * 128, :], writes=[Kt], sem=f"ktl{i % 2}")
        p.op("vector", lambda e, Q=Q: e.tensor_tensor_scan(
            out=cq[:], data0=ones[:], data1=Q[:], initial=0.0, op0=ALU.mult, op1=ALU.add),
             reads=[ones, Q], writes=[cq])
        p.op("vector", lambda e, Kt=Kt: e.tensor_mul(out=krs[:], in0=Kt[:], in1=rsb[:]),
             reads=[Kt, rsb], writes=[krs])
        p.op("vector", lambda e: e.tensor_tensor_scan(
            out=KS[:], data0=ones[:], data1=krs[:], initial=0.0, op0=ALU.mult, op1=ALU.add),
             reads=[ones, krs], writes=[KS])
        p.op("vector", lambda e: e.tensor_mul(out=t2[:], in0=cq[:], in1=krs[:]),
             reads=[cq, krs], writes=[t2])
        p.op("vector", lambda e: e.tensor_tensor_scan(
            out=LS[:], data0=ones[:], data1=t2[:], initial=0.0, op0=ALU.mult, op1=ALU.add),
             reads=[ones, t2], writes=[LS])
        if store_ends:
            for j, t in enumerate((cq, KS, LS)):
                p.op("vector", lambda e, j=j, t=t, i=i: e.tensor_copy(
                    out=ends[:, i, j:j + 1], in_=lastcol(t)),
                     reads=[t], writes=[(ends, (i, j))])

    for b in range(B):
        for db in range(8):
            tile_scans(b * 8 + db, b, db, True)
    sqc = p.dram("po_sqc", [16, 128], F32).ap()
    sag = p.dram("po_sag", [128, 128], F32).ap()
    for i in range(16):
        p.dma("gpsimd", sqc[i:i + 1, :],
              bass.AP(ends[:].tensor, ends[:].offset + i * 3 + 0, [[ends[:].ap[0][0], 128], [1, 1]]),
              reads=[(ends, (i, 0))], sem="sqst")
    p.coll("AllGather", ALU.bypass, [sqc], [sag], reads=[(sqc, None)], writes=[(sag, None)])

    gsl = [p.sbuf(f"gsl{r}", [16, 128], F32) for r in range(2)]
    cacc = [p.sbuf(f"cacc{i}", [16, 128], F32) for i in range(2)]
    cq_ps = p.psum("cq_ps", [128, 16], F32)
    carryq = p.sbuf("carryq", [128, 16], F32)
    carrypk = p.sbuf("carrypk", [128, 16], F32)

    def compose(agap, dst_col, tag):
        p.op("vector", lambda e: e.memset(cacc[0][:], 0.0), writes=[cacc[0]])
        prev = cacc[0]
        for r in range(8):
            g = gsl[r % 2]
            p.dma("sync", g[:], agap[r * 16:(r + 1) * 16, :], reads=[(agap, None)], writes=[g], sem=tag + "g")
            nxt = cacc[(r + 1) % 2] if r < 7 else cacc[(r + 1) % 2]
            p.op("vector", lambda e, g=g, r=r, prev=prev, nxt=nxt: e.scalar_tensor_tensor(
                out=nxt[:], in0=g[:], scalar=maskc[:, r:r + 1], in1=prev[:],
                op0=ALU.mult, op1=ALU.add),
                 reads=[g, maskc, prev], writes=[nxt])
            prev = nxt
        p.op("tensor", lambda e, prev=prev: e.transpose(out=cq_ps[:, 0:16], in_=prev[:],
                                                        identity=ident[0:16, 0:16]),
             reads=[prev, ident], writes=[cq_ps])
        p.op("vector", lambda e, dst_col=dst_col: e.tensor_copy(out=dst_col[:], in_=cq_ps[:, 0:16]),
             reads=[cq_ps], writes=[dst_col])

    compose(sag, carryq, "cg")
    spk = p.dram("po_spk", [16, 128], F32).ap()
    pag = p.dram("po_pag", [128, 128], F32).ap()
    send = p.sbuf("send", [128, 1], F32)
    for i in range(16):
        p.op("vector", lambda e, i=i: e.scalar_tensor_tensor(
            out=send[:], in0=ends[:, i, 1:2], scalar=carryq[:, i:i + 1], in1=ends[:, i, 2:3],
            op0=ALU.mult, op1=ALU.add),
             reads=[(ends, (i, 1)), (ends, (i, 2)), carryq], writes=[send])
        p.dma("gpsimd", spk[i:i + 1, :], send[:], reads=[send], sem="spkst")
    p.coll("AllGather", ALU.bypass, [spk], [pag], reads=[(spk, None)], writes=[(pag, None)])
    compose(pag, carrypk, "cp")
    c2 = p.sbuf("c2", [128, TLOC], F32)
    c3 = p.sbuf("c3", [128, TLOC], F32)
    pk = p.sbuf("pk", [128, TLOC], F32)
    pkv = p.sbuf("pkv", [128, TLOC], BF16)
    for b in range(B):
        for db in range(8):
            i = b * 8 + db
            tile_scans(i, b, db, False)
            p.dma("sync", vt[:], vT_i[b, db * 128:(db + 1) * 128, :], writes=[vt], sem="vtl")
            p.op("vector", lambda e, i=i: e.scalar_tensor_tensor(
                out=c2[:], in0=KS[:], scalar=carryq[:, i:i + 1], in1=LS[:],
                op0=ALU.mult, op1=ALU.add),
                 reads=[KS, LS, carryq], writes=[c2])
            p.op("vector", lambda e, i=i: e.tensor_scalar(
                out=c3[:], in0=c2[:], scalar1=carrypk[:, i:i + 1], scalar2=None, op0=ALU.add),
                 reads=[c2, carrypk], writes=[c3])
            p.op("vector", lambda e: e.tensor_mul(out=pk[:], in0=c3[:], in1=rsb[:]),
                 reads=[c3, rsb], writes=[pk])
            p.op("vector", lambda e: e.tensor_mul(out=pkv[:], in0=pk[:], in1=vt[:]),
                 reads=[pk, vt], writes=[pkv])
            p.dma("gpsimd", pkv_o[b, db * 128:(db + 1) * 128, :], pkv[:], reads=[pkv], sem="pkvst")
    return


def build_fused():
    p = Prog("fused")
    nc = p.nc
    ident_in = p.dram("ident_in", [128, 128], F32, kind="ExternalInput").ap()
    x0 = p.dram("x0", [B, TLOC, D], F32, kind="ExternalInput").ap()
    x6 = p.dram("x6", [B, TLOC, D], F32, kind="ExternalOutput").ap()
    moe_w = dict(
        We=p.dram("We", [8, D, D], F32, kind="ExternalInput").ap(),
        be=p.dram("be", [8, D], F32, kind="ExternalInput").ap(),
        Wr=p.dram("Wr", [8, D], F32, kind="ExternalInput").ap(),
        br=p.dram("br", [8], F32, kind="ExternalInput").ap(),
    )
    mf_io = _mfront_io(p, kind_out="Internal", pre="mf_", make_x=False)
    mb_w = _mback_w_io(p)
    lf_w = dict(
        Wr=p.dram("lf_Wr", [D, D], F32, kind="ExternalInput").ap(),
        Wk=p.dram("lf_Wk", [D, D], F32, kind="ExternalInput").ap(),
        Wv=p.dram("lf_Wv", [D, D], F32, kind="ExternalInput").ap(),
        mur=p.dram("lf_mur", [128, 8], F32, kind="ExternalInput").ap(),
        muk=p.dram("lf_muk", [128, 8], F32, kind="ExternalInput").ap(),
    )
    la_w = dict(
        Wq=p.dram("la_Wq", [D, D], F32, kind="ExternalInput").ap(),
        Wk=p.dram("la_Wk", [D, D], F32, kind="ExternalInput").ap(),
        Wv=p.dram("la_Wv", [D, D], F32, kind="ExternalInput").ap(),
        bqkv=p.dram("la_bqkv", [128, 3, 8], F32, kind="ExternalInput").ap(),
        cw=p.dram("la_cw", [4, 1024, 256], F32, kind="ExternalInput").ap(),
        cb=p.dram("la_cb", [1, 1024], F32, kind="ExternalInput").ap(),
        hmask=p.dram("la_hmask", [128, 1], F32, kind="ExternalInput").ap(),
        lncols=p.dram("la_lncols", [128, 6, 8], F32, kind="ExternalInput").ap(),
    )
    lb_w = dict(
        Wb=p.dram("lb_Wb", [4096, D], F32, kind="ExternalInput").ap(),
        Wu=p.dram("lb_Wu", [D, 4096], F32, kind="ExternalInput").ap(),
        bb=p.dram("lb_bb", [128, 32], F32, kind="ExternalInput").ap(),
        bu=p.dram("lb_bu", [1, D], F32, kind="ExternalInput").ap(),
        lncols=p.dram("lb_lncols", [128, 2, 8], F32, kind="ExternalInput").ap(),
    )
    sels = {h: (p.dram(f"selL_{h}", [96, 6], BF16, kind="ExternalInput").ap(),
                p.dram(f"selR_{h}", [96, 6], BF16, kind="ExternalInput").ap())
            for h in ("h1", "h2", "h3", "h4")}
    selbc = p.dram("selbc", [64, 32, 128], BF16, kind="ExternalInput").ap()
    alog2 = p.dram("alog2", [2, 4, 128, 16], F32, kind="ExternalInput").ap()
    maskprev = p.dram("maskprev", [16, 8], F32, kind="ExternalInput").ap()
    rss = p.dram("rss", [1, TLOC], F32, kind="ExternalInput").ap()
    offtab_d = p.dram("offtab", [1, 1024], mybir.dt.int32, kind="ExternalInput").ap()
    dtw2_d = p.dram("dtw2", [64, 2, 512], F32, kind="ExternalInput").ap()
    dtb2_d = p.dram("dtb2", [128, 2, 4], F32, kind="ExternalInput").ap()
    x1h = p.dram("x1h", [B, TLOC + 6, D], F32).ap()
    x2 = p.dram("x2s", [B, TLOC, D], F32).ap()
    x3h = p.dram("x3h", [B, TLOC + 1, D], F32).ap()
    x4h = p.dram("x4h", [B, TLOC + 3, D], F32).ap()
    x5h = p.dram("x5h", [B, TLOC + 1, D], F32).ap()
    ag = {}
    for d_ in "fb":
        ag[f"duT_ag_{d_}"] = p.dram(f"duT_ag_{d_}", [8 * 4096, TLOC], BF16).ap()
        ag[f"bc_ag_{d_}"] = p.dram(f"bc_ag_{d_}", [8 * 64, TLOC], BF16).ap()
        ag[f"r_ag_{d_}"] = p.dram(f"r_ag_{d_}", [8 * 128, TLOC], BF16).ap()
    dmy = p.dram("dmy", [2, 2, 512, S], BF16).ap()
    ylocal2 = [p.dram(f"ylocal{di}", [512, S], BF16).ap() for di in range(2)]
    y_ag2 = [p.dram(f"y_ag{di}", [8 * 512, S], BF16).ap() for di in range(2)]
    ylocT = p.dram("ylocT", [B, 2, 2048, TLOC], BF16).ap()
    g1s = p.dram("g1s", [B, TLOC, D], F32).ap()
    qkvl = {n: p.dram(n, [B, D, TLOC], BF16).ap() for n in ("qTl", "kTl", "vTl")}
    pkvl = p.dram("pkvTl", [B, D, TLOC], BF16).ap()

    regs = {}
    regs["rs"] = p.stack.enter_context(nc.scalar.register("rs"))
    regs["rg"] = p.stack.enter_context(nc.gpsimd.register("rg"))
    regs["ry"] = p.stack.enter_context(nc.sync.register("ry"))
    offtab_sb = p.sbuf("offtab_sb", [1, 1024], mybir.dt.int32)
    p.dma("sync", offtab_sb[:], offtab_d, writes=[offtab_sb])
    regs["offtab"] = offtab_sb

    with p.stage("s1"):
        io1 = dict(x=x0, ident_in=ident_in, xo=x1h[:, 3:3 + TLOC, :], **moe_w)
        emit_moe(p, io1)
    with p.stage("hx1"):
        emit_halo(p, "h1",
                  lambda b: (x1h[b, 3:6, :], x1h[b, TLOC:TLOC + 3, :]),
                  lambda b, side: (x1h[b, 0:3, :] if side == "L" else x1h[b, TLOC + 3:TLOC + 6, :]),
                  3, 3, sels["h1"][0], sels["h1"][1], None)
    def colls_after_dir(d_):
        di2 = 0 if d_ == "f" else 1
        for nm in ("duT", "bcT", "rT"):
            agk = {"duT": f"duT_ag_{d_}", "bcT": f"bc_ag_{d_}", "rT": f"r_ag_{d_}"}[nm]
            srcv = mf_io[f"{nm}_{d_}"].rearrange("b d t -> (b d) t")
            p.coll("AllGather", ALU.bypass, [srcv], [ag[agk]],
                   reads=[(mf_io[f"{nm}_{d_}"], None)], writes=[(ag[agk], None)])
        for tn, nm in ((1, "duT"),):
            for db in range(4):
                idx = (di2 * 4 + db) * 2 + tn
                agt = ag[f"duT_ag_{d_}"]
                def fg(e, tn=tn, di2=di2, db=db, idx=idx, agt=agt):
                    e.reg_load(regs["rs"], offtab_sb[0:1, idx:idx + 1])
                    return e.dma_start(out=dmy[tn, di2, db * 128:(db + 1) * 128, :],
                                       in_=bass.AP(agt.tensor, regs["rs"],
                                                   [[TLOC, 128], [4096 * TLOC, 8], [1, TLOC]]))
                p.op("scalar", fg, reads=[(agt, None), (offtab_sb, None)],
                     writes=[(dmy, (tn, di2, db))], dma=True, dma_sem=f"gx{tn}{db % 2}")

    with p.stage("s2"):
        io2 = dict(mf_io)
        io2["x"] = x1h
        io2["ident_in"] = ident_in
        io2["_colls_after_dir"] = colls_after_dir
        io2["no_dT"] = True
        emit_mfront(p, io2)
    with p.stage("s3"):
        io3 = dict(ag)
        io3["dmy"] = dmy
        io3["alog2"] = alog2
        io3["selbc"] = selbc
        io3["ylocal2"] = ylocal2
        io3["dtw2"] = dtw2_d
        io3["dtb2"] = dtb2_d

        def _ycoll(di):
            p.coll("AllGather", ALU.bypass, [ylocal2[di]], [y_ag2[di]],
                   reads=[(ylocal2[di], None)], writes=[(y_ag2[di], None)])
        io3["_ycoll"] = _ycoll
        emit_scan_fused(p, io3, regs)
    with p.stage("gy"):
        for b in range(B):
            for di in range(2):
                for kbhi in range(4):
                    idx = 512 + (b * 2 + di) * 4 + kbhi
                    def fg(e, b=b, di=di, kbhi=kbhi, idx=idx):
                        e.reg_load(regs["rg"], offtab_sb[0:1, idx:idx + 1])
                        return e.dma_start(
                            out=ylocT[b, di, kbhi * 512:(kbhi + 1) * 512, :],
                            in_=bass.AP(y_ag2[di].tensor, regs["rg"],
                                        [[S, 512], [1, TLOC]]))
                    p.op("gpsimd", fg, reads=[(y_ag2[di], None), (offtab_sb, None)],
                         writes=[(ylocT, (b, di, kbhi))], dma=True, dma_sem=f"gy{kbhi}")
    with p.stage("s4"):
        io4 = dict(mb_w)
        io4["x"] = x1h[:, 3:3 + TLOC, :]
        io4["ident_in"] = ident_in
        io4["xo"] = x2
        for d_ in "fb":
            io4[f"uT_{d_}"] = mf_io[f"uT_{d_}"]
            io4[f"zT_{d_}"] = mf_io[f"zT_{d_}"]
        for d_, di in (("f", 0), ("b", 1)):
            io4[f"yT_{d_}"] = ylocT[:, di]
        emit_mback(p, io4)
    with p.stage("s5"):
        io5 = dict(x=x2, ident_in=ident_in, xo=x3h[:, 1:1 + TLOC, :], **moe_w)
        emit_moe(p, io5)
    with p.stage("hx2"):
        emit_halo(p, "h2",
                  lambda b: (x3h[b, 1:4, :], x3h[b, TLOC - 2:TLOC + 1, :]),
                  lambda b, side: (x3h[b, 0:1, :] if side == "L" else None),
                  1, 0, sels["h2"][0], sels["h2"][1], None)
    with p.stage("s6"):
        io6 = dict(x=x3h, ident_in=ident_in, xo=x4h[:, 3:3 + TLOC, :], **lf_w)
        emit_l1ffn(p, io6)
    with p.stage("hx3"):
        emit_halo(p, "h3",
                  lambda b: (x4h[b, 3:6, :], x4h[b, TLOC:TLOC + 3, :]),
                  lambda b, side: (x4h[b, 0:3, :] if side == "L" else None),
                  3, 0, sels["h3"][0], sels["h3"][1], None)
    with p.stage("s7"):
        io7 = dict(x=x4h, ident_in=ident_in, g1=g1s, qT=qkvl["qTl"], kT=qkvl["kTl"],
                   vT=qkvl["vTl"], **la_w)
        emit_l1af(p, io7)
    with p.stage("s8"):
        io8 = dict(qTl=qkvl["qTl"], kTl=qkvl["kTl"], vTl=qkvl["vTl"], rss=rss,
                   maskprev=maskprev, ident_in=ident_in, pkvTl=pkvl)
        emit_pooled_fused(p, io8)
    with p.stage("s9"):
        io9 = dict(x=x4h[:, 3:3 + TLOC, :], g1=g1s, pkvT=pkvl, ident_in=ident_in,
                   xo=x5h[:, 1:1 + TLOC, :], **lb_w)
        emit_l1ab(p, io9)
    with p.stage("hx4"):
        emit_halo(p, "h4",
                  lambda b: (x5h[b, 1:4, :], x5h[b, TLOC - 2:TLOC + 1, :]),
                  lambda b, side: (x5h[b, 0:1, :] if side == "L" else None),
                  1, 0, sels["h4"][0], sels["h4"][1], None)
    with p.stage("s10"):
        io10 = dict(x=x5h, ident_in=ident_in, xo=x6, **lf_w)
        emit_l1ffn(p, io10)
    p.finalize()
    return p


def kernel_fused(inputs):
    ins = {k: np.asarray(v) for k, v in inputs.items()}
    p = _PROG_CACHE.get("fused")
    if p is None:
        p = _PROG_CACHE["fused"] = build_fused()
    x0f = np.ascontiguousarray(np.asarray(ins["L0_attn_x"], np.float32))
    in_maps = []
    for c in range(N_CORES):
        lo, hi = c * TLOC, (c + 1) * TLOC
        d = dict(
            ident_in=_ident_np(),
            x0=np.ascontiguousarray(x0f[:, lo:hi]),
            We=ins["L0_ffn_W_experts"], be=ins["L0_ffn_b_experts"],
            Wr=ins["L0_ffn_W_router"], br=ins["L0_ffn_b_router"],
            fw=ins['L0_attn_fusion_w'], fb=np.asarray(ins['L0_attn_fusion_b'])[None, :],
            Dv_f=_col16(ins['L0_attn_fwd_D']), Dv_b=_col16(ins['L0_attn_bwd_D']),
            outw_f=ins['L0_attn_fwd_out_proj_w'], outw_b=ins['L0_attn_bwd_out_proj_w'],
            lf_Wr=ins['L1_ffn_W_r'], lf_Wk=ins['L1_ffn_W_k'], lf_Wv=ins['L1_ffn_W_v'],
            lf_mur=_col8(ins['L1_ffn_mu_r']), lf_muk=_col8(ins['L1_ffn_mu_k']),
            la_Wq=ins['L1_attn_Wq'], la_Wk=ins['L1_attn_Wk'], la_Wv=ins['L1_attn_Wv'],
            la_cw=np.ascontiguousarray(np.asarray(ins['L1_attn_conv_w']).transpose(2, 0, 1)),
            la_cb=np.asarray(ins['L1_attn_conv_b'])[None, :],
            la_hmask=np.full((128, 1), 0.0 if lo == 0 else 1.0, np.float32),
            lb_Wb=ins['L1_attn_Wb'], lb_Wu=ins['L1_attn_Wu'],
            lb_bb=_col32(ins['L1_attn_bb']), lb_bu=np.asarray(ins['L1_attn_bu'])[None, :],
        )
        for d_, pref in (("f", "fwd_"), ("b", "bwd_")):
            d[f"mf_Win_{d_}"] = ins[f"L0_attn_{pref}in_proj_w"]
            d[f"mf_convw_{d_}"] = np.ascontiguousarray(
                np.asarray(ins[f"L0_attn_{pref}conv_w"])[:, 0, :].reshape(16, 128, 4).transpose(1, 0, 2))
            d[f"mf_convb_{d_}"] = _col16(ins[f"L0_attn_{pref}conv_b"])
            d[f"mf_xproj_{d_}"] = ins[f"L0_attn_{pref}x_proj_w"]
            d[f"mf_dtw_{d_}"] = ins[f"L0_attn_{pref}dt_proj_w"]
            d[f"mf_dtb_{d_}"] = _col16(ins[f"L0_attn_{pref}dt_proj_b"])
        lncols = np.zeros((128, 6, 8), np.float32)
        lncols[:, 0] = _col8(ins['L1_attn_ln_conv_g']); lncols[:, 1] = _col8(ins['L1_attn_ln_conv_b'])
        lncols[:, 2] = _col8(ins['L1_attn_ln_attn_g']); lncols[:, 3] = _col8(ins['L1_attn_ln_attn_b'])
        d["la_lncols"] = lncols
        bqkv = np.zeros((128, 3, 8), np.float32)
        bqkv[:, 0] = _col8(ins['L1_attn_bq']); bqkv[:, 1] = _col8(ins['L1_attn_bk'])
        bqkv[:, 2] = _col8(ins['L1_attn_bv'])
        d["la_bqkv"] = bqkv
        lncols2 = np.zeros((128, 2, 8), np.float32)
        lncols2[:, 0] = _col8(ins['L1_attn_ln_boom_g']); lncols2[:, 1] = _col8(ins['L1_attn_ln_boom_b'])
        d["lb_lncols"] = lncols2
        for h in ("h1", "h2", "h3", "h4"):
            sl = np.zeros((96, 6), np.float32)
            sr = np.zeros((96, 6), np.float32)
            if c > 0:
                for b in range(B):
                    for j in range(3):
                        sl[(c - 1) * 12 + b * 6 + 3 + j, b * 3 + j] = 1.0
            if c < 7:
                for b in range(B):
                    for j in range(3):
                        sr[(c + 1) * 12 + b * 6 + j, b * 3 + j] = 1.0
            d[f"selL_{h}"] = sl.astype(BF16NP)
            d[f"selR_{h}"] = sr.astype(BF16NP)
        sb_, sq_ = c // 4, c % 4
        selbc = np.zeros((64, 32, 128), np.float32)
        for j in range(32):
            selbc[sb_ * 32 + j, j, :] = 1.0
        d["selbc"] = selbc.astype(BF16NP)
        al = np.zeros((2, 4, 128, 16), np.float32)
        for di, pref in enumerate(("fwd_", "bwd_")):
            alog = np.asarray(ins[f"L0_attn_{pref}A_log"], np.float32)
            al[di] = alog[sq_ * 512:(sq_ + 1) * 512].reshape(4, 128, 16)
        d["alog2"] = al
        mp = np.zeros((16, 8), np.float32)
        mp[:, :c] = 1.0
        d["maskprev"] = mp
        d["rss"] = (1.0 / np.arange(lo + 1, hi + 1, dtype=np.float32))[None, :]
        offtab = np.zeros((1, 1024), np.int32)
        base_row = sb_ * 2048 + sq_ * 512
        for di in range(2):
            for db in range(4):
                off = (base_row + db * 128) * TLOC
                offtab[0, (di * 4 + db) * 2] = off
                offtab[0, (di * 4 + db) * 2 + 1] = off
            offtab[0, 32 + di] = sb_ * 32 * TLOC
        offtab[0, 40] = sb_ * 64 * TLOC
        for b in range(B):
            for di in range(2):
                for kbhi in range(4):
                    off = (b * 4 + kbhi) * 512 * S + c * TLOC
                    offtab[0, 512 + (b * 2 + di) * 4 + kbhi] = off
        d["offtab"] = offtab
        dtw2 = np.zeros((64, 2, 512), np.float32)
        dtb2 = np.zeros((128, 2, 4), np.float32)
        for di, pref in enumerate(("fwd_", "bwd_")):
            dw = np.asarray(ins[f"L0_attn_{pref}dt_proj_w"], np.float32)
            dtw2[:, di, :] = dw[sq_ * 512:(sq_ + 1) * 512, :].T
            db_ = np.asarray(ins[f"L0_attn_{pref}dt_proj_b"], np.float32)
            dtb2[:, di, :] = db_[sq_ * 512:(sq_ + 1) * 512].reshape(4, 128).T
        d["dtw2"] = dtw2
        d["dtb2"] = dtb2
        in_maps.append(d)
    res = run_bass_kernel_spmd(p.nc, in_maps, core_ids=list(range(N_CORES)))
    out = np.empty((B, S, D), np.float32)
    for c in range(N_CORES):
        out[:, c * TLOC:(c + 1) * TLOC, :] = res.results[c]["x6"]
    return out



_PROG_CACHE = {}

def _prog(name, builder):
    if name not in _PROG_CACHE:
        _PROG_CACHE[name] = builder()
    return _PROG_CACHE[name]

def _ident_np():
    return np.eye(128, dtype=np.float32)

def _col8(v):
    return np.ascontiguousarray(np.asarray(v).reshape(8, 128).T)

def _col16(v):
    return np.ascontiguousarray(np.asarray(v).reshape(16, 128).T)

def _col32(v):
    return np.ascontiguousarray(np.asarray(v).reshape(32, 128).T)

def _sel_np():
    sel = np.zeros((32, 32, 128), np.float32)
    for j in range(32):
        sel[j, j, :] = 1.0
    return sel.astype(BF16NP)

def _run(p, in_maps):
    return run_bass_kernel_spmd(p.nc, in_maps, core_ids=list(range(N_CORES)))


def run_moe(x_full, We, be, Wr, br):
    p = _prog("moe", build_moe)
    in_maps = []
    for c in range(N_CORES):
        sl = x_full[:, c * TLOC:(c + 1) * TLOC, :]
        in_maps.append(dict(x=np.ascontiguousarray(sl), We=We, be=be, Wr=Wr, br=br,
                            ident_in=_ident_np()))
    res = _run(p, in_maps)
    out = np.empty_like(x_full)
    for c in range(N_CORES):
        out[:, c * TLOC:(c + 1) * TLOC, :] = res.results[c]["xo"]
    return out


def run_mfront(x1, ins):
    p = _prog("mfront", build_mamba_front)
    in_maps = []
    for c in range(N_CORES):
        lo, hi = c * TLOC, (c + 1) * TLOC
        xh = np.zeros((B, TLOC + 6, D), np.float32)
        xh[:, 3:3 + TLOC] = x1[:, lo:hi]
        if lo > 0:
            xh[:, 0:3] = x1[:, lo - 3:lo]
        if hi < S:
            xh[:, 3 + TLOC:] = x1[:, hi:hi + 3]
        d = dict(x=xh, ident_in=_ident_np())
        for d_, pref in (("f", "fwd_"), ("b", "bwd_")):
            d[f"Win_{d_}"] = ins[f"L0_attn_{pref}in_proj_w"]
            d[f"convw_{d_}"] = np.ascontiguousarray(
                ins[f"L0_attn_{pref}conv_w"][:, 0, :].reshape(16, 128, 4).transpose(1, 0, 2))
            d[f"convb_{d_}"] = _col16(ins[f"L0_attn_{pref}conv_b"])
            d[f"xproj_{d_}"] = ins[f"L0_attn_{pref}x_proj_w"]
            d[f"dtw_{d_}"] = ins[f"L0_attn_{pref}dt_proj_w"]
            d[f"dtb_{d_}"] = _col16(ins[f"L0_attn_{pref}dt_proj_b"])
        in_maps.append(d)
    res = _run(p, in_maps)
    st = {}
    for name in ("uT", "zT", "dT", "duT", "bcT"):
        for d_ in "fb":
            st[f"{name}_{d_}"] = np.concatenate(
                [np.asarray(res.results[c][f"{name}_{d_}"]) for c in range(N_CORES)], axis=2)
    return st


def run_scan(st, ins):
    p = _prog("scan", lambda: build_scan(T=S))
    in_maps, meta = [], []
    for d_ in "fb":
        alog = np.asarray(ins[f'L0_attn_{"fwd" if d_ == "f" else "bwd"}_A_log'], np.float32)
        for b in range(B):
            for half in range(2):
                dsl = slice(half * 1024, (half + 1) * 1024)
                dT = st[f'dT_{d_}'][b, dsl, :]
                duT = st[f'duT_{d_}'][b, dsl, :]
                bcT = st[f'bcT_{d_}'][b]
                if d_ == "b":
                    dT = dT[:, ::-1]; duT = duT[:, ::-1]; bcT = bcT[:, ::-1]
                in_maps.append(dict(dT=np.ascontiguousarray(dT), duT=np.ascontiguousarray(duT),
                                    bcT=np.ascontiguousarray(bcT),
                                    alog=np.ascontiguousarray(alog[dsl].reshape(8, 128, 16)),
                                    sel=_sel_np()))
                meta.append((d_, b, half))
    res = _run(p, in_maps)
    out = {d_: np.zeros((B, 2048, S), BF16NP) for d_ in "fb"}
    for c, (d_, b, half) in enumerate(meta):
        y = np.asarray(res.results[c]["yT"])
        if d_ == "b":
            y = y[:, ::-1]
        out[d_][b, half * 1024:(half + 1) * 1024, :] = y
    return out


def run_mback(x1, st, ys, ins):
    p = _prog("mback", build_mamba_back)
    in_maps = []
    for c in range(N_CORES):
        lo, hi = c * TLOC, (c + 1) * TLOC
        d = dict(x=np.ascontiguousarray(x1[:, lo:hi]),
                 outw_f=ins['L0_attn_fwd_out_proj_w'], outw_b=ins['L0_attn_bwd_out_proj_w'],
                 fw=ins['L0_attn_fusion_w'], fb=np.asarray(ins['L0_attn_fusion_b'])[None, :],
                 ident_in=_ident_np(),
                 Dv_f=_col16(ins['L0_attn_fwd_D']), Dv_b=_col16(ins['L0_attn_bwd_D']))
        for d_ in "fb":
            d[f"yT_{d_}"] = np.ascontiguousarray(ys[d_][:, :, lo:hi])
            d[f"uT_{d_}"] = np.ascontiguousarray(st[f'uT_{d_}'][:, :, lo:hi])
            d[f"zT_{d_}"] = np.ascontiguousarray(st[f'zT_{d_}'][:, :, lo:hi])
        in_maps.append(d)
    res = _run(p, in_maps)
    out = np.empty_like(x1)
    for c in range(N_CORES):
        out[:, c * TLOC:(c + 1) * TLOC, :] = res.results[c]["xo"]
    return out


def run_l1ffn_launch(x_full, ins):
    p = _prog("l1ffn", build_l1ffn)
    in_maps = []
    for c in range(N_CORES):
        lo, hi = c * TLOC, (c + 1) * TLOC
        xh = np.zeros((B, TLOC + 1, D), np.float32)
        xh[:, 1:] = x_full[:, lo:hi]
        if lo > 0:
            xh[:, 0] = x_full[:, lo - 1]
        in_maps.append(dict(x=xh, Wr=ins['L1_ffn_W_r'], Wk=ins['L1_ffn_W_k'], Wv=ins['L1_ffn_W_v'],
                            mur=_col8(ins['L1_ffn_mu_r']), muk=_col8(ins['L1_ffn_mu_k']),
                            ident_in=_ident_np()))
    res = _run(p, in_maps)
    out = np.empty_like(x_full)
    for c in range(N_CORES):
        out[:, c * TLOC:(c + 1) * TLOC, :] = res.results[c]["xo"]
    return out


def run_l1af(x4, ins):
    p = _prog("l1af", build_l1attn_front)
    lncols = np.zeros((128, 6, 8), np.float32)
    lncols[:, 0] = _col8(ins['L1_attn_ln_conv_g']); lncols[:, 1] = _col8(ins['L1_attn_ln_conv_b'])
    lncols[:, 2] = _col8(ins['L1_attn_ln_attn_g']); lncols[:, 3] = _col8(ins['L1_attn_ln_attn_b'])
    bqkv = np.zeros((128, 3, 8), np.float32)
    bqkv[:, 0] = _col8(ins['L1_attn_bq']); bqkv[:, 1] = _col8(ins['L1_attn_bk'])
    bqkv[:, 2] = _col8(ins['L1_attn_bv'])
    in_maps = []
    for c in range(N_CORES):
        lo, hi = c * TLOC, (c + 1) * TLOC
        xh = np.zeros((B, TLOC + 3, D), np.float32)
        xh[:, 3:] = x4[:, lo:hi]
        if lo > 0:
            xh[:, 0:3] = x4[:, lo - 3:lo]
        in_maps.append(dict(x=xh, Wq=ins['L1_attn_Wq'], Wk=ins['L1_attn_Wk'], Wv=ins['L1_attn_Wv'],
                            bqkv=bqkv, cw=np.ascontiguousarray(np.asarray(ins['L1_attn_conv_w']).transpose(2, 0, 1)),
                            cb=np.asarray(ins['L1_attn_conv_b'])[None, :], lncols=lncols,
                            ident_in=_ident_np(),
                            hmask=np.full((128, 1), 0.0 if lo == 0 else 1.0, np.float32)))
    res = _run(p, in_maps)
    g1 = np.concatenate([np.asarray(res.results[c]["g1"]) for c in range(N_CORES)], axis=1)
    qT = np.concatenate([np.asarray(res.results[c]["qT"]) for c in range(N_CORES)], axis=2)
    kT = np.concatenate([np.asarray(res.results[c]["kT"]) for c in range(N_CORES)], axis=2)
    vT = np.concatenate([np.asarray(res.results[c]["vT"]) for c in range(N_CORES)], axis=2)
    return g1, qT, kT, vT


def run_pooled(qT, kT, vT):
    p = _prog("pooled", lambda: build_pooled(T=S))
    rs = (1.0 / np.arange(1, S + 1, dtype=np.float32))[None, :]
    in_maps = [dict(qT=np.ascontiguousarray(qT[:, c * 128:(c + 1) * 128]),
                    kT=np.ascontiguousarray(kT[:, c * 128:(c + 1) * 128]),
                    vT=np.ascontiguousarray(vT[:, c * 128:(c + 1) * 128]), rs=rs)
               for c in range(N_CORES)]
    res = _run(p, in_maps)
    return np.concatenate([np.asarray(res.results[c]["pkvT"]) for c in range(N_CORES)], axis=1)


def run_l1ab(x4, g1, pkvT, ins):
    p = _prog("l1ab", build_l1attn_back)
    lncols = np.zeros((128, 2, 8), np.float32)
    lncols[:, 0] = _col8(ins['L1_attn_ln_boom_g']); lncols[:, 1] = _col8(ins['L1_attn_ln_boom_b'])
    in_maps = []
    for c in range(N_CORES):
        lo, hi = c * TLOC, (c + 1) * TLOC
        in_maps.append(dict(x=np.ascontiguousarray(x4[:, lo:hi]),
                            g1=np.ascontiguousarray(g1[:, lo:hi]),
                            pkvT=np.ascontiguousarray(pkvT[:, :, lo:hi]),
                            Wb=ins['L1_attn_Wb'], Wu=ins['L1_attn_Wu'],
                            bb=_col32(ins['L1_attn_bb']),
                            bu=np.asarray(ins['L1_attn_bu'])[None, :], lncols=lncols,
                            ident_in=_ident_np()))
    res = _run(p, in_maps)
    out = np.empty_like(x4)
    for c in range(N_CORES):
        out[:, c * TLOC:(c + 1) * TLOC, :] = res.results[c]["xo"]
    return out


FUSED = True


def kernel(**inputs):
    if FUSED:
        return kernel_fused(inputs)
    return kernel_unfused(**inputs)


def kernel_unfused(**inputs):
    x0 = np.ascontiguousarray(np.asarray(inputs["L0_attn_x"], np.float32))
    moe_args = (np.asarray(inputs["L0_ffn_W_experts"], np.float32),
                np.asarray(inputs["L0_ffn_b_experts"], np.float32),
                np.asarray(inputs["L0_ffn_W_router"], np.float32),
                np.asarray(inputs["L0_ffn_b_router"], np.float32))
    ins = {k: np.asarray(v) for k, v in inputs.items()}
    x1 = run_moe(x0, *moe_args)
    st = run_mfront(x1, ins)
    ys = run_scan(st, ins)
    x2 = run_mback(x1, st, ys, ins)
    x3 = run_moe(x2, *moe_args)
    x4 = run_l1ffn_launch(x3, ins)
    g1, qT, kT, vT = run_l1af(x4, ins)
    pkvT = run_pooled(qT, kT, vT)
    x5 = run_l1ab(x4, g1, pkvT, ins)
    x6 = run_l1ffn_launch(x5, ins)
    return x6
